# version 01
# baseline (speedup 1.0000x reference)
import numpy as np
import concourse.bass as bass
import concourse.mybir as mybir

F32 = mybir.dt.float32
BF16 = mybir.dt.bfloat16
AF = mybir.ActivationFunctionType
ALU = mybir.AluOpType

D = 2048
NCH = 16
DS = 128
C = 64
PT = 512
SQK = float(np.sqrt(1.0 / (1.0 + 1e-6)))
LN_05SQK = float(np.log(0.5 * SQK))
EPS_LN = 1e-5
EPS_RMS = 1e-6
B_STAGE = 99


class Prog:
    def __init__(self):
        self.ops = []
        self.count = {}

    def add(self, engine, fn, waits=(), incs=()):
        self.ops.append((engine, fn, tuple(waits), tuple(incs)))
        for sem, amt in incs:
            self.count[sem.name] = self.count.get(sem.name, 0) + amt

    def cur(self, sem):
        return self.count.get(sem.name, 0)

    def emit(self, block):
        ops = self.ops

        def body_for(engine_name):
            def body(eng):
                for e, fn, waits, incs in ops:
                    if e != engine_name:
                        continue
                    for sem, val in waits:
                        eng.wait_ge(sem, val)
                    inst = fn(eng)
                    for sem, amt in incs:
                        inst = inst.then_inc(sem, amt)
            return body

        block.tensor(body_for("tensor"))
        block.vector(body_for("vector"))
        block.scalar(body_for("scalar"))
        block.gpsimd(body_for("gpsimd"))
        block.sync(body_for("sync"))


class Arena:

    def __init__(self, tensor, ncols):
        self.t = tensor
        self.n = ncols
        self.off = 0
        self.marks = []

    def alloc(self, ncols):
        assert self.off + ncols <= self.n, f"arena overflow {self.off}+{ncols}>{self.n}"
        v = self.t[:, self.off:self.off + ncols]
        self.off += ncols
        return v

    def push(self):
        self.marks.append(self.off)

    def pop(self):
        self.off = self.marks.pop()


def host_constants():
    cst = {}
    cst["ones_col"] = np.ones((128, 1), dtype=np.float32)
    cst["twos_col"] = np.full((128, 1), 2.0, dtype=np.float32)
    cst["ones_row"] = np.ones((1, PT), dtype=np.float32)
    cst["eye128"] = np.eye(128, dtype=np.float32)
    rm = np.ones((128, PT), dtype=np.float32)
    rm[:, ::C] = 0.0
    cst["resetmask"] = rm
    up_strict = (np.arange(C)[None, :] > np.arange(C)[:, None]).astype(np.float32)
    up_incl = (np.arange(C)[None, :] >= np.arange(C)[:, None]).astype(np.float32)
    cst["mSTRICT"] = up_strict
    cst["mSTRICTN"] = -up_strict
    cst["mINCL"] = up_incl
    cst["eye64x2"] = np.kron(np.eye(2, dtype=np.float32), np.eye(C, dtype=np.float32))
    return cst


def host_layout(inputs, n_tok, n_cores=8):
    g = {}
    x = np.ascontiguousarray(inputs["L0_attn_x"][0, :n_tok, :].T)
    g["xT"] = x
    my_tok = n_tok // n_cores
    nblk = my_tok // 512
    x5 = x.reshape(16, 128, n_cores, nblk, 512).transpose(0, 3, 2, 1, 4)
    for m_ in range(16):
        g[f"x4d_{m_}"] = np.ascontiguousarray(x5[m_])
    for nm in ("Wk", "Wv", "Wq", "Waf", "Was", "ir_w1"):
        g[nm] = np.ascontiguousarray(inputs["L0_attn_" + nm])
    g["pw_w"] = np.ascontiguousarray(inputs["L0_attn_pw_w"])
    g["nm_w1"] = np.ascontiguousarray(inputs["L0_attn_nm_w1"])
    g["nm_w2"] = np.ascontiguousarray(inputs["L0_attn_nm_w2"])
    g["ir_w2"] = np.ascontiguousarray(inputs["L0_attn_ir_w2"])
    g["Wo"] = np.ascontiguousarray(inputs["L0_attn_Wo"])
    g["wmod"] = np.ascontiguousarray(inputs["L0_wmod"])
    g["ccolT"] = np.ascontiguousarray(inputs["L0_c"].reshape(16, 128).T)
    g["baf"] = inputs["L0_attn_baf"].reshape(128, 1)
    g["bas"] = inputs["L0_attn_bas"].reshape(128, 1)
    g["lamf"] = inputs["L0_attn_fast_lambda"].reshape(128, 1)
    g["lams"] = inputs["L0_attn_slow_lambda"].reshape(128, 1)
    g["nm_b1"] = np.ascontiguousarray(inputs["L0_attn_nm_b1"].reshape(2, 128).T)
    g["nm_b2"] = inputs["L0_attn_nm_b2"].reshape(128, 1)
    g["ir_b1"] = inputs["L0_attn_ir_b1"].reshape(128, 1)
    g["ir_b2"] = inputs["L0_attn_ir_b2"].reshape(1, 1)
    g["pw_b"] = inputs["L0_attn_pw_b"].reshape(1, 3)
    g["norm_w"] = inputs["L0_attn_norm_w"].reshape(128, 1)
    g["gate_wT"] = np.ascontiguousarray(inputs["L0_ffn_gate_w"].T)
    g["W1T"] = np.ascontiguousarray(inputs["L0_ffn_W1"].transpose(0, 2, 1))
    g["W3T"] = np.ascontiguousarray(inputs["L0_ffn_W3"].transpose(0, 2, 1))
    g["W2T"] = np.ascontiguousarray(inputs["L0_ffn_W2"].transpose(0, 2, 1))
    g["Ws1T"] = np.ascontiguousarray(inputs["L0_ffn_Ws1"].T)
    g["Ws3T"] = np.ascontiguousarray(inputs["L0_ffn_Ws3"].T)
    g["Ws2T"] = np.ascontiguousarray(inputs["L0_ffn_Ws2"].T)
    g.update(host_constants())
    return g


class MK:
    def __init__(self, nc, n_tok, n_cores=8, do_scan=True, do_ffn=True, debug_taps=()):
        self.nc = nc
        self.T = n_tok
        self.NPT = self.T // PT
        self.NCK = self.T // C
        self.n_cores = n_cores
        self.do_scan = do_scan
        self.do_ffn = do_ffn
        self.P = Prog()
        self.sems = {}
        self.debug_taps = set(debug_taps)
        self.dbg = {}

    def sem(self, name):
        if name not in self.sems:
            self.sems[name] = self.nc.semaphore(name).__enter__()
        return self.sems[name]

    def tap(self, name, view, shape, after_sem, after_val):
        if name not in self.debug_taps:
            return
        t = self.nc.dram_tensor("dbg_" + name, list(shape), view.dtype, kind="ExternalOutput").ap()
        self.dbg[name] = t
        s = self.sem("dbgs")
        self.P.add("sync", lambda e, t=t, v=view: e.dma_start(t, v),
                   waits=[(after_sem, after_val)], incs=[(s, 16)])

    def declare_io(self):
        nc, P = self.nc, self.P
        g = {}
        g["xT"] = nc.dram_tensor("xT", [D, self.T], F32, kind="ExternalInput").ap()
        my_tok_ = self.T // self.n_cores
        for m_ in range(16):
            g[f"x4d_{m_}"] = nc.dram_tensor(f"x4d_{m_}",
                                            [max(my_tok_ // 512, 1), self.n_cores, 128, 512],
                                            F32, kind="ExternalInput").ap()
        for nm, shp in [("Wk", [D, DS]), ("Wv", [D, DS]), ("Wq", [D, DS]),
                        ("Waf", [D, DS]), ("Was", [D, DS]), ("ir_w1", [D, DS]),
                        ("pw_w", [D, 3]), ("nm_w1", [DS, 256]), ("nm_w2", [256, DS]),
                        ("ir_w2", [DS, 1]), ("Wo", [DS, D]), ("wmod", [D, 6 * D]),
                        ("ccolT", [128, 16]), ("baf", [128, 1]), ("bas", [128, 1]),
                        ("lamf", [128, 1]), ("lams", [128, 1]), ("nm_b1", [128, 2]),
                        ("nm_b2", [128, 1]), ("ir_b1", [128, 1]), ("ir_b2", [1, 1]),
                        ("pw_b", [1, 3]), ("norm_w", [128, 1]),
                        ("gate_wT", [D, 16]), ("W1T", [16, D, 1408]), ("W3T", [16, D, 1408]),
                        ("W2T", [16, 1408, D]), ("Ws1T", [D, 2816]), ("Ws3T", [D, 2816]),
                        ("Ws2T", [2816, D]),
                        ("ones_col", [128, 1]), ("twos_col", [128, 1]), ("ones_row", [1, PT]),
                        ("eye128", [128, 128]), ("resetmask", [128, PT]),
                        ("mSTRICT", [C, C]), ("mSTRICTN", [C, C]), ("mINCL", [C, C]),
                        ("eye64x2", [128, 128])]:
            g[nm] = nc.dram_tensor(nm, shp, F32, kind="ExternalInput").ap()
        NCK = self.NCK
        g["dPF"] = nc.dram_tensor("dPF", [NCK, 128, 192], F32).ap()
        g["dPS"] = nc.dram_tensor("dPS", [NCK, 128, 192], F32).ap()
        g["dV"] = nc.dram_tensor("dV", [NCK, 128, 64], F32).ap()
        g["dVs"] = nc.dram_tensor("dVs", [NCK, 128, 64], F32).ap()
        g["dMIT"] = nc.dram_tensor("dMIT", [NCK, 128, 128], BF16).ap()
        g["dGQT"] = nc.dram_tensor("dGQT", [NCK, 128, C], BF16).ap()
        g["dGOT"] = nc.dram_tensor("dGOT", [NCK, 128, C], BF16).ap()
        g["dUR"] = nc.dram_tensor("dUR", [NCK, 128, 128], BF16).ap()
        g["dNU"] = nc.dram_tensor("dNU", [NCK, 128, C], F32).ap()
        g["dROWS"] = nc.dram_tensor("dROWS", [NCK, 1, 512], F32).ap()
        g["dPC"] = nc.dram_tensor("dPC", [NCK, 128, 2], F32).ap()
        self.g = g
        return g

    def emit_A0(self, ex, ar, ex_psum):
        nc, P, g = self.nc, self.P, self.g
        sb = lambda name, shape, dt=F32: ex.enter_context(nc.sbuf_tensor(name, shape, dt))
        ps = lambda name, shape, dt=F32: ex_psum.enter_context(nc.psum_tensor(name, shape, dt))

        self.wpb = sb("wpb", [128, 6 * NCH * 128], BF16)
        self.wof = sb("wof", [128, D], BF16)
        self.pwc = sb("pwc", [128, 3 * NCH], BF16)
        self.nmw1 = sb("nmw1", [128, 256], BF16)
        self.nmw2 = sb("nmw2", [128, 256], BF16)
        self.irw2 = sb("irw2", [128, 1], BF16)
        self.cols = sb("cols", [128, 160])
        self.mcols = sb("mcols", [128, 96])
        self.rows0 = sb("rows0", [1, 2048])
        self.consts = sb("consts", [128, 8])
        self.onesrow = sb("onesrow", [1, PT])
        self.eyef = sb("eyef", [128, 128])
        self.eyeb = sb("eyeb", [128, 128], BF16)
        self.rmask = sb("rmask", [128, PT])
        self.mstrict = sb("mstrict", [C, C])
        self.mstrictn = sb("mstrictn", [C, C])
        self.mincl = sb("mincl", [C, C])
        self.onescolb = sb("onescolb", [128, 1], BF16)

        sd = self.sem("a0d")
        sc = self.sem("a0c")

        def ld(dst, src):
            P.add("sync", lambda e, d=dst, s=src: e.dma_start(d, s), incs=[(sd, 16)])

        ar.push()
        wtmp = ar.alloc(2048 + 64)[:, 0:2048]
        wtmp = ar.t[:, ar.marks[-1]:ar.marks[-1] + 2048]
        self.wtmp = wtmp
        ld(self.consts[:, 0:1], g["ones_col"])
        ld(self.consts[:, 1:2], g["twos_col"])
        ld(self.onesrow[:, :], g["ones_row"])
        ld(self.eyef[:, :], g["eye128"])
        ld(self.rmask[:, :], g["resetmask"])
        ld(self.mstrict[:, :], g["mSTRICT"])
        ld(self.mstrictn[:, :], g["mSTRICTN"])
        ld(self.mincl[:, :], g["mINCL"])
        ld(self.cols[:, 0:1], g["baf"]); ld(self.cols[:, 1:2], g["bas"])
        ld(self.cols[:, 2:3], g["lamf"]); ld(self.cols[:, 3:4], g["lams"])
        ld(self.cols[:, 6:8], g["nm_b1"]); ld(self.cols[:, 8:9], g["nm_b2"])
        ld(self.cols[:, 9:10], g["ir_b1"]); ld(self.cols[:, 10:11], g["norm_w"])
        ld(self.rows0[0:1, 0:1], g["ir_b2"])
        ld(self.rows0[0:1, 4:7], g["pw_b"])
        ld(self.cols[:, 12:28], g["ccolT"])
        w0 = P.cur(sd)

        P.add("vector", lambda e: e.memset(self.cols[:, 60:61], 1e-8), incs=[(sc, 1)])
        P.add("vector", lambda e: e.memset(self.cols[:, 61:62], EPS_LN), incs=[(sc, 1)])
        P.add("vector", lambda e: e.memset(self.cols[:, 62:63], EPS_RMS), incs=[(sc, 1)])
        P.add("vector", lambda e: e.memset(self.rows0[0:1, 30:31], EPS_LN), incs=[(sc, 1)])
        P.add("vector", lambda e: e.memset(self.rows0[0:1, 31:32], EPS_RMS), incs=[(sc, 1)])
        P.add("vector", lambda e: e.tensor_copy(self.eyeb[:, :], self.eyef[:, :]),
              waits=[(sd, 4 * 16)], incs=[(sc, 1)])
        P.add("vector", lambda e: e.tensor_copy(self.onescolb[:, :], self.consts[:, 0:1]),
              waits=[(sd, 16)], incs=[(sc, 1)])
        P.add("scalar", lambda e: e.activation(self.cols[:, 100:101], self.cols[:, 2:3], AF.Sigmoid),
              waits=[(sd, w0)], incs=[(sc, 1)])
        P.add("scalar", lambda e: e.activation(self.cols[:, 102:103], self.cols[:, 100:101], AF.Ln, bias=self.cols[:, 60:61]),
              incs=[(sc, 1)])
        P.add("vector", lambda e: e.tensor_scalar(self.cols[:, 4:5], self.cols[:, 102:103], 8.0, None, ALU.mult),
              waits=[(sc, P.cur(sc))], incs=[(sc, 1)])
        P.add("vector", lambda e: e.tensor_scalar(self.cols[:, 105:106], self.cols[:, 3:4], -1.0, None, ALU.mult),
              incs=[(sc, 1)])
        P.add("scalar", lambda e: e.activation(self.cols[:, 103:104], self.cols[:, 105:106], AF.Exp),
              waits=[(sc, P.cur(sc))], incs=[(sc, 1)])
        P.add("vector", lambda e: e.tensor_scalar(self.cols[:, 104:105], self.cols[:, 103:104], -0.5, 1.0,
                                                  ALU.mult, ALU.add),
              waits=[(sc, P.cur(sc))], incs=[(sc, 1)])
        P.add("vector", lambda e: e.tensor_tensor(self.cols[:, 104:105], self.cols[:, 103:104],
                                                  self.cols[:, 104:105], ALU.mult),
              waits=[(sc, P.cur(sc))], incs=[(sc, 1)])
        P.add("vector", lambda e: e.tensor_scalar(self.cols[:, 5:6], self.cols[:, 104:105], -8.0, 8e-8,
                                                  ALU.mult, ALU.add),
              waits=[(sc, P.cur(sc))], incs=[(sc, 1)])
        P.add("vector", lambda e: e.tensor_scalar(self.cols[:, 106:107], self.cols[:, 8:9], 0.1, None, ALU.mult),
              incs=[(sc, 1)])
        P.add("scalar", lambda e: e.activation(self.cols[:, 28:44], self.cols[:, 12:28], AF.Silu),
              incs=[(sc, 1)])
        sil_done = P.cur(sc)

        pmod = [ps("pmod0", [128, 96]), ps("pmod1", [128, 96])]
        smod = self.sem("a0m")
        wslab = [ar.alloc(6 * D), ar.alloc(6 * D)]
        P.add("vector", lambda e: e.memset(self.mcols[:, :], 0.0), incs=[(smod, 1)])
        add_done = {}
        for dc in range(NCH):
            par = dc % 2
            P.add("sync", lambda e, dc=dc, par=par: e.dma_start(
                wslab[par][:, :], g["wmod"][dc * 128:(dc + 1) * 128, :]),
                waits=[(smod, max(P.cur(smod) - 2, 0))], incs=[(sd, 16)])
            wslab_ready = P.cur(sd)
            for jj in range(96):
                P.add("tensor",
                      lambda e, par=par, dc=dc, jj=jj: e.matmul(
                          pmod[par][:, jj:jj + 1], wslab[par][:, jj * 128:(jj + 1) * 128],
                          self.cols[:, 28 + dc:29 + dc], start=True, stop=True),
                      waits=([(sd, wslab_ready), (sc, sil_done),
                              (smod, add_done.get(dc - 2, 0))] if jj == 0 else []),
                      incs=([(smod, 1)] if jj == 95 else []))
            P.add("vector", lambda e, par=par: e.tensor_tensor(
                self.mcols[:, :], self.mcols[:, :], pmod[par][:, :], ALU.add),
                  waits=[(smod, P.cur(smod))], incs=[(smod, 1)])
            add_done[dc] = P.cur(smod)
        mod_done = P.cur(smod)
        for base in (32, 80):
            P.add("scalar", lambda e, b=base: e.activation(self.cols[:, 44:60], self.mcols[:, b:b + 16], AF.Sign),
                  waits=[(sc, P.cur(sc)), (smod, mod_done)], incs=[(sc, 1)])
            P.add("scalar", lambda e, b=base: e.activation(self.mcols[:, b:b + 16], self.mcols[:, b:b + 16],
                                                           AF.Abs),
                  waits=[(sc, P.cur(sc))], incs=[(sc, 1)])
            P.add("vector", lambda e, b=base: e.tensor_scalar(self.mcols[:, b:b + 16], self.mcols[:, b:b + 16],
                                                              0.25, None, ALU.max),
                  waits=[(sc, P.cur(sc))], incs=[(sc, 1)])
            P.add("vector", lambda e, b=base: e.tensor_tensor(self.mcols[:, b:b + 16], self.mcols[:, b:b + 16],
                                                              self.cols[:, 44:60], ALU.mult),
                  waits=[(sc, P.cur(sc))], incs=[(sc, 1)])
        P.add("vector", lambda e: e.tensor_scalar(self.mcols[:, 16:32], self.mcols[:, 16:32], 1.0, None, ALU.add),
              incs=[(sc, 1)])
        P.add("vector", lambda e: e.tensor_scalar(self.mcols[:, 64:80], self.mcols[:, 64:80], 1.0, None, ALU.add),
              incs=[(sc, 1)])
        a0_mod_done = P.cur(sc)

        pu = ps("pu", [128, 16])
        su = self.sem("a0u")
        wnames = ["Wk", "Wv", "Wq", "Waf", "Was", "ir_w1"]
        for wi, wn in enumerate(wnames):
            P.add("sync", lambda e, wn=wn: e.dma_start(self.wtmp.rearrange("p (c o) -> p c o", c=NCH), g[wn].rearrange("(c p) o -> p c o", p=128)),
                  waits=[(su, P.cur(su))], incs=[(sd, 16)])
            wrdy = P.cur(sd)
            for ch in range(NCH):
                P.add("tensor", lambda e, wi=wi, ch=ch: e.matmul(
                    pu[:, wi + 8:wi + 9], self.wtmp[:, ch * 128:(ch + 1) * 128],
                    self.mcols[:, ch:ch + 1], start=(ch == 0), stop=(ch == NCH - 1)),
                    waits=([(sd, wrdy), (sc, a0_mod_done)] if ch == 0 else []),
                    incs=([(su, 1)] if ch == NCH - 1 else []))
            for ch in range(NCH):
                P.add("vector", lambda e, wi=wi, ch=ch: e.tensor_scalar(
                    self.wpb[:, (wi * NCH + ch) * 128:(wi * NCH + ch + 1) * 128],
                    self.wtmp[:, ch * 128:(ch + 1) * 128],
                    self.mcols[:, 16 + ch:17 + ch], None, ALU.mult),
                    waits=([(su, P.cur(su))] if ch == 0 else []),
                    incs=([(su, 1)] if ch == NCH - 1 else []))
            wfold = P.cur(su)
            for ch in range(NCH):
                P.add("tensor", lambda e, wi=wi, ch=ch: e.matmul(
                    pu[:, wi:wi + 1],
                    self.wpb[:, (wi * NCH + ch) * 128:(wi * NCH + ch + 1) * 128],
                    self.onescolb[:, :], start=(ch == 0), stop=(ch == NCH - 1)),
                    waits=([(su, wfold)] if ch == 0 else []),
                    incs=([(su, 1)] if ch == NCH - 1 else []))
        pwtmp = sb("pwtmp", [128, 3 * NCH])
        P.add("sync", lambda e: e.dma_start(pwtmp.rearrange("p (c o) -> p c o", c=NCH), g["pw_w"].rearrange("(c p) o -> p c o", p=128)),
              incs=[(sd, 16)])
        pwrdy = P.cur(sd)
        for ch in range(NCH):
            P.add("tensor", lambda e, ch=ch: e.matmul(
                pu[0:3, 14:15],
                pwtmp[:, ch * 3:(ch + 1) * 3], self.mcols[:, ch:ch + 1],
                start=(ch == 0), stop=(ch == NCH - 1)),
                waits=([(sd, pwrdy), (sc, a0_mod_done)] if ch == 0 else []),
                incs=([(su, 1)] if ch == NCH - 1 else []))
        for ch in range(NCH):
            P.add("vector", lambda e, ch=ch: e.tensor_scalar(
                self.pwc[:, ch * 3:(ch + 1) * 3], pwtmp[:, ch * 3:(ch + 1) * 3],
                self.mcols[:, 16 + ch:17 + ch], None, ALU.mult),
                incs=([(su, 1)] if ch == NCH - 1 else []))
        pwfold = P.cur(su)
        for ch in range(NCH):
            P.add("tensor", lambda e, ch=ch: e.matmul(
                pu[0:3, 15:16], self.pwc[:, ch * 3:(ch + 1) * 3], self.onescolb[:, :],
                start=(ch == 0), stop=(ch == NCH - 1)),
                waits=([(su, pwfold)] if ch == 0 else []),
                incs=([(su, 1)] if ch == NCH - 1 else []))
        allu = P.cur(su)

        ptr = ps("ptr", [1, 512])
        str_ = self.sem("a0t")
        P.add("vector", lambda e: e.tensor_copy(self.cols[:, 64:80], pu[:, :]),
              waits=[(su, allu)], incs=[(str_, 1)])
        pu_sb = self.cols[:, 64:80]
        for slot, srcc, dst0, w in [(0, 0, 128, 128), (1, 1, 256, 128), (2, 2, 384, 128),
                                    (3, 3, 512, 128), (4, 4, 640, 128), (5, 5, 768, 128),
                                    (6, 8, 896, 128), (7, 9, 1024, 128), (8, 10, 1152, 128),
                                    (9, 11, 1280, 128), (10, 12, 1408, 128), (11, 13, 1536, 128),
                                    (12, 15, 16, 3), (13, 14, 20, 3)]:
            P.add("tensor", lambda e, srcc=srcc: e.transpose(ptr[0:1, 0:128], pu_sb[:, srcc:srcc + 1],
                                                             self.eyef[:, :]),
                  waits=[(str_, P.cur(str_))], incs=[(str_, 1)])
            P.add("vector", lambda e, dst0=dst0, w=w: e.tensor_copy(self.rows0[0:1, dst0:dst0 + w],
                                                                    ptr[0:1, 0:w]),
                  waits=[(str_, P.cur(str_))], incs=[(str_, 1)])

        P.add("sync", lambda e: e.dma_start(self.wtmp[:, 0:256], g["nm_w1"]),
              waits=[(su, P.cur(su)), (str_, P.cur(str_))], incs=[(sd, 16)])
        P.add("sync", lambda e: e.dma_start(self.wtmp[:, 256:512].rearrange("p (c o) -> p c o", c=2),
                                            g["nm_w2"].rearrange("(c p) o -> p c o", p=128)), incs=[(sd, 16)])
        P.add("sync", lambda e: e.dma_start(self.wtmp[:, 512:513], g["ir_w2"]), incs=[(sd, 16)])
        nmr = P.cur(sd)
        P.add("vector", lambda e: e.tensor_copy(self.nmw1[:, :], self.wtmp[:, 0:256]),
              waits=[(sd, nmr)], incs=[(sc, 1)])
        P.add("vector", lambda e: e.tensor_copy(self.nmw2[:, :], self.wtmp[:, 256:512]), incs=[(sc, 1)])
        P.add("vector", lambda e: e.tensor_copy(self.irw2[:, :], self.wtmp[:, 512:513]), incs=[(sc, 1)])
        P.add("sync", lambda e: e.dma_start(self.wtmp[:, 0:2048], g["Wo"]),
              waits=[(sc, P.cur(sc))], incs=[(sd, 16)])
        wordy = P.cur(sd)
        P.add("vector", lambda e: e.tensor_scalar(self.wof[:, :], self.wtmp[:, 0:2048],
                                                  self.cols[:, 10:11], None, ALU.mult),
              waits=[(sd, wordy)], incs=[(sc, 1)])
        self.a0_waits = [(sc, P.cur(sc)), (su, P.cur(su)), (str_, P.cur(str_)),
                         (sd, P.cur(sd)), (smod, P.cur(smod))]

    def esem(self, name):
        return self.sem("E_" + name)

    def op(self, eng, fn, deps=(), extra_waits=(), inc=None):
        P = self.P
        waits = [(self.esem(d), P.cur(self.esem(d))) for d in deps] + list(extra_waits)
        amt = inc if inc is not None else (16 if eng == "sync" else 1)
        P.add(eng, fn, waits=waits, incs=[(self.esem(eng), amt)])

    def barrier_waits(self):
        return [(self.esem(d), self.P.cur(self.esem(d)))
                for d in ("tensor", "vector", "scalar", "gpsimd", "sync")]

    def emit_A1(self, ex, ar, arb):
        nc, P, g = self.nc, self.P, self.g
        ps = lambda name, shape, dt=F32: ex.enter_context(nc.psum_tensor(name, shape, dt))
        op = self.op

        xf = ar.alloc(NCH * PT)
        k_n = ar.alloc(PT); q_n = ar.alloc(PT); vv = ar.alloc(PT)
        e_rf = ar.alloc(PT); e_rs = ar.alloc(PT); e_ir = ar.alloc(PT)
        rbc = ar.alloc(PT)
        nu = ar.alloc(PT)
        TP = [ar.alloc(PT) for _ in range(9)]
        S = {}
        for st in ("f", "s"):
            S[st] = dict(cp=ar.alloc(PT), U=ar.alloc(PT), Pp=ar.alloc(PT),
                         la=TP[0], LA=TP[1], LX=TP[2], E=TP[3], En=TP[4], Es=TP[5],
                         af=TP[6], c=TP[7], Em=TP[8])
        AQK = {"f": ar.alloc(3 * PT), "s": ar.alloc(3 * PT)}
        gevacB16 = arb.alloc(768)
        uevac = ar.alloc(256)
        uevacB = ar.alloc(256)
        rowsA = ar.alloc(2048)
        R_m, R_sd, R_rstd, R_negm, R_pw0, R_pw1, R_pw2, R_imp, R_g, R_a2f, R_a2s, R_sc0, R_sc1 = (
            rowsA[0:1, i * 128:(i + 1) * 128][:, 0:PT] if False else rowsA[0:1, i * PT // 4:(i + 1) * PT // 4]
            for i in range(13))
        R_m = rowsA[0:1, 0:PT]; R_sd = rowsA[0:1, PT:2 * PT]
        R_rstd = rowsA[0:1, 2 * PT:3 * PT]; R_negm = rowsA[0:1, 3 * PT:4 * PT]
        rowsB = ar.alloc(2048)
        R_pw0 = rowsB[0:1, 0:PT]; R_pw1 = rowsB[0:1, PT:2 * PT]; R_pw2 = rowsB[0:1, 2 * PT:3 * PT]
        R_imp = rowsB[0:1, 3 * PT:4 * PT]
        rowsC = ar.alloc(1024)
        R_sumk = rowsC[0:1, 0:PT]; R_sumq = rowsC[0:1, PT:2 * PT]
        rowsD = ar.alloc(512)
        R_a2f = rowsD[0:1, 0:C]; R_a2s = rowsD[0:1, C:2 * C]
        R_a2fB = rowsD[0:1, 2 * C:3 * C]; R_a2sB = rowsD[0:1, 3 * C:4 * C]
        self.sync_after = {}

        xb = arb.alloc(NCH * PT)
        scrb = arb.alloc(2 * PT)
        kb = arb.alloc(PT); qb = arb.alloc(PT); vb = arb.alloc(PT); girb = arb.alloc(PT)
        LL = arb.alloc(128); Llo = arb.alloc(128)
        POW = arb.alloc(512); Pq = arb.alloc(512)
        Pu = arb.alloc(128); Plo = arb.alloc(128); Fb = arb.alloc(128)
        GQTstack = arb.alloc(2 * C); GOTstack = arb.alloc(2 * C)
        URstack = arb.alloc(256)
        Ubf = arb.alloc(PT); Ubs = arb.alloc(PT)
        S_ub = {"f": Ubf, "s": Ubs}

        pA = ps("pA", [128, PT])
        pB = ps("pB", [128, PT])
        pRa = ps("pRa", [1, PT])
        pRb = ps("pRb", [1, PT])
        pC = ps("pC", [128, PT])
        pG = ps("pG", [C, 192])
        pTB = ex.enter_context(nc.psum_tensor("pTB", [128, 128], BF16))
        pW = ps("pW", [1, PT])

        NPT_ = self.NPT
        wb = self.wpb

        first_tile_waits = self.barrier_waits() + self.a0_waits

        for j in range(NPT_):
            tw = first_tile_waits if j == 0 else None
            for ch in range(NCH):
                P.add("sync", lambda e, ch=ch, j=j: e.dma_start(
                    xf[:, ch * PT:(ch + 1) * PT], g["xT"][ch * 128:(ch + 1) * 128, j * PT:(j + 1) * PT]),
                    waits=((tw if tw is not None else self.barrier_waits()) if ch == 0 else []),
                    incs=[(self.esem("sync"), 16)])
            for ch in range(NCH):
                op("vector", lambda e, ch=ch: e.tensor_copy(xb[:, ch * PT:(ch + 1) * PT],
                                                            xf[:, ch * PT:(ch + 1) * PT]),
                   deps=("sync",) if ch == 0 else ())
            cast_done = P.cur(self.esem("vector"))
            for ch in range(NCH):
                op("tensor", lambda e, ch=ch: e.matmul(
                    pRa[0:1, :], self.onescolb[:, :], xb[:, ch * PT:(ch + 1) * PT],
                    start=(ch == 0), stop=(ch == NCH - 1)), deps=("vector",) if ch == 0 else ())
            for ch in range(NCH):
                op("gpsimd", lambda e, ch=ch: e.tensor_tensor(
                    scrb[:, (ch % 2) * PT:(ch % 2 + 1) * PT], xb[:, ch * PT:(ch + 1) * PT],
                    xb[:, ch * PT:(ch + 1) * PT], ALU.mult),
                   deps=("vector", "tensor") if ch == 0 else (("tensor",) if ch >= 2 else ()))
                op("tensor", lambda e, ch=ch: e.matmul(
                    pRb[0:1, :], self.onescolb[:, :], scrb[:, (ch % 2) * PT:(ch % 2 + 1) * PT],
                    start=(ch == 0), stop=(ch == NCH - 1)), deps=("gpsimd",))
            op("vector", lambda e: e.tensor_scalar(R_m, pRa[0:1, :], 1.0 / D, None, ALU.mult), deps=("tensor",))
            op("vector", lambda e: e.tensor_scalar(R_sd, pRb[0:1, :], 1.0 / D, None, ALU.mult))
            op("vector", lambda e: e.tensor_tensor(R_negm, R_m, R_m, ALU.mult), deps=("vector",))
            op("vector", lambda e: e.tensor_tensor(R_sd, R_sd, R_negm, ALU.subtract), deps=("vector",))
            op("scalar", lambda e: e.activation(R_sd, R_sd, AF.Sqrt, bias=self.rows0[0:1, 30:31]), deps=("vector",))
            op("vector", lambda e: e.reciprocal(R_rstd, R_sd), deps=("scalar",))
            op("vector", lambda e: e.tensor_scalar(R_negm, R_m, -1.0, None, ALU.mult))
            op("tensor", lambda e: e.matmul(pC[:, :], self.onesrow[0:1, 0:128], R_rstd,
                                            start=True, stop=True), deps=("vector",))
            op("vector", lambda e: e.tensor_copy(rbc[:, :], pC[:, :]), deps=("tensor",))
            rbc_done = P.cur(self.esem("vector"))

            projs = [("k", 0, None), ("v", 1, vv), ("q", 2, None), ("rf", 3, e_rf),
                     ("rs", 4, e_rs), ("ir1", 5, e_ir)]
            for pi, (pname, wi, dst) in enumerate(projs):
                pp = pA if pi % 2 == 0 else pB
                for ch in range(NCH):
                    op("tensor", lambda e, wi=wi, ch=ch, pp=pp: e.matmul(
                        pp[:, :], wb[:, (wi * NCH + ch) * 128:(wi * NCH + ch + 1) * 128],
                        xb[:, ch * PT:(ch + 1) * PT], start=(ch == 0), stop=False),
                       deps=("vector",) if (pi < 2 and ch == 0) else ())
                op("tensor", lambda e, wi=wi, pp=pp: e.matmul(
                    pp[:, :], self.rows0[0:1, 128 * (1 + wi):128 * (2 + wi)], R_negm,
                    start=False, stop=False))
                op("tensor", lambda e, wi=wi, pp=pp: e.matmul(
                    pp[:, :], self.rows0[0:1, 128 * (7 + wi):128 * (8 + wi)], R_sd,
                    start=False, stop=True))
                tgt = {"k": k_n, "q": q_n}.get(pname, dst)
                op("vector", lambda e, pp=pp, tgt=tgt: e.tensor_tensor(tgt, pp[:, :], rbc, ALU.mult),
                   deps=("tensor",))
            for m_ in range(3):
                for ch in range(NCH):
                    op("tensor", lambda e, m_=m_, ch=ch: e.matmul(
                        pW[0:1, :], self.pwc[:, ch * 3 + m_:ch * 3 + m_ + 1], xb[:, ch * PT:(ch + 1) * PT],
                        start=(ch == 0), stop=False), deps=("vector",) if ch == 0 else ())
                op("tensor", lambda e, m_=m_: e.matmul(
                    pW[0:1, :], self.rows0[0:1, 16 + m_:17 + m_], R_negm, start=False, stop=False))
                op("tensor", lambda e, m_=m_: e.matmul(
                    pW[0:1, :], self.rows0[0:1, 20 + m_:21 + m_], R_sd, start=False, stop=True))
                dstr = (R_pw0, R_pw1, R_pw2)[m_]
                op("vector", lambda e, dstr=dstr: e.tensor_tensor(dstr, pW[0:1, :], R_rstd, ALU.mult),
                   deps=("tensor",))
                op("scalar", lambda e, dstr=dstr, m_=m_: e.activation(
                    dstr, dstr, AF.Exp, bias=self.rows0[0:1, 4 + m_:5 + m_]), deps=("vector",))
            op("vector", lambda e: e.tensor_tensor(R_sumk, R_pw0, R_pw1, ALU.add), deps=("scalar",))
            op("vector", lambda e: e.tensor_tensor(R_sumk, R_sumk, R_pw2, ALU.add), deps=("vector",))
            op("vector", lambda e: e.reciprocal(R_sumq, R_sumk), deps=("vector",))
            first = True
            for dstr in (R_pw0, R_pw1, R_pw2):
                op("vector", lambda e, dstr=dstr: e.tensor_tensor(dstr, dstr, R_sumq, ALU.mult),
                   deps=("vector",) if first else ())
                first = False
            pw_done = P.cur(self.esem("vector"))

            for li, (src, srow) in enumerate(((k_n, R_sumk), (q_n, R_sumq))):
                pRx = pRa if li == 0 else pRb
                op("gpsimd", lambda e, src=src: e.tensor_tensor(xf[:, 0:PT], src, src, ALU.mult),
                   deps=("vector",))
                op("tensor", lambda e, pRx=pRx: e.matmul(pRx[0:1, :], self.consts[:, 0:1], xf[:, 0:PT],
                                                         start=True, stop=True), deps=("gpsimd", "vector"))
                op("scalar", lambda e, srow=srow, pRx=pRx: e.activation(srow, pRx[0:1, :], AF.Sqrt),
                   deps=("tensor",))
                op("vector", lambda e, srow=srow: e.tensor_scalar(srow, srow, 1e-12, None, ALU.max),
                   deps=("scalar",))
                op("vector", lambda e, srow=srow: e.reciprocal(srow, srow), deps=("vector",))
                op("tensor", lambda e, srow=srow: e.matmul(pC[:, :], self.onesrow[0:1, 0:128], srow,
                                                           start=True, stop=True), deps=("vector",))
                op("vector", lambda e, src=src: e.tensor_tensor(src, src, pC[:, :], ALU.mult),
                   deps=("tensor",))
            kq_done = P.cur(self.esem("vector"))
            op("vector", lambda e: e.tensor_copy(vb[:, :], vv))
            op("vector", lambda e: e.tensor_copy(kb[:, :], k_n))
            op("vector", lambda e: e.tensor_copy(qb[:, :], q_n))

            for h_ in range(2):
                op("tensor", lambda e, h_=h_: e.matmul(pA[:, :], self.nmw1[:, h_ * 128:(h_ + 1) * 128],
                                                       vb[:, :], start=True, stop=True),
                   deps=("vector",) if h_ == 0 else ("scalar",))
                op("scalar", lambda e, h_=h_: e.activation(scrb[:, h_ * PT:(h_ + 1) * PT],
                                                           pA[:, :], AF.Gelu,
                                                           bias=self.cols[:, 6 + h_:7 + h_]), deps=("tensor",))
            for h_ in range(2):
                op("tensor", lambda e, h_=h_: e.matmul(pB[:, :], self.nmw2[:, h_ * 128:(h_ + 1) * 128],
                                                       scrb[:, h_ * PT:(h_ + 1) * PT],
                                                       start=(h_ == 0), stop=(h_ == 1)),
                   deps=("scalar", "vector") if h_ == 0 else ())
            op("scalar", lambda e: e.activation(nu[:, 0:PT] if False else nu, pB[:, :], AF.Identity,
                                                bias=self.cols[:, 106:107], scale=0.1), deps=("tensor",))

            op("scalar", lambda e: e.activation(girb[:, :], e_ir, AF.Gelu, bias=self.cols[:, 9:10]),
               deps=("vector",))
            op("tensor", lambda e: e.matmul(pW[0:1, :], self.irw2[:, :], girb[:, :], start=True, stop=True),
               deps=("scalar", "vector"))
            op("scalar", lambda e: e.activation(R_imp, pW[0:1, :], AF.Sigmoid,
                                                bias=self.rows0[0:1, 0:1]), deps=("tensor",))
            op("vector", lambda e: e.tensor_scalar(R_imp, R_imp, 0.5, None, ALU.is_gt), deps=("scalar",))
            op("tensor", lambda e: e.matmul(pC[:, :], self.onesrow[0:1, 0:128], R_imp,
                                            start=True, stop=True), deps=("vector",))
            op("vector", lambda e: e.tensor_copy(xf[:, PT:2 * PT], pC[:, :]), deps=("tensor",))

            for st, e_r, bcol, lacol in (("f", e_rf, 0, 4), ("s", e_rs, 1, 5)):
                ss = S[st]
                op("scalar", lambda e, e_r=e_r, bcol=bcol, ss=ss: e.activation(
                    ss["af"], e_r, AF.Sigmoid, bias=self.cols[:, bcol:bcol + 1]), deps=("vector",))
                op("vector", lambda e, ss=ss, lacol=lacol: e.tensor_scalar(
                    ss["la"], ss["af"], self.cols[:, lacol:lacol + 1], None, ALU.mult), deps=("scalar",))
                op("vector", lambda e, ss=ss: e.tensor_tensor_scan(
                    ss["LA"], self.rmask[:, :], ss["la"], 0.0, ALU.mult, ALU.add), deps=("vector",))
                op("vector", lambda e, ss=ss: e.tensor_tensor(ss["LX"], ss["LA"], ss["la"], ALU.subtract),
                   deps=("vector",))
                op("scalar", lambda e, ss=ss: e.activation(ss["E"], ss["LX"], AF.Exp), deps=("vector",))
                op("scalar", lambda e, ss=ss: e.activation(ss["Pp"], ss["LA"], AF.Exp))
                op("scalar", lambda e, ss=ss: e.activation(ss["af"], ss["la"], AF.Exp))
                op("scalar", lambda e, ss=ss: e.activation(ss["Em"], ss["LA"], AF.Exp, scale=-1.0))
                op("vector", lambda e, ss=ss: e.tensor_scalar(ss["En"], ss["E"], -1.0, None, ALU.mult),
                   deps=("scalar",))
                op("vector", lambda e, ss=ss: e.tensor_scalar(ss["Es"], ss["E"], -0.5 * SQK, None, ALU.mult))
                op("vector", lambda e, ss=ss: e.tensor_scalar(ss["c"], ss["af"], -1.0, 1.0, ALU.mult, ALU.add))
                op("vector", lambda e, ss=ss: e.tensor_tensor(ss["c"], ss["c"], k_n, ALU.mult), deps=("vector",))
                if st == "s":
                    op("vector", lambda e, ss=ss: e.tensor_tensor(ss["c"], ss["c"], xf[:, PT:2 * PT], ALU.mult),
                       deps=("vector",))
                op("vector", lambda e, ss=ss: e.tensor_tensor(ss["cp"], ss["c"], ss["Em"], ALU.mult),
                   deps=("vector",))
                lac = ss["LA"].rearrange("p (ck t) -> p ck t", t=C)[:, :, C - 1:C]
                lacb = lac.broadcast_to([128, PT // C, C])
                op("vector", lambda e, ss=ss, lacb=lacb: e.tensor_tensor(
                    ss["LX"].rearrange("p (ck t) -> p ck t", t=C), lacb,
                    ss["LA"].rearrange("p (ck t) -> p ck t", t=C), ALU.subtract))
                op("scalar", lambda e, ss=ss: e.activation(ss["LX"], ss["LX"], AF.Exp), deps=("vector",))
                op("vector", lambda e, ss=ss, st=st: e.tensor_tensor(S_ub[st][:, :], ss["c"], ss["LX"],
                                                                     ALU.mult),
                   deps=("scalar", "vector"))
                A = AQK[st]
                o_kn = 0 if st == "f" else PT
                o_qp = PT if st == "f" else 2 * PT
                o_qs = 2 * PT if st == "f" else 0
                op("vector", lambda e, ss=ss, A=A, o=o_qs: e.tensor_tensor(A[:, o:o + PT], q_n, ss["Es"], ALU.mult),
                   deps=("vector",))
                op("vector", lambda e, ss=ss, A=A, o=o_kn: e.tensor_tensor(A[:, o:o + PT], k_n, ss["En"], ALU.mult))
                op("vector", lambda e, ss=ss, A=A, o=o_qp: e.tensor_tensor(A[:, o:o + PT], q_n, ss["Pp"], ALU.mult))
            prep_done = P.cur(self.esem("vector"))

            if j == 0:
                op("vector", lambda e: e.memset(LL[:, :], 0.0), deps=("vector",))
            for ck in range(PT // C):
                gck = j * (PT // C) + ck
                par = gck % 2
                scr_wait = [(self.esem("sync"), self.sync_after.get(gck - 2, 0))]
                cpf_ck = S["f"]["cp"][:, ck * C:(ck + 1) * C]
                cps_ck = S["s"]["cp"][:, ck * C:(ck + 1) * C]
                rhsf = AQK["f"].rearrange("p (s t) -> p s t", s=3)[:, :, ck * C:(ck + 1) * C]
                rhss = AQK["s"].rearrange("p (s t) -> p s t", s=3)[:, :, ck * C:(ck + 1) * C]
                GQTst = GQTstack[:, par * C:(par + 1) * C]
                GOTst = GOTstack[:, par * C:(par + 1) * C]
                URst = URstack[:, par * 128:(par + 1) * 128]
                op("tensor", lambda e, cpf_ck=cpf_ck, rhsf=rhsf: e.matmul(pG[:, :], cpf_ck, rhsf,
                                                                          start=True, stop=True),
                   deps=("vector",))
                op("vector", lambda e: e.tensor_tensor(LL[0:C, 0:C], pG[:, 0:C], self.mstrictn[:, :],
                                                       ALU.mult), deps=("tensor",), extra_waits=scr_wait)
                op("vector", lambda e, GOTst=GOTst: e.tensor_tensor(GOTst[0:C, :], pG[:, C:2 * C],
                                                                    self.mincl[:, :], ALU.mult))
                op("vector", lambda e, GQTst=GQTst: e.tensor_tensor(GQTst[0:C, :], pG[:, 2 * C:3 * C],
                                                                    self.mstrict[:, :], ALU.mult))
                op("tensor", lambda e, cps_ck=cps_ck, rhss=rhss: e.matmul(pG[:, :], cps_ck, rhss,
                                                                          start=True, stop=True),
                   deps=("vector",))
                op("vector", lambda e: e.tensor_tensor(LL[C:2 * C, C:2 * C], pG[:, 0:C],
                                                       self.mstrictn[:, :], ALU.mult), deps=("tensor",))
                op("vector", lambda e, GOTst=GOTst: e.tensor_tensor(GOTst[C:2 * C, :], pG[:, C:2 * C],
                                                                    self.mincl[:, :], ALU.mult))
                op("vector", lambda e, GQTst=GQTst: e.tensor_tensor(GQTst[C:2 * C, :], pG[:, 2 * C:3 * C],
                                                                    self.mstrict[:, :], ALU.mult))
                P.add("sync", lambda e, gck=gck, GQTst=GQTst: e.dma_start(g["dGQT"][gck], GQTst),
                      waits=[(self.esem("vector"), P.cur(self.esem("vector")))],
                      incs=[(self.esem("sync"), 16)])
                P.add("sync", lambda e, gck=gck, GOTst=GOTst: e.dma_start(g["dGOT"][gck], GOTst),
                      incs=[(self.esem("sync"), 16)])
                for st, half in (("f", 0), ("s", 1)):
                    useg = S_ub[st][:, ck * C:(ck + 1) * C]
                    op("tensor", lambda e, useg=useg: e.transpose(pTB[0:C, :], useg, self.eyeb[:, :]),
                       deps=("vector",))
                    op("vector", lambda e, URst=URst, half=half: e.tensor_copy(
                        URst[half * C:(half + 1) * C, :], pTB[0:C, :]), deps=("tensor",))
                P.add("sync", lambda e, gck=gck, URst=URst: e.dma_start(g["dUR"][gck], URst),
                      waits=[(self.esem("vector"), P.cur(self.esem("vector")))],
                      incs=[(self.esem("sync"), 16)])
                op("tensor", lambda e: e.transpose(pTB[:, :], LL[:, :], self.eyeb[:, :]),
                   deps=("vector",))
                op("vector", lambda e: e.tensor_copy(Llo[:, :], pTB[:, :]), deps=("tensor",))
                op("vector", lambda e: e.scalar_tensor_tensor(Pu[:, :], LL[:, :], -1.0, self.eyeb[:, :],
                                                              ALU.mult, ALU.add), deps=("vector",))
                op("vector", lambda e: e.scalar_tensor_tensor(Plo[:, :], Llo[:, :], -1.0, self.eyeb[:, :],
                                                              ALU.mult, ALU.add))
                pu, plo = LL, Llo
                cu, clo = Pu, Plo
                for gi, k in enumerate((2, 4, 8, 16, 32)):
                    last = (k == 32)
                    npu = POW[:, (gi % 2) * 256:(gi % 2) * 256 + 128]
                    nplo = POW[:, (gi % 2) * 256 + 128:(gi % 2) * 256 + 256]
                    op("tensor", lambda e, pu=pu, plo=plo: e.matmul(pA[:, 0:128], plo, pu,
                                                                    start=True, stop=True),
                       deps=("vector",))
                    op("tensor", lambda e, pu=pu, plo=plo: e.matmul(pA[:, 128:256], pu, plo,
                                                                    start=True, stop=True))
                    op("vector", lambda e, npu=npu: e.tensor_copy(npu, pA[:, 0:128]),
                       deps=("tensor",))
                    op("vector", lambda e, nplo=nplo: e.tensor_copy(nplo, pA[:, 128:256]))
                    op("vector", lambda e, npu=npu: e.scalar_tensor_tensor(
                        Fb[:, :], npu, 1.0, self.eyeb[:, :], ALU.mult, ALU.add), deps=("vector",))
                    op("tensor", lambda e, clo=clo: e.matmul(pA[:, 256:384], clo, Fb[:, :],
                                                             start=True, stop=True), deps=("vector",))
                    if not last:
                        op("tensor", lambda e, clo=clo: e.matmul(pA[:, 384:512], Fb[:, :], clo,
                                                                 start=True, stop=True))
                    ncu = Pq[:, (gi % 2) * 256:(gi % 2) * 256 + 128]
                    nclo = Pq[:, (gi % 2) * 256 + 128:(gi % 2) * 256 + 256]
                    op("vector", lambda e, ncu=ncu: e.tensor_copy(ncu, pA[:, 256:384]), deps=("tensor",))
                    if not last:
                        op("vector", lambda e, nclo=nclo: e.tensor_copy(nclo, pA[:, 384:512]))
                    pu, plo = npu, nplo
                    cu, clo = ncu, nclo
                P.add("sync", lambda e, gck=gck, cu=cu: e.dma_start(g["dMIT"][gck], cu),
                      waits=[(self.esem("vector"), P.cur(self.esem("vector")))],
                      incs=[(self.esem("sync"), 16)])
                vready = [(self.esem("vector"), prep_done)]
                P.add("sync", lambda e, gck=gck, ck=ck: e.dma_start(
                    g["dPF"][gck][:, 0:64], AQK["f"][:, ck * C:(ck + 1) * C]),
                    waits=vready, incs=[(self.esem("sync"), 16)])
                P.add("sync", lambda e, gck=gck, ck=ck: e.dma_start(
                    g["dPF"][gck][:, 64:128], AQK["f"][:, PT + ck * C:PT + (ck + 1) * C]),
                    incs=[(self.esem("sync"), 16)])
                P.add("sync", lambda e, gck=gck, ck=ck: e.dma_start(
                    g["dPF"][gck][:, 128:192], AQK["f"][:, 2 * PT + ck * C:2 * PT + (ck + 1) * C]),
                    incs=[(self.esem("sync"), 16)])
                P.add("sync", lambda e, gck=gck, ck=ck: e.dma_start(
                    g["dPS"][gck][:, 0:64], AQK["s"][:, ck * C:(ck + 1) * C]),
                    incs=[(self.esem("sync"), 16)])
                P.add("sync", lambda e, gck=gck, ck=ck: e.dma_start(
                    g["dPS"][gck][:, 64:128], AQK["s"][:, PT + ck * C:PT + (ck + 1) * C]),
                    incs=[(self.esem("sync"), 16)])
                P.add("sync", lambda e, gck=gck, ck=ck: e.dma_start(
                    g["dPS"][gck][:, 128:192], AQK["s"][:, 2 * PT + ck * C:2 * PT + (ck + 1) * C]),
                    incs=[(self.esem("sync"), 16)])
                for m_, Rp in enumerate((R_pw0, R_pw1, R_pw2)):
                    P.add("sync", lambda e, gck=gck, m_=m_, Rp=Rp, ck=ck: e.dma_start(
                        g["dROWS"][gck][0:1, (2 + m_) * C:(3 + m_) * C], Rp[0:1, ck * C:(ck + 1) * C]),
                        incs=[(self.esem("sync"), 16)])
                P.add("sync", lambda e, gck=gck, ck=ck: e.dma_start(
                    g["dV"][gck], vv[:, ck * C:(ck + 1) * C]), incs=[(self.esem("sync"), 16)])
                op("vector", lambda e, ck=ck, par=par: e.tensor_scalar(
                    xf[:, 3 * PT + par * C:3 * PT + (par + 1) * C], vv[:, ck * C:(ck + 1) * C],
                    SQK, None, ALU.mult), extra_waits=scr_wait)
                P.add("sync", lambda e, gck=gck, par=par: e.dma_start(
                    g["dVs"][gck], xf[:, 3 * PT + par * C:3 * PT + (par + 1) * C]),
                    waits=[(self.esem("vector"), P.cur(self.esem("vector")))],
                    incs=[(self.esem("sync"), 16)])
                P.add("sync", lambda e, gck=gck, ck=ck: e.dma_start(
                    g["dNU"][gck], nu[:, ck * C:(ck + 1) * C]),
                    waits=[(self.esem("scalar"), P.cur(self.esem("scalar")))],
                    incs=[(self.esem("sync"), 16)])
                pc_scr = xf[:, 3 * PT + 2 * C + par * 2:3 * PT + 2 * C + par * 2 + 2]
                op("vector", lambda e, pc_scr=pc_scr, ck=ck: e.tensor_copy(
                    pc_scr[:, 0:1], S["f"]["Pp"][:, ck * C + C - 1:ck * C + C]), extra_waits=scr_wait)
                op("vector", lambda e, pc_scr=pc_scr, ck=ck: e.tensor_copy(
                    pc_scr[:, 1:2], S["s"]["Pp"][:, ck * C + C - 1:ck * C + C]))
                P.add("sync", lambda e, gck=gck, pc_scr=pc_scr: e.dma_start(g["dPC"][gck], pc_scr),
                      waits=[(self.esem("vector"), P.cur(self.esem("vector")))],
                      incs=[(self.esem("sync"), 16)])
                self.sync_after[gck] = P.cur(self.esem("sync"))
        self.a1_end = self.barrier_waits()

    def emit_B(self, ex, ar, arb):
        nc, P, g = self.nc, self.P, self.g
        ps = lambda name, shape, dt=F32: ex.enter_context(nc.psum_tensor(name, shape, dt))
        op = self.op
        NCK = self.NCK
        Ev = self.esem("vector"); Et = self.esem("tensor"); Es_ = self.esem("scalar")
        Esy = self.esem("sync")

        outar = ar.alloc(self.T)
        self.outar = outar
        F0 = ar.alloc(128); S0 = ar.alloc(128); nSp = ar.alloc(1)
        FLAGS = ar.alloc(max(self.NCK, 64))
        self.flags_view = FLAGS
        oas = ar.alloc(192)
        esq = ar.alloc(64)
        vrow = ar.alloc(64)
        stream = []
        for par in range(2):
            stream.append(dict(
                PF=ar.alloc(192), PS=ar.alloc(192), V=ar.alloc(64), Vs=ar.alloc(64),
                NU=ar.alloc(64), PC=ar.alloc(2), ROWS=ar.alloc(512),
                MITf=arb.alloc(64), MITs=arb.alloc(64),
                GQTf=arb.alloc(64), GQTs=arb.alloc(64),
                GOTf=arb.alloc(64), GOTs=arb.alloc(64),
                URf=arb.alloc(128), URs=arb.alloc(128),
                PWBC=ar.alloc(192), SROW=ar.alloc(64),
            ))
        Wb = arb.alloc(128)
        WbTf = arb.alloc(128)
        WbTs = arb.alloc(128)
        SWf = arb.alloc(128)
        SWs = arb.alloc(128)

        psB1 = ps("psB1", [128, 128])
        pT1 = ex.enter_context(nc.psum_tensor("pT1", [64, 128], BF16))
        pT2 = ex.enter_context(nc.psum_tensor("pT2", [64, 128], BF16))
        psSW = ps("psSW", [64, 256])
        psDOO = [ps("psDOO0", [128, 192]), ps("psDOO1", [128, 192])]
        psNEW = ps("psNEW", [128, 256])
        psMISC = ps("psMISC", [128, 192])

        op("vector", lambda e: e.memset(F0[:, :], 0.0), deps=("sync", "tensor", "scalar", "gpsimd"))
        op("vector", lambda e: e.memset(outar[:, :], 0.0))
        op("vector", lambda e: e.memset(S0[:, :], 0.0))
        op("vector", lambda e: e.memset(nSp[:, :], 0.0))
        op("vector", lambda e: e.memset(FLAGS[0:1, :], 1.0))

        def stream_chunk(ck):
            if ck >= NCK:
                return
            par = ck % 2
            sb = stream[par]
            first = True
            def dma(dst, srcfn):
                nonlocal first
                w = [(Ev, P.cur(Ev)), (Et, P.cur(Et)), (Es_, P.cur(Es_))] if first else []
                first = False
                P.add("sync", lambda e, dst=dst, srcfn=srcfn: e.dma_start(dst, srcfn()), waits=w,
                      incs=[(Esy, 16)])
            dma(sb["PF"], lambda ck=ck: g["dPF"][ck])
            dma(sb["PS"], lambda ck=ck: g["dPS"][ck])
            dma(sb["V"], lambda ck=ck: g["dV"][ck])
            dma(sb["Vs"], lambda ck=ck: g["dVs"][ck])
            dma(sb["NU"], lambda ck=ck: g["dNU"][ck])
            dma(sb["PC"], lambda ck=ck: g["dPC"][ck])
            dma(sb["ROWS"][0:1, :], lambda ck=ck: g["dROWS"][ck])
            dma(sb["MITf"][0:64, :], lambda ck=ck: g["dMIT"][ck][0:64, 0:64])
            dma(sb["MITs"][0:64, :], lambda ck=ck: g["dMIT"][ck][64:128, 64:128])
            dma(sb["GQTf"][0:64, :], lambda ck=ck: g["dGQT"][ck][0:64, :])
            dma(sb["GQTs"][0:64, :], lambda ck=ck: g["dGQT"][ck][64:128, :])
            dma(sb["GOTf"][0:64, :], lambda ck=ck: g["dGOT"][ck][0:64, :])
            dma(sb["GOTs"][0:64, :], lambda ck=ck: g["dGOT"][ck][64:128, :])
            dma(sb["URf"][0:64, :], lambda ck=ck: g["dUR"][ck][0:64, :])
            dma(sb["URs"][0:64, :], lambda ck=ck: g["dUR"][ck][64:128, :])

        def do_chunk(ck):
            ST = B_STAGE
            par = ck % 2
            sb = stream[par]
            pD = psDOO[par]
            for m_ in range(3):
                op("tensor", lambda e, sb=sb, m_=m_: e.matmul(
                    psMISC[:, m_ * C:(m_ + 1) * C], self.onesrow[0:1, 0:128],
                    sb["ROWS"][0:1, (2 + m_) * C:(3 + m_) * C], start=True, stop=True),
                   deps=("vector", "sync", "scalar") if m_ == 0 else ())
            op("vector", lambda e, sb=sb: e.tensor_copy(sb["PWBC"][:, :], psMISC[:, :]),
               deps=("tensor",))
            if ST < 2:
                return
            op("tensor", lambda e, sb=sb: e.matmul(psB1[:, 0:64], F0[:, :], sb["PF"][:, 0:64],
                                                   start=True, stop=False), deps=("vector",))
            op("tensor", lambda e, sb=sb: e.matmul(psB1[:, 0:64], self.eyef[:, :], sb["V"],
                                                   start=False, stop=True))
            op("tensor", lambda e, sb=sb: e.matmul(psB1[:, 64:128], S0[:, :], sb["PS"][:, 64:128],
                                                   start=True, stop=False))
            op("tensor", lambda e, sb=sb: e.matmul(psB1[:, 64:128], self.eyef[:, :], sb["V"],
                                                   start=False, stop=True))
            op("vector", lambda e: e.tensor_copy(Wb[:, :], psB1[:, :]), deps=("tensor",))
            if ST < 3:
                return
            op("tensor", lambda e: e.transpose(pT1[:, :], Wb[:, 0:64], self.eyeb[:, :]),
               deps=("vector",))
            op("tensor", lambda e: e.transpose(pT2[:, :], Wb[:, 64:128], self.eyeb[:, :]))
            op("vector", lambda e: e.tensor_copy(WbTf[0:64, :], pT1[:, :]), deps=("tensor",))
            op("vector", lambda e: e.tensor_copy(WbTs[0:64, :], pT2[:, :]))
            if ST < 4:
                return
            op("tensor", lambda e, sb=sb: e.matmul(psSW[:, 0:128], sb["MITf"][0:64, :],
                                                   WbTf[0:64, :], start=True, stop=True),
               deps=("vector",))
            op("tensor", lambda e, sb=sb: e.matmul(psSW[:, 128:256], sb["MITs"][0:64, :],
                                                   WbTs[0:64, :], start=True, stop=True))
            op("vector", lambda e: e.tensor_copy(SWf[0:64, :], psSW[:, 0:128]), deps=("tensor",))
            op("vector", lambda e: e.tensor_copy(SWs[0:64, :], psSW[:, 128:256]))
            if ST < 5:
                return
            op("tensor", lambda e, sb=sb, pD=pD: e.matmul(pD[:, 0:64], F0[:, :],
                                                          sb["PF"][:, 128:192],
                                                          start=True, stop=False),
               deps=("vector", "scalar"))
            op("tensor", lambda e, sb=sb, pD=pD: e.matmul(pD[:, 0:64], S0[:, :],
                                                          sb["PS"][:, 0:64],
                                                          start=False, stop=False))
            op("tensor", lambda e, sb=sb, pD=pD: e.matmul(pD[:, 0:64], self.eyef[:, :],
                                                          sb["Vs"], start=False, stop=False))
            op("tensor", lambda e, sb=sb, pD=pD: e.matmul(pD[:, 0:64], SWf[0:64, :],
                                                          sb["GQTf"][0:64, :],
                                                          start=False, stop=False))
            op("tensor", lambda e, sb=sb, pD=pD: e.matmul(pD[:, 0:64], SWs[0:64, :],
                                                          sb["GQTs"][0:64, :],
                                                          start=False, stop=True))
            op("tensor", lambda e, sb=sb, pD=pD: e.matmul(pD[:, 64:128], F0[:, :],
                                                          sb["PF"][:, 64:128],
                                                          start=True, stop=False))
            op("tensor", lambda e, sb=sb, pD=pD: e.matmul(pD[:, 64:128], SWf[0:64, :],
                                                          sb["GOTf"][0:64, :],
                                                          start=False, stop=True))
            op("tensor", lambda e, sb=sb, pD=pD: e.matmul(pD[:, 128:192], S0[:, :],
                                                          sb["PS"][:, 128:192],
                                                          start=True, stop=False))
            op("tensor", lambda e, sb=sb, pD=pD: e.matmul(pD[:, 128:192], SWs[0:64, :],
                                                          sb["GOTs"][0:64, :],
                                                          start=False, stop=True))
            if ST < 6:
                return
            op("scalar", lambda e, pD=pD: e.activation(esq[:, :], pD[:, 0:64], AF.Square),
               deps=("tensor",))
            if ST == 60:
                return
            op("tensor", lambda e: e.matmul(psMISC[0:1, 0:64], self.consts[:, 0:1], esq[:, :],
                                            start=True, stop=True), deps=("scalar", "vector"))
            if ST == 61:
                return
            op("scalar", lambda e, sb=sb: e.activation(sb["SROW"][0:1, :], psMISC[0:1, 0:64],
                                                       AF.Sigmoid, scale=float(1.0 / (1.0 + 1e-6))),
               deps=("tensor",))
            if ST == 62:
                return
            op("vector", lambda e: e.tensor_scalar(vrow[0:1, :], psMISC[0:1, 0:64], 38.0, None,
                                                   ALU.is_ge), deps=("tensor", "scalar"))
            if ST == 63:
                return
            op("vector", lambda e, ck=ck: e.tensor_reduce(FLAGS[0:1, ck:ck + 1], vrow[0:1, :],
                                                          mybir.AxisListType.X, ALU.min),
               deps=("vector",))
            if ST == 64:
                return
            if ST < 7:
                return
            srow = sb["SROW"]
            op("vector", lambda e, srow=srow: e.tensor_scalar(oas[0:1, 0:C], srow[0:1, :], 0.7,
                                                              None, ALU.is_gt), deps=("scalar",))
            op("vector", lambda e: e.tensor_scalar(oas[0:1, C:2 * C], oas[0:1, 0:C], -0.1, 1.0,
                                                   ALU.mult, ALU.add), deps=("vector",))
            op("tensor", lambda e: e.matmul(psMISC[:, 0:C], self.onesrow[0:1, 0:128],
                                            oas[0:1, 0:C], start=True, stop=True),
               deps=("vector", "scalar"))
            op("tensor", lambda e: e.matmul(psMISC[:, C:2 * C], self.onesrow[0:1, 0:128],
                                            oas[0:1, C:2 * C], start=True, stop=True))
            op("vector", lambda e, sb=sb: e.tensor_tensor(oas[:, 0:C], sb["NU"], psMISC[:, 0:C],
                                                          ALU.mult), deps=("tensor",))
            op("vector", lambda e: e.tensor_copy(oas[:, C:2 * C], psMISC[:, C:2 * C]))
            op("vector", lambda e: e.tensor_tensor(oas[:, 2 * C:2 * C + 1], nSp[:, :],
                                                   oas[:, C:C + 1], ALU.mult), deps=("vector",))
            op("vector", lambda e: e.tensor_tensor(oas[:, 0:1], oas[:, 0:1], oas[:, 2 * C:2 * C + 1],
                                                   ALU.add), deps=("vector",))
            op("vector", lambda e: e.tensor_tensor_scan(oas[:, 2 * C:3 * C], oas[:, C:2 * C],
                                                        oas[:, 0:C], 0.0, ALU.mult, ALU.add),
               deps=("vector",))
            op("vector", lambda e: e.tensor_copy(nSp[:, :], oas[:, 3 * C - 1:3 * C]),
               deps=("vector",))
            oc = outar[:, ck * C:(ck + 1) * C]
            op("vector", lambda e, sb=sb, oc=oc, pD=pD: e.tensor_tensor(
                oc, pD[:, 64:128], sb["PWBC"][:, 0:C], ALU.mult), deps=("scalar",))
            op("vector", lambda e, sb=sb, pD=pD: e.tensor_tensor(
                oas[:, 0:C], pD[:, 128:192], sb["PWBC"][:, C:2 * C], ALU.mult))
            op("vector", lambda e, sb=sb: e.tensor_tensor(
                oas[:, C:2 * C], oas[:, 2 * C:3 * C], sb["PWBC"][:, 2 * C:3 * C], ALU.mult),
               deps=("vector",))
            op("vector", lambda e, oc=oc: e.tensor_tensor(oc, oc, oas[:, 0:C], ALU.add),
               deps=("vector",))
            op("vector", lambda e, oc=oc: e.tensor_tensor(oc, oc, oas[:, C:2 * C], ALU.add),
               deps=("vector",))
            if ST < 8:
                return
            op("tensor", lambda e, sb=sb: e.matmul(psNEW[:, 0:128], sb["URf"][0:64, :],
                                                   SWf[0:64, :], start=True, stop=True),
               deps=("vector",))
            op("tensor", lambda e, sb=sb: e.matmul(psNEW[:, 128:256], sb["URs"][0:64, :],
                                                   SWs[0:64, :], start=True, stop=True))
            op("vector", lambda e, sb=sb: e.scalar_tensor_tensor(
                F0[:, :], F0[:, :], sb["PC"][:, 0:1], psNEW[:, 0:128], ALU.mult, ALU.add),
               deps=("tensor",))
            op("vector", lambda e, sb=sb: e.scalar_tensor_tensor(
                S0[:, :], S0[:, :], sb["PC"][:, 1:2], psNEW[:, 128:256], ALU.mult, ALU.add))

        def emit_dbg(tag):
            nc_ = self.nc
            self.op("vector", lambda e: e.tensor_copy(oas[:, :], psDOO[0][:, :]),
                    deps=("tensor", "scalar", "vector"))
            views = dict(Wb=(Wb, BF16), WbTf=(WbTf[0:64, :], BF16), WbTs=(WbTs[0:64, :], BF16),
                         SWf=(SWf[0:64, :], BF16), SWs=(SWs[0:64, :], BF16),
                         pD=(oas, F32), PWBC=(stream[0]["PWBC"], F32),
                         SROW=(stream[0]["SROW"], F32),
                         F0=(F0, F32), S0=(S0, F32), FLAGS=(FLAGS[0:1, :], F32))
            w = self.barrier_waits()
            for nm, (v, dt) in views.items():
                t = nc_.dram_tensor(f"o_dbg{tag}_{nm}", list(v.shape), dt,
                                    kind="ExternalOutput").ap()
                self.P.add("sync", lambda e, t=t, v=v: e.dma_start(t, v),
                           waits=w, incs=[(self.esem("sync"), 16)])
                w = ()

        stream_chunk(0)
        stream_chunk(1)
        for ck in range(NCK):
            do_chunk(ck)
            if ck == 0 and getattr(self, "_dbgB", False):
                emit_dbg("0")
            stream_chunk(ck + 2)
        self.b_end = self.barrier_waits()


    def emit_C(self, ex, ar, arb):
        nc, P, g, op = self.nc, self.P, self.g, self.op
        NC_ = self.n_cores
        my_tok = self.T // NC_
        NBLK = my_tok // 512
        assert my_tok % 512 == 0
        NFF = 11
        NSH = 22

        dOUTA = nc.dram_tensor("dOUTA", [NBLK, NC_, 128, 512], F32).ap()
        o_y = nc.dram_tensor("o_y", [16, 128, my_tok], F32, kind="ExternalOutput").ap()
        xT4 = [g[f"x4d_{m_}"] for m_ in range(16)]
        _cidc = {}

        def cid(e):
            if NC_ == 1:
                return 0
            k = id(e)
            if k not in _cidc:
                _cidc[k] = e.partition_id()
            return _cidc[k]

        myat = ar.alloc(512)
        sq = ar.alloc(512)
        hsq = [ar.alloc(512), ar.alloc(512)]
        hbuf = ar.alloc(16 * 512)
        xc = [ar.alloc(512), ar.alloc(512)]
        ysb = ar.alloc(16 * 512)
        w1l = ar.alloc(2048)
        w3l = ar.alloc(2048)
        w2l = ar.alloc(1408)
        gsc = ar.alloc(96)
        h1b = arb.alloc(NFF * 512)
        w1b = arb.alloc(2048)
        w3b = arb.alloc(2048)
        w2b = arb.alloc(1408)
        wpbA = Arena(self.wpb[:, :], 6 * NCH * 128)
        h2bf = wpbA.alloc(16 * 512)
        ybf = wpbA.alloc(512)
        gwb = wpbA.alloc(256)
        crows = ar.alloc(2560)
        geAll = ar.alloc(512)
        gerow = ar.alloc(512)

        pA = ex.enter_context(nc.psum_tensor("cpA", [128, 512], F32))
        pB = ex.enter_context(nc.psum_tensor("cpB", [128, 512], F32))
        pC = ex.enter_context(nc.psum_tensor("cpC", [128, 512], F32))
        pY = [ex.enter_context(nc.psum_tensor("cpY0", [128, 512], F32)),
              ex.enter_context(nc.psum_tensor("cpY1", [128, 512], F32))]
        pR1 = ex.enter_context(nc.psum_tensor("cpR1", [1, 512], F32))
        pR2 = ex.enter_context(nc.psum_tensor("cpR2", [1, 512], F32))

        Esy = self.esem("sync")

        for b_ in range(NBLK):
            P.add("sync", lambda e, b_=b_: e.dma_start(
                dOUTA[b_].rearrange("c p t -> p c t"),
                self.outar.rearrange("p (c b t) -> p c b t", c=NC_, b=NBLK)[:, :, b_, :]),
                waits=self.barrier_waits() if b_ == 0 else (), incs=[(Esy, 16)])
        P.add("sync", lambda e: e.dma_start(
            w1l[:, 0:256].rearrange("p (s e) -> p s e", s=NCH),
            g["gate_wT"].rearrange("(s p) e -> p s e", p=128)), incs=[(Esy, 16)])
        op("vector", lambda e: e.tensor_copy(gwb[:, :], w1l[:, 0:256]), deps=("sync",))
        op("vector", lambda e: e.memset(ysb[:, :], 0.0))

        mc = self.mcols
        for b in range(NBLK):
            tb = b * 512

            P.add("sync", lambda e, b=b: e.dma_start(
                myat[:, :], dOUTA[b][cid(e), :, :]),
                waits=[(self.esem("vector"), P.cur(self.esem("vector")))], incs=[(Esy, 16)])
            op("vector", lambda e: e.tensor_tensor(sq[:, :], myat[:, :], myat[:, :], ALU.mult),
               deps=("sync",))
            op("tensor", lambda e: e.matmul(pR1[0:1, :], self.consts[:, 0:1], sq[:, :],
                                            start=True, stop=True), deps=("vector", "scalar"))
            op("scalar", lambda e: e.activation(crows[0:1, 0:512], pR1[0:1, :], AF.Sqrt,
                                                scale=float(1.0 / 128.0),
                                                bias=self.rows0[0:1, 31:32]), deps=("tensor",))
            op("vector", lambda e: e.reciprocal(crows[0:1, 512:1024], crows[0:1, 0:512]),
               deps=("scalar",))
            op("tensor", lambda e: e.matmul(pA[:, :], self.onesrow[0:1, 0:128],
                                            crows[0:1, 512:1024], start=True, stop=True),
               deps=("vector",))
            op("vector", lambda e: e.tensor_tensor(ybf[:, :], myat[:, :], pA[:, :], ALU.mult),
               deps=("tensor",))

            for m in range(16):
                par = m % 2
                pO = pB if par == 0 else pC
                P.add("sync", lambda e, m=m, b=b, par=par: e.dma_start(
                    xc[par][:, :], xT4[m][b][cid(e), :, :]),
                    waits=[(self.esem("vector"), P.cur(self.esem("vector")))], incs=[(Esy, 16)])
                op("tensor", lambda e, m=m, pO=pO: e.matmul(
                    pO[:, :], self.wof[:, m * 128:(m + 1) * 128], ybf[:, :],
                    start=True, stop=True), deps=("vector",))
                op("vector", lambda e, m=m, pO=pO, par=par: e.scalar_tensor_tensor(
                    hbuf[:, m * 512:(m + 1) * 512], pO[:, :], mc[:, 32 + m:33 + m],
                    xc[par][:, :], ALU.mult, ALU.add), deps=("tensor", "sync"))

            for m in range(16):
                op("tensor", lambda e, m=m: e.matmul(
                    pR1[0:1, :], self.consts[:, 0:1], hbuf[:, m * 512:(m + 1) * 512],
                    start=(m == 0), stop=(m == 15)),
                   deps=("vector",) if m == 0 else ())
            for m in range(16):
                par = m % 2
                op("vector", lambda e, m=m, par=par: e.tensor_tensor(
                    hsq[par][:, :], hbuf[:, m * 512:(m + 1) * 512],
                    hbuf[:, m * 512:(m + 1) * 512], ALU.mult),
                   deps=("tensor",) if m < 2 else ("tensor",))
                op("tensor", lambda e, m=m, par=par: e.matmul(
                    pR2[0:1, :], self.consts[:, 0:1], hsq[par][:, :],
                    start=(m == 0), stop=(m == 15)), deps=("vector",))

            op("vector", lambda e: e.tensor_scalar(crows[0:1, 0:512], pR1[0:1, :],
                                                   float(1.0 / 2048.0), None, ALU.mult),
               deps=("tensor",))
            op("vector", lambda e: e.tensor_tensor(crows[0:1, 512:1024], crows[0:1, 0:512],
                                                   crows[0:1, 0:512], ALU.mult),
               deps=("vector",))
            op("vector", lambda e: e.tensor_scalar(crows[0:1, 1024:1536], pR2[0:1, :],
                                                   float(1.0 / 2048.0), EPS_LN,
                                                   ALU.mult, ALU.add), deps=("vector",))
            op("vector", lambda e: e.tensor_tensor(crows[0:1, 1024:1536], crows[0:1, 1024:1536],
                                                   crows[0:1, 512:1024], ALU.subtract),
               deps=("vector",))
            op("scalar", lambda e: e.activation(crows[0:1, 1536:2048], crows[0:1, 1024:1536],
                                                AF.Sqrt), deps=("vector",))
            op("vector", lambda e: e.reciprocal(crows[0:1, 512:1024], crows[0:1, 1536:2048]),
               deps=("scalar",))
            op("vector", lambda e: e.tensor_tensor(crows[0:1, 2048:2560], crows[0:1, 0:512],
                                                   crows[0:1, 512:1024], ALU.mult),
               deps=("vector",))
            op("tensor", lambda e: e.matmul(pA[:, :], self.onesrow[0:1, 0:128],
                                            crows[0:1, 512:1024], start=True, stop=True),
               deps=("vector",))
            op("tensor", lambda e: e.matmul(pB[:, :], self.onesrow[0:1, 0:128],
                                            crows[0:1, 2048:2560], start=True, stop=True))

            for m in range(16):
                op("vector", lambda e, m=m: e.tensor_tensor(
                    sq[:, :], hbuf[:, m * 512:(m + 1) * 512], pA[:, :], ALU.mult),
                   deps=("tensor",) if m == 0 else ("vector",))
                op("vector", lambda e: e.tensor_tensor(sq[:, :], sq[:, :], pB[:, :],
                                                       ALU.subtract), deps=("vector",))
                op("vector", lambda e, m=m: e.tensor_scalar(
                    h2bf[:, m * 512:(m + 1) * 512], sq[:, :], mc[:, 64 + m:65 + m],
                    mc[:, 48 + m:49 + m], ALU.mult, ALU.add), deps=("vector",))

            for s4 in range(4):
                for s in range(16):
                    op("tensor", lambda e, s=s, s4=s4: e.matmul(
                        pC[:, 0:16], h2bf[:, 512 * s + 128 * s4:512 * s + 128 * s4 + 128],
                        gwb[:, 16 * s:16 * s + 16], start=(s == 0), stop=(s == 15)),
                       deps=("vector", "scalar") if s == 0 else ())
                op("scalar", lambda e: e.activation(gsc[:, 0:16], pC[:, 0:16], AF.Sigmoid),
                   deps=("tensor",))
                for gi in range(4):
                    op("vector", lambda e, gi=gi: e.tensor_reduce(
                        gsc[:, 16 + gi:17 + gi], gsc[:, 4 * gi:4 * gi + 4],
                        mybir.AxisListType.X, ALU.max),
                       deps=("scalar",) if gi == 0 else ("vector",))
                op("vector", lambda e: e.tensor_reduce(gsc[:, 20:21], gsc[:, 16:20],
                                                       mybir.AxisListType.X, ALU.max),
                   deps=("vector",))
                op("vector", lambda e: e.tensor_scalar(gsc[:, 24:28], gsc[:, 16:20],
                                                       gsc[:, 20:21], None, ALU.is_ge),
                   deps=("vector",))
                op("vector", lambda e: e.scalar_tensor_tensor(gsc[:, 28:32], gsc[:, 24:28],
                                                              -1e9, gsc[:, 16:20],
                                                              ALU.mult, ALU.add),
                   deps=("vector",))
                op("vector", lambda e: e.tensor_reduce(gsc[:, 21:22], gsc[:, 28:32],
                                                       mybir.AxisListType.X, ALU.max),
                   deps=("vector",))
                op("vector", lambda e: e.tensor_scalar(gsc[:, 24:28], gsc[:, 16:20],
                                                       gsc[:, 21:22], None, ALU.is_ge),
                   deps=("vector",))
                op("vector", lambda e: e.tensor_tensor(
                    gsc[:, 32:48].rearrange("p (a z) -> p a z", a=4),
                    gsc[:, 0:16].rearrange("p (a z) -> p a z", a=4),
                    gsc[:, 24:28].rearrange("p (a z) -> p a z", a=4).broadcast_to([128, 4, 4]),
                    ALU.mult), deps=("vector",))
                op("vector", lambda e: e.tensor_copy(gsc[:, 48:64], gsc[:, 32:48]),
                   deps=("vector",))
                for it in range(3):
                    op("vector", lambda e: e.tensor_reduce(gsc[:, 22:23], gsc[:, 48:64],
                                                           mybir.AxisListType.X, ALU.max),
                       deps=("vector",))
                    op("vector", lambda e: e.tensor_scalar(gsc[:, 64:80], gsc[:, 48:64],
                                                           gsc[:, 22:23], None, ALU.is_ge),
                       deps=("vector",))
                    op("vector", lambda e: e.scalar_tensor_tensor(gsc[:, 48:64], gsc[:, 64:80],
                                                                  -1e9, gsc[:, 48:64],
                                                                  ALU.mult, ALU.add),
                       deps=("vector",))
                op("vector", lambda e: e.tensor_reduce(gsc[:, 23:24], gsc[:, 48:64],
                                                       mybir.AxisListType.X, ALU.max),
                   deps=("vector",))
                op("vector", lambda e: e.tensor_scalar(gsc[:, 64:80], gsc[:, 32:48],
                                                       gsc[:, 23:24], None, ALU.is_ge),
                   deps=("vector",))
                op("vector", lambda e: e.tensor_tensor(gsc[:, 80:96], gsc[:, 0:16],
                                                       gsc[:, 64:80], ALU.mult),
                   deps=("vector",))
                op("vector", lambda e: e.tensor_reduce(gsc[:, 20:21], gsc[:, 80:96],
                                                       mybir.AxisListType.X, ALU.add),
                   deps=("vector",))
                op("vector", lambda e: e.reciprocal(gsc[:, 21:22], gsc[:, 20:21]),
                   deps=("vector",))
                op("vector", lambda e: e.tensor_scalar(gsc[:, 32:48], gsc[:, 80:96],
                                                       gsc[:, 21:22], None, ALU.mult),
                   deps=("vector",))
                op("tensor", lambda e: e.transpose(pA[0:16, 0:128], gsc[:, 32:48],
                                                   self.eyef[:, :]), deps=("vector",))
                op("vector", lambda e, s4=s4: e.tensor_copy(
                    geAll[0:16, 128 * s4:128 * s4 + 128], pA[0:16, 0:128]), deps=("tensor",))

            def ffn_branch(is_shared, e_idx):
                nch_ff = NSH if is_shared else NFF
                halves = 2 if is_shared else 1
                per = nch_ff // halves
                if not is_shared:
                    P.add("sync", lambda e: e.dma_start(gerow[0:1, :], geAll[e_idx:e_idx + 1, :]),
                          waits=[(self.esem("tensor"), P.cur(self.esem("tensor"))),
                                 (self.esem("vector"), P.cur(self.esem("vector")))],
                          incs=[(Esy, 16)])
                    op("tensor", lambda e: e.matmul(pA[:, :], self.onesrow[0:1, 0:128],
                                                    gerow[0:1, :], start=True, stop=True),
                       deps=("vector", "sync"))
                for hf in range(halves):
                    for c in range(per):
                        ch = hf * per + c
                        if is_shared:
                            s1 = g["Ws1T"][:, 128 * ch:128 * ch + 128]
                            s3 = g["Ws3T"][:, 128 * ch:128 * ch + 128]
                        else:
                            s1 = g["W1T"][e_idx][:, 128 * ch:128 * ch + 128]
                            s3 = g["W3T"][e_idx][:, 128 * ch:128 * ch + 128]
                        P.add("sync", lambda e, s1=s1: e.dma_start(
                            w1l.rearrange("p (s c) -> p s c", s=NCH),
                            s1.rearrange("(s p) c -> p s c", p=128)),
                            waits=[(self.esem("vector"), P.cur(self.esem("vector")))],
                            incs=[(Esy, 16)])
                        P.add("sync", lambda e, s3=s3: e.dma_start(
                            w3l.rearrange("p (s c) -> p s c", s=NCH),
                            s3.rearrange("(s p) c -> p s c", p=128)), incs=[(Esy, 16)])
                        op("vector", lambda e: e.tensor_copy(w1b[:, :], w1l[:, :]),
                           deps=("sync", "tensor"))
                        op("vector", lambda e: e.tensor_copy(w3b[:, :], w3l[:, :]))
                        for s in range(16):
                            op("tensor", lambda e, s=s: e.matmul(
                                pB[:, :], w1b[:, 128 * s:128 * s + 128],
                                h2bf[:, 512 * s:512 * s + 512], start=(s == 0), stop=(s == 15)),
                               deps=("vector", "scalar") if s == 0 else ())
                        for s in range(16):
                            op("tensor", lambda e, s=s: e.matmul(
                                pC[:, :], w3b[:, 128 * s:128 * s + 128],
                                h2bf[:, 512 * s:512 * s + 512], start=(s == 0), stop=(s == 15)),
                               deps=("vector",) if s == 0 else ())
                        op("scalar", lambda e: e.activation(sq[:, :], pB[:, :], AF.Sigmoid),
                           deps=("tensor",))
                        op("vector", lambda e: e.tensor_tensor(sq[:, :], sq[:, :], pB[:, :],
                                                               ALU.mult),
                           deps=("scalar", "vector"))
                        op("vector", lambda e: e.tensor_tensor(sq[:, :], sq[:, :], pC[:, :],
                                                               ALU.mult), deps=("vector",))
                        if is_shared:
                            op("vector", lambda e, c=c: e.tensor_copy(
                                h1b[:, 512 * c:512 * c + 512], sq[:, :]), deps=("vector",))
                        else:
                            op("vector", lambda e, c=c: e.tensor_tensor(
                                h1b[:, 512 * c:512 * c + 512], sq[:, :], pA[:, :], ALU.mult),
                               deps=("vector",))
                    for m in range(16):
                        if is_shared:
                            s2 = g["Ws2T"][1408 * hf:1408 * (hf + 1), 128 * m:128 * m + 128]
                        else:
                            s2 = g["W2T"][e_idx][:, 128 * m:128 * m + 128]
                        P.add("sync", lambda e, s2=s2: e.dma_start(
                            w2l.rearrange("p (s c) -> p s c", s=NFF),
                            s2.rearrange("(s p) c -> p s c", p=128)),
                            waits=[(self.esem("vector"), P.cur(self.esem("vector")))],
                            incs=[(Esy, 16)])
                        op("vector", lambda e: e.tensor_copy(w2b[:, :], w2l[:, :]),
                           deps=("sync", "tensor"))
                        pYm = pY[m % 2]
                        for c in range(per):
                            op("tensor", lambda e, c=c, pYm=pYm: e.matmul(
                                pYm[:, :], w2b[:, 128 * c:128 * c + 128],
                                h1b[:, 512 * c:512 * c + 512], start=(c == 0),
                                stop=(c == per - 1)),
                               deps=("vector",) if c == 0 else ())
                        op("vector", lambda e, m=m, pYm=pYm: e.tensor_tensor(
                            ysb[:, 512 * m:512 * m + 512], ysb[:, 512 * m:512 * m + 512],
                            pYm[:, :], ALU.add), deps=("tensor",))

            for e_idx in range(16):
                ffn_branch(False, e_idx)
            ffn_branch(True, 0)

            if b == 0 and getattr(self, "_dbgC", False):
                wdb = self.barrier_waits()
                for nm, v, dt in [("hbuf", hbuf, F32), ("h2bf", h2bf, BF16),
                                  ("geAll", geAll[0:16, :], F32), ("ysb", ysb, F32),
                                  ("myat", myat, F32), ("crows", crows[0:1, :], F32),
                                  ("ybf", ybf, BF16), ("wof", self.wof[:, :], BF16),
                                  ("mcols", self.mcols[:, :], F32)]:
                    tdb = nc.dram_tensor(f"o_dbgC_{nm}", list(v.shape), dt,
                                         kind="ExternalOutput").ap()
                    P.add("sync", lambda e, t=tdb, v=v: e.dma_start(t, v),
                          waits=wdb, incs=[(Esy, 16)])
                    wdb = ()
            for m in range(16):
                op("vector", lambda e, m=m: e.scalar_tensor_tensor(
                    hbuf[:, m * 512:(m + 1) * 512], ysb[:, m * 512:(m + 1) * 512],
                    mc[:, 80 + m:81 + m], hbuf[:, m * 512:(m + 1) * 512],
                    ALU.mult, ALU.add),
                   deps=("vector", "sync") if m == 0 else ("vector",))
                P.add("sync", lambda e, m=m, tb=tb: e.dma_start(
                    o_y[m][:, tb:tb + 512], hbuf[:, m * 512:(m + 1) * 512]),
                    waits=[(self.esem("vector"), P.cur(self.esem("vector")))],
                    incs=[(Esy, 16)])
            if b + 1 < NBLK:
                op("vector", lambda e: e.memset(ysb[:, :], 0.0), deps=("tensor",))

    def finalize(self):
        if hasattr(self, "_dbgcols"):
            t = self.nc.dram_tensor("o_dbgcols", [128, 160], F32, kind="ExternalOutput").ap()
            self.P.add("sync", lambda e, t=t: e.dma_start(t, self.cols[:, :]),
                       waits=self.barrier_waits(), incs=[(self.esem("sync"), 16)])
        self.P.add("sync", lambda e: None, waits=self.barrier_waits())


def build(n_tok, prep_out=True, dbgcols=False, do_B=False, out_B=False, do_C=False,
          n_cores=8):
    from contextlib import ExitStack
    nc = bass.Bass("TRN2", target_bir_lowering=False, debug=False)
    mk = MK(nc, n_tok, n_cores=n_cores)
    if dbgcols: mk._dbgcols = True
    if prep_out:
        pass
    g = mk.declare_io()
    if prep_out:
        NCK = mk.NCK
        for nm, shp, dt in [("dPF", [NCK, 128, 192], F32), ("dPS", [NCK, 128, 192], F32),
                            ("dV", [NCK, 128, 64], F32),
                            ("dVs", [NCK, 128, 64], F32), ("dMIT", [NCK, 128, 128], BF16),
                            ("dGQT", [NCK, 128, C], BF16), ("dGOT", [NCK, 128, C], BF16),
                            ("dUR", [NCK, 128, 128], BF16),
                            ("dNU", [NCK, 128, C], F32), ("dROWS", [NCK, 1, 512], F32),
                            ("dPC", [NCK, 128, 2], F32)]:
            g[nm] = nc.dram_tensor("o_" + nm, shp, dt, kind="ExternalOutput").ap()
    from contextlib import ExitStack
    with ExitStack() as ex:
        arT = ex.enter_context(nc.sbuf_tensor("arenaF", [128, 29696], F32))
        arbT = ex.enter_context(nc.sbuf_tensor("arenaB", [128, 16384], BF16))
        ar = Arena(arT[:, :], 29696)
        arb = Arena(arbT[:, :], 16384)
        with ExitStack() as ex_psum:
            mk.emit_A0(ex, ar, ex_psum)
        ar.pop()
        ar.push()
        with ExitStack() as ex_psA1:
            mk.emit_A1(ex_psA1, ar, arb)
        if do_B:
            ar.pop()
            ar.push()
            arb2 = Arena(arbT[:, :], 16384)
            with ExitStack() as ex_psB:
                mk.emit_B(ex_psB, ar, arb2)
            if out_B:
                t = nc.dram_tensor("o_dOUT", [128, mk.T], F32, kind="ExternalOutput").ap()
                mk.P.add("sync", lambda e, t=t: e.dma_start(t, mk.outar),
                         waits=mk.barrier_waits(), incs=[(mk.esem("sync"), 16)])
        if do_C:
            ar.pop()
            arbC = Arena(arbT[:, :], 16384)
            with ExitStack() as ex_psC:
                mk.emit_C(ex_psC, ar, arbC)
        mk.finalize()
        with nc.Block() as block:
            mk.P.emit(block)
    return nc, mk


class ex_psB_wrap:
    def __init__(self, ex_outer, ex_psum):
        self.ex_outer = ex_outer
        self.ex_psum = ex_psum

    def enter_context(self, cm):
        return self.ex_psum.enter_context(cm)

build_phaseA = build



from concourse.bass_utils import run_bass_kernel_spmd

_IN_NAMES = ["xT", "Wk", "Wv", "Wq", "Waf", "Was", "ir_w1", "pw_w", "nm_w1", "nm_w2",
             "ir_w2", "Wo", "wmod", "ccolT", "baf", "bas", "lamf", "lams", "nm_b1",
             "nm_b2", "ir_b1", "ir_b2", "pw_b", "norm_w", "gate_wT", "W1T", "W3T",
             "W2T", "Ws1T", "Ws3T", "Ws2T", "ones_col", "twos_col", "ones_row",
             "eye128", "resetmask", "mSTRICT", "mSTRICTN", "mINCL", "eye64x2"] + [
             f"x4d_{m}" for m in range(16)]

N_TOK = 8192
N_CORES = 8


def kernel(**inputs):
    nc, mk = build(N_TOK, prep_out=False, do_B=True, do_C=True, n_cores=N_CORES)
    gh = host_layout(inputs, N_TOK, n_cores=N_CORES)
    in_map = {nm: np.ascontiguousarray(gh[nm], dtype=np.float32) for nm in _IN_NAMES}
    res = run_bass_kernel_spmd(nc, [in_map] * N_CORES, core_ids=list(range(N_CORES)))
    my_tok = N_TOK // N_CORES
    out = np.zeros((1, N_TOK, 2048), dtype=np.float32)
    for c in range(N_CORES):
        oy = np.asarray(res.results[c]["o_y"], dtype=np.float32)
        out[0, c * my_tok:(c + 1) * my_tok, :] = \
            oy.transpose(2, 0, 1).reshape(my_tok, 2048)
    return out


# version 02
# speedup vs baseline: 1.2767x; 1.2767x over previous
import numpy as np
import concourse.bass as bass
import concourse.mybir as mybir

F32 = mybir.dt.float32
BF16 = mybir.dt.bfloat16
AF = mybir.ActivationFunctionType
ALU = mybir.AluOpType

D = 2048
NCH = 16
DS = 128
C = 64
PT = 512
SQK = float(np.sqrt(1.0 / (1.0 + 1e-6)))
LN_05SQK = float(np.log(0.5 * SQK))
EPS_LN = 1e-5
EPS_RMS = 1e-6
B_STAGE = 99


class Prog:
    def __init__(self):
        self.ops = []
        self.count = {}

    def add(self, engine, fn, waits=(), incs=()):
        self.ops.append((engine, fn, tuple(waits), tuple(incs)))
        for sem, amt in incs:
            self.count[sem.name] = self.count.get(sem.name, 0) + amt

    def cur(self, sem):
        return self.count.get(sem.name, 0)

    def emit(self, block):
        ops = self.ops

        def body_for(engine_name):
            def body(eng):
                for e, fn, waits, incs in ops:
                    if e != engine_name:
                        continue
                    for sem, val in waits:
                        eng.wait_ge(sem, val)
                    inst = fn(eng)
                    for sem, amt in incs:
                        inst = inst.then_inc(sem, amt)
            return body

        block.tensor(body_for("tensor"))
        block.vector(body_for("vector"))
        block.scalar(body_for("scalar"))
        block.gpsimd(body_for("gpsimd"))
        block.sync(body_for("sync"))


class Arena:

    def __init__(self, tensor, ncols):
        self.t = tensor
        self.n = ncols
        self.off = 0
        self.marks = []

    def alloc(self, ncols):
        assert self.off + ncols <= self.n, f"arena overflow {self.off}+{ncols}>{self.n}"
        v = self.t[:, self.off:self.off + ncols]
        self.off += ncols
        return v

    def push(self):
        self.marks.append(self.off)

    def pop(self):
        self.off = self.marks.pop()


def host_constants():
    cst = {}
    cst["ones_col"] = np.ones((128, 1), dtype=np.float32)
    cst["twos_col"] = np.full((128, 1), 2.0, dtype=np.float32)
    cst["ones_row"] = np.ones((1, PT), dtype=np.float32)
    cst["eye128"] = np.eye(128, dtype=np.float32)
    rm = np.ones((128, PT), dtype=np.float32)
    rm[:, ::C] = 0.0
    cst["resetmask"] = rm
    up_strict = (np.arange(C)[None, :] > np.arange(C)[:, None]).astype(np.float32)
    up_incl = (np.arange(C)[None, :] >= np.arange(C)[:, None]).astype(np.float32)
    cst["mSTRICT"] = up_strict
    cst["mSTRICTN"] = -up_strict
    cst["mINCL"] = up_incl
    cst["eye64x2"] = np.kron(np.eye(2, dtype=np.float32), np.eye(C, dtype=np.float32))
    return cst


def host_layout(inputs, n_tok, n_cores=8):
    g = {}
    x = np.ascontiguousarray(inputs["L0_attn_x"][0, :n_tok, :].T)
    g["xT"] = x
    my_tok = n_tok // n_cores
    nblk = my_tok // 512
    x5 = x.reshape(16, 128, n_cores, nblk, 512).transpose(0, 3, 2, 1, 4)
    for m_ in range(16):
        g[f"x4d_{m_}"] = np.ascontiguousarray(x5[m_])
    for nm in ("Wk", "Wv", "Wq", "Waf", "Was", "ir_w1"):
        g[nm] = np.ascontiguousarray(inputs["L0_attn_" + nm])
    g["pw_w"] = np.ascontiguousarray(inputs["L0_attn_pw_w"])
    g["nm_w1"] = np.ascontiguousarray(inputs["L0_attn_nm_w1"])
    g["nm_w2"] = np.ascontiguousarray(inputs["L0_attn_nm_w2"])
    g["ir_w2"] = np.ascontiguousarray(inputs["L0_attn_ir_w2"])
    g["Wo"] = np.ascontiguousarray(inputs["L0_attn_Wo"])
    g["wmod"] = np.ascontiguousarray(inputs["L0_wmod"])
    g["ccolT"] = np.ascontiguousarray(inputs["L0_c"].reshape(16, 128).T)
    g["baf"] = inputs["L0_attn_baf"].reshape(128, 1)
    g["bas"] = inputs["L0_attn_bas"].reshape(128, 1)
    g["lamf"] = inputs["L0_attn_fast_lambda"].reshape(128, 1)
    g["lams"] = inputs["L0_attn_slow_lambda"].reshape(128, 1)
    g["nm_b1"] = np.ascontiguousarray(inputs["L0_attn_nm_b1"].reshape(2, 128).T)
    g["nm_b2"] = inputs["L0_attn_nm_b2"].reshape(128, 1)
    g["ir_b1"] = inputs["L0_attn_ir_b1"].reshape(128, 1)
    g["ir_b2"] = inputs["L0_attn_ir_b2"].reshape(1, 1)
    g["pw_b"] = inputs["L0_attn_pw_b"].reshape(1, 3)
    g["norm_w"] = inputs["L0_attn_norm_w"].reshape(128, 1)
    g["gate_wT"] = np.ascontiguousarray(inputs["L0_ffn_gate_w"].T)
    g["W1T"] = np.ascontiguousarray(inputs["L0_ffn_W1"].transpose(0, 2, 1))
    g["W3T"] = np.ascontiguousarray(inputs["L0_ffn_W3"].transpose(0, 2, 1))
    g["W2T"] = np.ascontiguousarray(inputs["L0_ffn_W2"].transpose(0, 2, 1))
    g["Ws1T"] = np.ascontiguousarray(inputs["L0_ffn_Ws1"].T)
    g["Ws3T"] = np.ascontiguousarray(inputs["L0_ffn_Ws3"].T)
    g["Ws2T"] = np.ascontiguousarray(inputs["L0_ffn_Ws2"].T)
    g.update(host_constants())
    return g


class MK:
    def __init__(self, nc, n_tok, n_cores=8, do_scan=True, do_ffn=True, debug_taps=()):
        self.nc = nc
        self.T = n_tok
        self.NPT = self.T // PT
        self.NCK = self.T // C
        self.n_cores = n_cores
        self.do_scan = do_scan
        self.do_ffn = do_ffn
        self.P = Prog()
        self.sems = {}
        self.debug_taps = set(debug_taps)
        self.dbg = {}

    def sem(self, name):
        if name not in self.sems:
            self.sems[name] = self.nc.semaphore(name).__enter__()
        return self.sems[name]

    def tap(self, name, view, shape, after_sem, after_val):
        if name not in self.debug_taps:
            return
        t = self.nc.dram_tensor("dbg_" + name, list(shape), view.dtype, kind="ExternalOutput").ap()
        self.dbg[name] = t
        s = self.sem("dbgs")
        self.P.add("sync", lambda e, t=t, v=view: e.dma_start(t, v),
                   waits=[(after_sem, after_val)], incs=[(s, 16)])

    def declare_io(self):
        nc, P = self.nc, self.P
        g = {}
        g["xT"] = nc.dram_tensor("xT", [D, self.T], F32, kind="ExternalInput").ap()
        my_tok_ = self.T // self.n_cores
        for m_ in range(16):
            g[f"x4d_{m_}"] = nc.dram_tensor(f"x4d_{m_}",
                                            [max(my_tok_ // 512, 1), self.n_cores, 128, 512],
                                            F32, kind="ExternalInput").ap()
        for nm, shp in [("Wk", [D, DS]), ("Wv", [D, DS]), ("Wq", [D, DS]),
                        ("Waf", [D, DS]), ("Was", [D, DS]), ("ir_w1", [D, DS]),
                        ("pw_w", [D, 3]), ("nm_w1", [DS, 256]), ("nm_w2", [256, DS]),
                        ("ir_w2", [DS, 1]), ("Wo", [DS, D]), ("wmod", [D, 6 * D]),
                        ("ccolT", [128, 16]), ("baf", [128, 1]), ("bas", [128, 1]),
                        ("lamf", [128, 1]), ("lams", [128, 1]), ("nm_b1", [128, 2]),
                        ("nm_b2", [128, 1]), ("ir_b1", [128, 1]), ("ir_b2", [1, 1]),
                        ("pw_b", [1, 3]), ("norm_w", [128, 1]),
                        ("gate_wT", [D, 16]), ("W1T", [16, D, 1408]), ("W3T", [16, D, 1408]),
                        ("W2T", [16, 1408, D]), ("Ws1T", [D, 2816]), ("Ws3T", [D, 2816]),
                        ("Ws2T", [2816, D]),
                        ("ones_col", [128, 1]), ("twos_col", [128, 1]), ("ones_row", [1, PT]),
                        ("eye128", [128, 128]), ("resetmask", [128, PT]),
                        ("mSTRICT", [C, C]), ("mSTRICTN", [C, C]), ("mINCL", [C, C]),
                        ("eye64x2", [128, 128])]:
            g[nm] = nc.dram_tensor(nm, shp, F32, kind="ExternalInput").ap()
        NCK = self.NCK
        g["dPF"] = nc.dram_tensor("dPF", [NCK, 128, 192], F32).ap()
        g["dPS"] = nc.dram_tensor("dPS", [NCK, 128, 192], F32).ap()
        g["dV"] = nc.dram_tensor("dV", [NCK, 128, 64], F32).ap()
        g["dVs"] = nc.dram_tensor("dVs", [NCK, 128, 64], F32).ap()
        g["dMIT"] = nc.dram_tensor("dMIT", [NCK, 128, 128], BF16).ap()
        g["dGQT"] = nc.dram_tensor("dGQT", [NCK, 128, C], BF16).ap()
        g["dGOT"] = nc.dram_tensor("dGOT", [NCK, 128, C], BF16).ap()
        g["dUR"] = nc.dram_tensor("dUR", [NCK, 128, 128], BF16).ap()
        g["dNU"] = nc.dram_tensor("dNU", [NCK, 128, C], F32).ap()
        g["dROWS"] = nc.dram_tensor("dROWS", [NCK, 1, 512], F32).ap()
        g["dPC"] = nc.dram_tensor("dPC", [NCK, 128, 2], F32).ap()
        my_tok_d = self.T // self.n_cores
        nblk_d = max(my_tok_d // 512, 1)
        g["dOUTA"] = nc.dram_tensor("dOUTA", [nblk_d, self.n_cores, 128, 512], F32).ap()
        g["dW1b"] = nc.dram_tensor("dW1b", [16, 11, 128, 2048], BF16).ap()
        g["dW3b"] = nc.dram_tensor("dW3b", [16, 11, 128, 2048], BF16).ap()
        g["dW2b"] = nc.dram_tensor("dW2b", [16, 16, 128, 1408], BF16).ap()
        g["dWs1b"] = nc.dram_tensor("dWs1b", [22, 128, 2048], BF16).ap()
        g["dWs3b"] = nc.dram_tensor("dWs3b", [22, 128, 2048], BF16).ap()
        g["dWs2b"] = nc.dram_tensor("dWs2b", [2, 16, 128, 1408], BF16).ap()
        self.g = g
        return g

    def emit_A0(self, ex, ar, ex_psum):
        nc, P, g = self.nc, self.P, self.g
        sb = lambda name, shape, dt=F32: ex.enter_context(nc.sbuf_tensor(name, shape, dt))
        ps = lambda name, shape, dt=F32: ex_psum.enter_context(nc.psum_tensor(name, shape, dt))

        self.wpb = sb("wpb", [128, 6 * NCH * 128], BF16)
        self.wof = sb("wof", [128, D], BF16)
        self.pwc = sb("pwc", [128, 3 * NCH], BF16)
        self.nmw1 = sb("nmw1", [128, 256], BF16)
        self.nmw2 = sb("nmw2", [128, 256], BF16)
        self.irw2 = sb("irw2", [128, 1], BF16)
        self.cols = sb("cols", [128, 160])
        self.mcols = sb("mcols", [128, 96])
        self.rows0 = sb("rows0", [1, 2048])
        self.consts = sb("consts", [128, 8])
        self.onesrow = sb("onesrow", [1, PT])
        self.eyef = sb("eyef", [128, 128])
        self.eyeb = sb("eyeb", [128, 128], BF16)
        self.rmask = sb("rmask", [128, PT])
        self.mstrict = sb("mstrict", [C, C])
        self.mstrictn = sb("mstrictn", [C, C])
        self.mincl = sb("mincl", [C, C])
        self.onescolb = sb("onescolb", [128, 1], BF16)

        sd = self.sem("a0d")
        sc = self.sem("a0c")

        def ld(dst, src):
            P.add("sync", lambda e, d=dst, s=src: e.dma_start(d, s), incs=[(sd, 16)])

        ar.push()
        wtmp = ar.alloc(2048 + 64)[:, 0:2048]
        wtmp = ar.t[:, ar.marks[-1]:ar.marks[-1] + 2048]
        self.wtmp = wtmp
        ld(self.consts[:, 0:1], g["ones_col"])
        ld(self.consts[:, 1:2], g["twos_col"])
        ld(self.onesrow[:, :], g["ones_row"])
        ld(self.eyef[:, :], g["eye128"])
        ld(self.rmask[:, :], g["resetmask"])
        ld(self.mstrict[:, :], g["mSTRICT"])
        ld(self.mstrictn[:, :], g["mSTRICTN"])
        ld(self.mincl[:, :], g["mINCL"])
        ld(self.cols[:, 0:1], g["baf"]); ld(self.cols[:, 1:2], g["bas"])
        ld(self.cols[:, 2:3], g["lamf"]); ld(self.cols[:, 3:4], g["lams"])
        ld(self.cols[:, 6:8], g["nm_b1"]); ld(self.cols[:, 8:9], g["nm_b2"])
        ld(self.cols[:, 9:10], g["ir_b1"]); ld(self.cols[:, 10:11], g["norm_w"])
        ld(self.rows0[0:1, 0:1], g["ir_b2"])
        ld(self.rows0[0:1, 4:7], g["pw_b"])
        ld(self.cols[:, 12:28], g["ccolT"])
        w0 = P.cur(sd)

        P.add("vector", lambda e: e.memset(self.cols[:, 60:61], 1e-8), incs=[(sc, 1)])
        P.add("vector", lambda e: e.memset(self.cols[:, 61:62], EPS_LN), incs=[(sc, 1)])
        P.add("vector", lambda e: e.memset(self.cols[:, 62:63], EPS_RMS), incs=[(sc, 1)])
        P.add("vector", lambda e: e.memset(self.rows0[0:1, 30:31], EPS_LN), incs=[(sc, 1)])
        P.add("vector", lambda e: e.memset(self.rows0[0:1, 31:32], EPS_RMS), incs=[(sc, 1)])
        P.add("vector", lambda e: e.tensor_copy(self.eyeb[:, :], self.eyef[:, :]),
              waits=[(sd, 4 * 16)], incs=[(sc, 1)])
        P.add("vector", lambda e: e.tensor_copy(self.onescolb[:, :], self.consts[:, 0:1]),
              waits=[(sd, 16)], incs=[(sc, 1)])
        P.add("scalar", lambda e: e.activation(self.cols[:, 100:101], self.cols[:, 2:3], AF.Sigmoid),
              waits=[(sd, w0)], incs=[(sc, 1)])
        P.add("scalar", lambda e: e.activation(self.cols[:, 102:103], self.cols[:, 100:101], AF.Ln, bias=self.cols[:, 60:61]),
              incs=[(sc, 1)])
        P.add("vector", lambda e: e.tensor_scalar(self.cols[:, 4:5], self.cols[:, 102:103], 8.0, None, ALU.mult),
              waits=[(sc, P.cur(sc))], incs=[(sc, 1)])
        P.add("vector", lambda e: e.tensor_scalar(self.cols[:, 105:106], self.cols[:, 3:4], -1.0, None, ALU.mult),
              incs=[(sc, 1)])
        P.add("scalar", lambda e: e.activation(self.cols[:, 103:104], self.cols[:, 105:106], AF.Exp),
              waits=[(sc, P.cur(sc))], incs=[(sc, 1)])
        P.add("vector", lambda e: e.tensor_scalar(self.cols[:, 104:105], self.cols[:, 103:104], -0.5, 1.0,
                                                  ALU.mult, ALU.add),
              waits=[(sc, P.cur(sc))], incs=[(sc, 1)])
        P.add("vector", lambda e: e.tensor_tensor(self.cols[:, 104:105], self.cols[:, 103:104],
                                                  self.cols[:, 104:105], ALU.mult),
              waits=[(sc, P.cur(sc))], incs=[(sc, 1)])
        P.add("vector", lambda e: e.tensor_scalar(self.cols[:, 5:6], self.cols[:, 104:105], -8.0, 8e-8,
                                                  ALU.mult, ALU.add),
              waits=[(sc, P.cur(sc))], incs=[(sc, 1)])
        P.add("vector", lambda e: e.tensor_scalar(self.cols[:, 106:107], self.cols[:, 8:9], 0.1, None, ALU.mult),
              incs=[(sc, 1)])
        P.add("scalar", lambda e: e.activation(self.cols[:, 28:44], self.cols[:, 12:28], AF.Silu),
              incs=[(sc, 1)])
        sil_done = P.cur(sc)

        pmod = [ps("pmod0", [128, 96]), ps("pmod1", [128, 96])]
        smod = self.sem("a0m")
        wslab = [ar.alloc(6 * D), ar.alloc(6 * D)]
        P.add("vector", lambda e: e.memset(self.mcols[:, :], 0.0), incs=[(smod, 1)])
        add_done = {}
        for dc in range(NCH):
            par = dc % 2
            P.add("sync", lambda e, dc=dc, par=par: e.dma_start(
                wslab[par][:, :], g["wmod"][dc * 128:(dc + 1) * 128, :]),
                waits=[(smod, max(P.cur(smod) - 2, 0))], incs=[(sd, 16)])
            wslab_ready = P.cur(sd)
            for jj in range(96):
                P.add("tensor",
                      lambda e, par=par, dc=dc, jj=jj: e.matmul(
                          pmod[par][:, jj:jj + 1], wslab[par][:, jj * 128:(jj + 1) * 128],
                          self.cols[:, 28 + dc:29 + dc], start=True, stop=True),
                      waits=([(sd, wslab_ready), (sc, sil_done),
                              (smod, add_done.get(dc - 2, 0))] if jj == 0 else []),
                      incs=([(smod, 1)] if jj == 95 else []))
            P.add("vector", lambda e, par=par: e.tensor_tensor(
                self.mcols[:, :], self.mcols[:, :], pmod[par][:, :], ALU.add),
                  waits=[(smod, P.cur(smod))], incs=[(smod, 1)])
            add_done[dc] = P.cur(smod)
        mod_done = P.cur(smod)
        for base in (32, 80):
            P.add("scalar", lambda e, b=base: e.activation(self.cols[:, 44:60], self.mcols[:, b:b + 16], AF.Sign),
                  waits=[(sc, P.cur(sc)), (smod, mod_done)], incs=[(sc, 1)])
            P.add("scalar", lambda e, b=base: e.activation(self.mcols[:, b:b + 16], self.mcols[:, b:b + 16],
                                                           AF.Abs),
                  waits=[(sc, P.cur(sc))], incs=[(sc, 1)])
            P.add("vector", lambda e, b=base: e.tensor_scalar(self.mcols[:, b:b + 16], self.mcols[:, b:b + 16],
                                                              0.25, None, ALU.max),
                  waits=[(sc, P.cur(sc))], incs=[(sc, 1)])
            P.add("vector", lambda e, b=base: e.tensor_tensor(self.mcols[:, b:b + 16], self.mcols[:, b:b + 16],
                                                              self.cols[:, 44:60], ALU.mult),
                  waits=[(sc, P.cur(sc))], incs=[(sc, 1)])
        P.add("vector", lambda e: e.tensor_scalar(self.mcols[:, 16:32], self.mcols[:, 16:32], 1.0, None, ALU.add),
              incs=[(sc, 1)])
        P.add("vector", lambda e: e.tensor_scalar(self.mcols[:, 64:80], self.mcols[:, 64:80], 1.0, None, ALU.add),
              incs=[(sc, 1)])
        a0_mod_done = P.cur(sc)

        pu = ps("pu", [128, 16])
        su = self.sem("a0u")
        wnames = ["Wk", "Wv", "Wq", "Waf", "Was", "ir_w1"]
        for wi, wn in enumerate(wnames):
            P.add("sync", lambda e, wn=wn: e.dma_start(self.wtmp.rearrange("p (c o) -> p c o", c=NCH), g[wn].rearrange("(c p) o -> p c o", p=128)),
                  waits=[(su, P.cur(su))], incs=[(sd, 16)])
            wrdy = P.cur(sd)
            for ch in range(NCH):
                P.add("tensor", lambda e, wi=wi, ch=ch: e.matmul(
                    pu[:, wi + 8:wi + 9], self.wtmp[:, ch * 128:(ch + 1) * 128],
                    self.mcols[:, ch:ch + 1], start=(ch == 0), stop=(ch == NCH - 1)),
                    waits=([(sd, wrdy), (sc, a0_mod_done)] if ch == 0 else []),
                    incs=([(su, 1)] if ch == NCH - 1 else []))
            for ch in range(NCH):
                P.add("vector", lambda e, wi=wi, ch=ch: e.tensor_scalar(
                    self.wpb[:, (wi * NCH + ch) * 128:(wi * NCH + ch + 1) * 128],
                    self.wtmp[:, ch * 128:(ch + 1) * 128],
                    self.mcols[:, 16 + ch:17 + ch], None, ALU.mult),
                    waits=([(su, P.cur(su))] if ch == 0 else []),
                    incs=([(su, 1)] if ch == NCH - 1 else []))
            wfold = P.cur(su)
            for ch in range(NCH):
                P.add("tensor", lambda e, wi=wi, ch=ch: e.matmul(
                    pu[:, wi:wi + 1],
                    self.wpb[:, (wi * NCH + ch) * 128:(wi * NCH + ch + 1) * 128],
                    self.onescolb[:, :], start=(ch == 0), stop=(ch == NCH - 1)),
                    waits=([(su, wfold)] if ch == 0 else []),
                    incs=([(su, 1)] if ch == NCH - 1 else []))
        pwtmp = sb("pwtmp", [128, 3 * NCH])
        P.add("sync", lambda e: e.dma_start(pwtmp.rearrange("p (c o) -> p c o", c=NCH), g["pw_w"].rearrange("(c p) o -> p c o", p=128)),
              incs=[(sd, 16)])
        pwrdy = P.cur(sd)
        for ch in range(NCH):
            P.add("tensor", lambda e, ch=ch: e.matmul(
                pu[0:3, 14:15],
                pwtmp[:, ch * 3:(ch + 1) * 3], self.mcols[:, ch:ch + 1],
                start=(ch == 0), stop=(ch == NCH - 1)),
                waits=([(sd, pwrdy), (sc, a0_mod_done)] if ch == 0 else []),
                incs=([(su, 1)] if ch == NCH - 1 else []))
        for ch in range(NCH):
            P.add("vector", lambda e, ch=ch: e.tensor_scalar(
                self.pwc[:, ch * 3:(ch + 1) * 3], pwtmp[:, ch * 3:(ch + 1) * 3],
                self.mcols[:, 16 + ch:17 + ch], None, ALU.mult),
                incs=([(su, 1)] if ch == NCH - 1 else []))
        pwfold = P.cur(su)
        for ch in range(NCH):
            P.add("tensor", lambda e, ch=ch: e.matmul(
                pu[0:3, 15:16], self.pwc[:, ch * 3:(ch + 1) * 3], self.onescolb[:, :],
                start=(ch == 0), stop=(ch == NCH - 1)),
                waits=([(su, pwfold)] if ch == 0 else []),
                incs=([(su, 1)] if ch == NCH - 1 else []))
        allu = P.cur(su)

        ptr = ps("ptr", [1, 512])
        str_ = self.sem("a0t")
        P.add("vector", lambda e: e.tensor_copy(self.cols[:, 64:80], pu[:, :]),
              waits=[(su, allu)], incs=[(str_, 1)])
        pu_sb = self.cols[:, 64:80]
        for slot, srcc, dst0, w in [(0, 0, 128, 128), (1, 1, 256, 128), (2, 2, 384, 128),
                                    (3, 3, 512, 128), (4, 4, 640, 128), (5, 5, 768, 128),
                                    (6, 8, 896, 128), (7, 9, 1024, 128), (8, 10, 1152, 128),
                                    (9, 11, 1280, 128), (10, 12, 1408, 128), (11, 13, 1536, 128),
                                    (12, 15, 16, 3), (13, 14, 20, 3)]:
            P.add("tensor", lambda e, srcc=srcc: e.transpose(ptr[0:1, 0:128], pu_sb[:, srcc:srcc + 1],
                                                             self.eyef[:, :]),
                  waits=[(str_, P.cur(str_))], incs=[(str_, 1)])
            P.add("vector", lambda e, dst0=dst0, w=w: e.tensor_copy(self.rows0[0:1, dst0:dst0 + w],
                                                                    ptr[0:1, 0:w]),
                  waits=[(str_, P.cur(str_))], incs=[(str_, 1)])

        P.add("sync", lambda e: e.dma_start(self.wtmp[:, 0:256], g["nm_w1"]),
              waits=[(su, P.cur(su)), (str_, P.cur(str_))], incs=[(sd, 16)])
        P.add("sync", lambda e: e.dma_start(self.wtmp[:, 256:512].rearrange("p (c o) -> p c o", c=2),
                                            g["nm_w2"].rearrange("(c p) o -> p c o", p=128)), incs=[(sd, 16)])
        P.add("sync", lambda e: e.dma_start(self.wtmp[:, 512:513], g["ir_w2"]), incs=[(sd, 16)])
        nmr = P.cur(sd)
        P.add("vector", lambda e: e.tensor_copy(self.nmw1[:, :], self.wtmp[:, 0:256]),
              waits=[(sd, nmr)], incs=[(sc, 1)])
        P.add("vector", lambda e: e.tensor_copy(self.nmw2[:, :], self.wtmp[:, 256:512]), incs=[(sc, 1)])
        P.add("vector", lambda e: e.tensor_copy(self.irw2[:, :], self.wtmp[:, 512:513]), incs=[(sc, 1)])
        P.add("sync", lambda e: e.dma_start(self.wtmp[:, 0:2048], g["Wo"]),
              waits=[(sc, P.cur(sc))], incs=[(sd, 16)])
        wordy = P.cur(sd)
        P.add("vector", lambda e: e.tensor_scalar(self.wof[:, :], self.wtmp[:, 0:2048],
                                                  self.cols[:, 10:11], None, ALU.mult),
              waits=[(sd, wordy)], incs=[(sc, 1)])
        self.a0_waits = [(sc, P.cur(sc)), (su, P.cur(su)), (str_, P.cur(str_)),
                         (sd, P.cur(sd)), (smod, P.cur(smod))]

    def esem(self, name):
        return self.sem("E_" + name)

    def op(self, eng, fn, deps=(), extra_waits=(), inc=None):
        P = self.P
        waits = [(self.esem(d), P.cur(self.esem(d))) for d in deps] + list(extra_waits)
        amt = inc if inc is not None else (16 if eng == "sync" else 1)
        P.add(eng, fn, waits=waits, incs=[(self.esem(eng), amt)])

    def barrier_waits(self):
        return [(self.esem(d), self.P.cur(self.esem(d)))
                for d in ("tensor", "vector", "scalar", "gpsimd", "sync")]

    def emit_A1(self, ex, ar, arb):
        nc, P, g = self.nc, self.P, self.g
        ps = lambda name, shape, dt=F32: ex.enter_context(nc.psum_tensor(name, shape, dt))
        op = self.op

        xf = ar.alloc(NCH * PT)
        k_n = ar.alloc(PT); q_n = ar.alloc(PT); vv = ar.alloc(PT)
        e_rf = ar.alloc(PT); e_rs = ar.alloc(PT); e_ir = ar.alloc(PT)
        rbc = ar.alloc(PT)
        nu = ar.alloc(PT)
        TP = [ar.alloc(PT) for _ in range(9)]
        S = {}
        for st in ("f", "s"):
            S[st] = dict(cp=ar.alloc(PT), U=ar.alloc(PT), Pp=ar.alloc(PT),
                         la=TP[0], LA=TP[1], LX=TP[2], E=TP[3], En=TP[4], Es=TP[5],
                         af=TP[6], c=TP[7], Em=TP[8])
        AQK = {"f": ar.alloc(3 * PT), "s": ar.alloc(3 * PT)}
        gevacB16 = arb.alloc(768)
        uevac = ar.alloc(256)
        uevacB = ar.alloc(256)
        rowsA = ar.alloc(2048)
        R_m, R_sd, R_rstd, R_negm, R_pw0, R_pw1, R_pw2, R_imp, R_g, R_a2f, R_a2s, R_sc0, R_sc1 = (
            rowsA[0:1, i * 128:(i + 1) * 128][:, 0:PT] if False else rowsA[0:1, i * PT // 4:(i + 1) * PT // 4]
            for i in range(13))
        R_m = rowsA[0:1, 0:PT]; R_sd = rowsA[0:1, PT:2 * PT]
        R_rstd = rowsA[0:1, 2 * PT:3 * PT]; R_negm = rowsA[0:1, 3 * PT:4 * PT]
        rowsB = ar.alloc(2048)
        R_pw0 = rowsB[0:1, 0:PT]; R_pw1 = rowsB[0:1, PT:2 * PT]; R_pw2 = rowsB[0:1, 2 * PT:3 * PT]
        R_imp = rowsB[0:1, 3 * PT:4 * PT]
        rowsC = ar.alloc(1024)
        R_sumk = rowsC[0:1, 0:PT]; R_sumq = rowsC[0:1, PT:2 * PT]
        rowsD = ar.alloc(512)
        R_a2f = rowsD[0:1, 0:C]; R_a2s = rowsD[0:1, C:2 * C]
        R_a2fB = rowsD[0:1, 2 * C:3 * C]; R_a2sB = rowsD[0:1, 3 * C:4 * C]
        self.sync_after = {}

        xb = arb.alloc(NCH * PT)
        scrb = arb.alloc(2 * PT)
        kb = arb.alloc(PT); qb = arb.alloc(PT); vb = arb.alloc(PT); girb = arb.alloc(PT)
        LL = arb.alloc(128); Llo = arb.alloc(128)
        POW = arb.alloc(512); Pq = arb.alloc(512)
        Pu = arb.alloc(128); Plo = arb.alloc(128); Fb = arb.alloc(128)
        GQTstack = arb.alloc(2 * C); GOTstack = arb.alloc(2 * C)
        URstack = arb.alloc(256)
        Ubf = arb.alloc(PT); Ubs = arb.alloc(PT)
        S_ub = {"f": Ubf, "s": Ubs}

        pA = ps("pA", [128, PT])
        pB = ps("pB", [128, PT])
        pRa = ps("pRa", [1, PT])
        pRb = ps("pRb", [1, PT])
        pC = ps("pC", [128, PT])
        pG = ps("pG", [C, 192])
        pTB = ex.enter_context(nc.psum_tensor("pTB", [128, 128], BF16))
        pW = ps("pW", [1, PT])

        NPT_ = self.NPT
        wb = self.wpb

        first_tile_waits = self.barrier_waits() + self.a0_waits

        for j in range(NPT_):
            tw = first_tile_waits if j == 0 else None
            for ch in range(NCH):
                P.add("sync", lambda e, ch=ch, j=j: e.dma_start(
                    xf[:, ch * PT:(ch + 1) * PT], g["xT"][ch * 128:(ch + 1) * 128, j * PT:(j + 1) * PT]),
                    waits=((tw if tw is not None else self.barrier_waits()) if ch == 0 else []),
                    incs=[(self.esem("sync"), 16)])
            for ch in range(NCH):
                op("vector", lambda e, ch=ch: e.tensor_copy(xb[:, ch * PT:(ch + 1) * PT],
                                                            xf[:, ch * PT:(ch + 1) * PT]),
                   deps=("sync",) if ch == 0 else ())
            cast_done = P.cur(self.esem("vector"))
            for ch in range(NCH):
                op("tensor", lambda e, ch=ch: e.matmul(
                    pRa[0:1, :], self.onescolb[:, :], xb[:, ch * PT:(ch + 1) * PT],
                    start=(ch == 0), stop=(ch == NCH - 1)), deps=("vector",) if ch == 0 else ())
            for ch in range(NCH):
                op("gpsimd", lambda e, ch=ch: e.tensor_tensor(
                    scrb[:, (ch % 2) * PT:(ch % 2 + 1) * PT], xb[:, ch * PT:(ch + 1) * PT],
                    xb[:, ch * PT:(ch + 1) * PT], ALU.mult),
                   deps=("vector", "tensor") if ch == 0 else (("tensor",) if ch >= 2 else ()))
                op("tensor", lambda e, ch=ch: e.matmul(
                    pRb[0:1, :], self.onescolb[:, :], scrb[:, (ch % 2) * PT:(ch % 2 + 1) * PT],
                    start=(ch == 0), stop=(ch == NCH - 1)), deps=("gpsimd",))
            op("vector", lambda e: e.tensor_scalar(R_m, pRa[0:1, :], 1.0 / D, None, ALU.mult), deps=("tensor",))
            op("vector", lambda e: e.tensor_scalar(R_sd, pRb[0:1, :], 1.0 / D, None, ALU.mult))
            op("vector", lambda e: e.tensor_tensor(R_negm, R_m, R_m, ALU.mult), deps=("vector",))
            op("vector", lambda e: e.tensor_tensor(R_sd, R_sd, R_negm, ALU.subtract), deps=("vector",))
            op("scalar", lambda e: e.activation(R_sd, R_sd, AF.Sqrt, bias=self.rows0[0:1, 30:31]), deps=("vector",))
            op("vector", lambda e: e.reciprocal(R_rstd, R_sd), deps=("scalar",))
            op("vector", lambda e: e.tensor_scalar(R_negm, R_m, -1.0, None, ALU.mult))
            op("tensor", lambda e: e.matmul(pC[:, :], self.onesrow[0:1, 0:128], R_rstd,
                                            start=True, stop=True), deps=("vector",))
            op("vector", lambda e: e.tensor_copy(rbc[:, :], pC[:, :]), deps=("tensor",))
            rbc_done = P.cur(self.esem("vector"))

            projs = [("k", 0, None), ("v", 1, vv), ("q", 2, None), ("rf", 3, e_rf),
                     ("rs", 4, e_rs), ("ir1", 5, e_ir)]
            for pi, (pname, wi, dst) in enumerate(projs):
                pp = pA if pi % 2 == 0 else pB
                for ch in range(NCH):
                    op("tensor", lambda e, wi=wi, ch=ch, pp=pp: e.matmul(
                        pp[:, :], wb[:, (wi * NCH + ch) * 128:(wi * NCH + ch + 1) * 128],
                        xb[:, ch * PT:(ch + 1) * PT], start=(ch == 0), stop=False),
                       deps=("vector",) if (pi < 2 and ch == 0) else ())
                op("tensor", lambda e, wi=wi, pp=pp: e.matmul(
                    pp[:, :], self.rows0[0:1, 128 * (1 + wi):128 * (2 + wi)], R_negm,
                    start=False, stop=False))
                op("tensor", lambda e, wi=wi, pp=pp: e.matmul(
                    pp[:, :], self.rows0[0:1, 128 * (7 + wi):128 * (8 + wi)], R_sd,
                    start=False, stop=True))
                tgt = {"k": k_n, "q": q_n}.get(pname, dst)
                op("vector", lambda e, pp=pp, tgt=tgt: e.tensor_tensor(tgt, pp[:, :], rbc, ALU.mult),
                   deps=("tensor",))
            for m_ in range(3):
                for ch in range(NCH):
                    op("tensor", lambda e, m_=m_, ch=ch: e.matmul(
                        pW[0:1, :], self.pwc[:, ch * 3 + m_:ch * 3 + m_ + 1], xb[:, ch * PT:(ch + 1) * PT],
                        start=(ch == 0), stop=False), deps=("vector",) if ch == 0 else ())
                op("tensor", lambda e, m_=m_: e.matmul(
                    pW[0:1, :], self.rows0[0:1, 16 + m_:17 + m_], R_negm, start=False, stop=False))
                op("tensor", lambda e, m_=m_: e.matmul(
                    pW[0:1, :], self.rows0[0:1, 20 + m_:21 + m_], R_sd, start=False, stop=True))
                dstr = (R_pw0, R_pw1, R_pw2)[m_]
                op("vector", lambda e, dstr=dstr: e.tensor_tensor(dstr, pW[0:1, :], R_rstd, ALU.mult),
                   deps=("tensor",))
                op("scalar", lambda e, dstr=dstr, m_=m_: e.activation(
                    dstr, dstr, AF.Exp, bias=self.rows0[0:1, 4 + m_:5 + m_]), deps=("vector",))
            op("vector", lambda e: e.tensor_tensor(R_sumk, R_pw0, R_pw1, ALU.add), deps=("scalar",))
            op("vector", lambda e: e.tensor_tensor(R_sumk, R_sumk, R_pw2, ALU.add), deps=("vector",))
            op("vector", lambda e: e.reciprocal(R_sumq, R_sumk), deps=("vector",))
            first = True
            for dstr in (R_pw0, R_pw1, R_pw2):
                op("vector", lambda e, dstr=dstr: e.tensor_tensor(dstr, dstr, R_sumq, ALU.mult),
                   deps=("vector",) if first else ())
                first = False
            pw_done = P.cur(self.esem("vector"))

            for li, (src, srow) in enumerate(((k_n, R_sumk), (q_n, R_sumq))):
                pRx = pRa if li == 0 else pRb
                op("gpsimd", lambda e, src=src: e.tensor_tensor(xf[:, 0:PT], src, src, ALU.mult),
                   deps=("vector",))
                op("tensor", lambda e, pRx=pRx: e.matmul(pRx[0:1, :], self.consts[:, 0:1], xf[:, 0:PT],
                                                         start=True, stop=True), deps=("gpsimd", "vector"))
                op("scalar", lambda e, srow=srow, pRx=pRx: e.activation(srow, pRx[0:1, :], AF.Sqrt),
                   deps=("tensor",))
                op("vector", lambda e, srow=srow: e.tensor_scalar(srow, srow, 1e-12, None, ALU.max),
                   deps=("scalar",))
                op("vector", lambda e, srow=srow: e.reciprocal(srow, srow), deps=("vector",))
                op("tensor", lambda e, srow=srow: e.matmul(pC[:, :], self.onesrow[0:1, 0:128], srow,
                                                           start=True, stop=True), deps=("vector",))
                op("vector", lambda e, src=src: e.tensor_tensor(src, src, pC[:, :], ALU.mult),
                   deps=("tensor",))
            kq_done = P.cur(self.esem("vector"))
            op("vector", lambda e: e.tensor_copy(vb[:, :], vv))
            op("vector", lambda e: e.tensor_copy(kb[:, :], k_n))
            op("vector", lambda e: e.tensor_copy(qb[:, :], q_n))

            for h_ in range(2):
                op("tensor", lambda e, h_=h_: e.matmul(pA[:, :], self.nmw1[:, h_ * 128:(h_ + 1) * 128],
                                                       vb[:, :], start=True, stop=True),
                   deps=("vector",) if h_ == 0 else ("scalar",))
                op("scalar", lambda e, h_=h_: e.activation(scrb[:, h_ * PT:(h_ + 1) * PT],
                                                           pA[:, :], AF.Gelu,
                                                           bias=self.cols[:, 6 + h_:7 + h_]), deps=("tensor",))
            for h_ in range(2):
                op("tensor", lambda e, h_=h_: e.matmul(pB[:, :], self.nmw2[:, h_ * 128:(h_ + 1) * 128],
                                                       scrb[:, h_ * PT:(h_ + 1) * PT],
                                                       start=(h_ == 0), stop=(h_ == 1)),
                   deps=("scalar", "vector") if h_ == 0 else ())
            op("scalar", lambda e: e.activation(nu[:, 0:PT] if False else nu, pB[:, :], AF.Identity,
                                                bias=self.cols[:, 106:107], scale=0.1), deps=("tensor",))

            op("scalar", lambda e: e.activation(girb[:, :], e_ir, AF.Gelu, bias=self.cols[:, 9:10]),
               deps=("vector",))
            op("tensor", lambda e: e.matmul(pW[0:1, :], self.irw2[:, :], girb[:, :], start=True, stop=True),
               deps=("scalar", "vector"))
            op("scalar", lambda e: e.activation(R_imp, pW[0:1, :], AF.Sigmoid,
                                                bias=self.rows0[0:1, 0:1]), deps=("tensor",))
            op("vector", lambda e: e.tensor_scalar(R_imp, R_imp, 0.5, None, ALU.is_gt), deps=("scalar",))
            op("tensor", lambda e: e.matmul(pC[:, :], self.onesrow[0:1, 0:128], R_imp,
                                            start=True, stop=True), deps=("vector",))
            op("vector", lambda e: e.tensor_copy(xf[:, PT:2 * PT], pC[:, :]), deps=("tensor",))

            for st, e_r, bcol, lacol in (("f", e_rf, 0, 4), ("s", e_rs, 1, 5)):
                ss = S[st]
                op("scalar", lambda e, e_r=e_r, bcol=bcol, ss=ss: e.activation(
                    ss["af"], e_r, AF.Sigmoid, bias=self.cols[:, bcol:bcol + 1]), deps=("vector",))
                op("vector", lambda e, ss=ss, lacol=lacol: e.tensor_scalar(
                    ss["la"], ss["af"], self.cols[:, lacol:lacol + 1], None, ALU.mult), deps=("scalar",))
                op("vector", lambda e, ss=ss: e.tensor_tensor_scan(
                    ss["LA"], self.rmask[:, :], ss["la"], 0.0, ALU.mult, ALU.add), deps=("vector",))
                op("vector", lambda e, ss=ss: e.tensor_tensor(ss["LX"], ss["LA"], ss["la"], ALU.subtract),
                   deps=("vector",))
                op("scalar", lambda e, ss=ss: e.activation(ss["E"], ss["LX"], AF.Exp), deps=("vector",))
                op("scalar", lambda e, ss=ss: e.activation(ss["Pp"], ss["LA"], AF.Exp))
                op("scalar", lambda e, ss=ss: e.activation(ss["af"], ss["la"], AF.Exp))
                op("scalar", lambda e, ss=ss: e.activation(ss["Em"], ss["LA"], AF.Exp, scale=-1.0))
                op("vector", lambda e, ss=ss: e.tensor_scalar(ss["En"], ss["E"], -1.0, None, ALU.mult),
                   deps=("scalar",))
                op("vector", lambda e, ss=ss: e.tensor_scalar(ss["Es"], ss["E"], -0.5 * SQK, None, ALU.mult))
                op("vector", lambda e, ss=ss: e.tensor_scalar(ss["c"], ss["af"], -1.0, 1.0, ALU.mult, ALU.add))
                op("vector", lambda e, ss=ss: e.tensor_tensor(ss["c"], ss["c"], k_n, ALU.mult), deps=("vector",))
                if st == "s":
                    op("vector", lambda e, ss=ss: e.tensor_tensor(ss["c"], ss["c"], xf[:, PT:2 * PT], ALU.mult),
                       deps=("vector",))
                op("vector", lambda e, ss=ss: e.tensor_tensor(ss["cp"], ss["c"], ss["Em"], ALU.mult),
                   deps=("vector",))
                lac = ss["LA"].rearrange("p (ck t) -> p ck t", t=C)[:, :, C - 1:C]
                lacb = lac.broadcast_to([128, PT // C, C])
                op("vector", lambda e, ss=ss, lacb=lacb: e.tensor_tensor(
                    ss["LX"].rearrange("p (ck t) -> p ck t", t=C), lacb,
                    ss["LA"].rearrange("p (ck t) -> p ck t", t=C), ALU.subtract))
                op("scalar", lambda e, ss=ss: e.activation(ss["LX"], ss["LX"], AF.Exp), deps=("vector",))
                op("vector", lambda e, ss=ss, st=st: e.tensor_tensor(S_ub[st][:, :], ss["c"], ss["LX"],
                                                                     ALU.mult),
                   deps=("scalar", "vector"))
                A = AQK[st]
                o_kn = 0 if st == "f" else PT
                o_qp = PT if st == "f" else 2 * PT
                o_qs = 2 * PT if st == "f" else 0
                op("vector", lambda e, ss=ss, A=A, o=o_qs: e.tensor_tensor(A[:, o:o + PT], q_n, ss["Es"], ALU.mult),
                   deps=("vector",))
                op("vector", lambda e, ss=ss, A=A, o=o_kn: e.tensor_tensor(A[:, o:o + PT], k_n, ss["En"], ALU.mult))
                op("vector", lambda e, ss=ss, A=A, o=o_qp: e.tensor_tensor(A[:, o:o + PT], q_n, ss["Pp"], ALU.mult))
            prep_done = P.cur(self.esem("vector"))

            if j == 0:
                op("vector", lambda e: e.memset(LL[:, :], 0.0), deps=("vector",))
            for ck in range(PT // C):
                gck = j * (PT // C) + ck
                par = gck % 2
                scr_wait = [(self.esem("sync"), self.sync_after.get(gck - 2, 0))]
                cpf_ck = S["f"]["cp"][:, ck * C:(ck + 1) * C]
                cps_ck = S["s"]["cp"][:, ck * C:(ck + 1) * C]
                rhsf = AQK["f"].rearrange("p (s t) -> p s t", s=3)[:, :, ck * C:(ck + 1) * C]
                rhss = AQK["s"].rearrange("p (s t) -> p s t", s=3)[:, :, ck * C:(ck + 1) * C]
                GQTst = GQTstack[:, par * C:(par + 1) * C]
                GOTst = GOTstack[:, par * C:(par + 1) * C]
                URst = URstack[:, par * 128:(par + 1) * 128]
                op("tensor", lambda e, cpf_ck=cpf_ck, rhsf=rhsf: e.matmul(pG[:, :], cpf_ck, rhsf,
                                                                          start=True, stop=True),
                   deps=("vector",))
                op("vector", lambda e: e.tensor_tensor(LL[0:C, 0:C], pG[:, 0:C], self.mstrictn[:, :],
                                                       ALU.mult), deps=("tensor",), extra_waits=scr_wait)
                op("vector", lambda e, GOTst=GOTst: e.tensor_tensor(GOTst[0:C, :], pG[:, C:2 * C],
                                                                    self.mincl[:, :], ALU.mult))
                op("vector", lambda e, GQTst=GQTst: e.tensor_tensor(GQTst[0:C, :], pG[:, 2 * C:3 * C],
                                                                    self.mstrict[:, :], ALU.mult))
                op("tensor", lambda e, cps_ck=cps_ck, rhss=rhss: e.matmul(pG[:, :], cps_ck, rhss,
                                                                          start=True, stop=True),
                   deps=("vector",))
                op("vector", lambda e: e.tensor_tensor(LL[C:2 * C, C:2 * C], pG[:, 0:C],
                                                       self.mstrictn[:, :], ALU.mult), deps=("tensor",))
                op("vector", lambda e, GOTst=GOTst: e.tensor_tensor(GOTst[C:2 * C, :], pG[:, C:2 * C],
                                                                    self.mincl[:, :], ALU.mult))
                op("vector", lambda e, GQTst=GQTst: e.tensor_tensor(GQTst[C:2 * C, :], pG[:, 2 * C:3 * C],
                                                                    self.mstrict[:, :], ALU.mult))
                P.add("sync", lambda e, gck=gck, GQTst=GQTst: e.dma_start(g["dGQT"][gck], GQTst),
                      waits=[(self.esem("vector"), P.cur(self.esem("vector")))],
                      incs=[(self.esem("sync"), 16)])
                P.add("sync", lambda e, gck=gck, GOTst=GOTst: e.dma_start(g["dGOT"][gck], GOTst),
                      incs=[(self.esem("sync"), 16)])
                for st, half in (("f", 0), ("s", 1)):
                    useg = S_ub[st][:, ck * C:(ck + 1) * C]
                    op("tensor", lambda e, useg=useg: e.transpose(pTB[0:C, :], useg, self.eyeb[:, :]),
                       deps=("vector",))
                    op("vector", lambda e, URst=URst, half=half: e.tensor_copy(
                        URst[half * C:(half + 1) * C, :], pTB[0:C, :]), deps=("tensor",))
                P.add("sync", lambda e, gck=gck, URst=URst: e.dma_start(g["dUR"][gck], URst),
                      waits=[(self.esem("vector"), P.cur(self.esem("vector")))],
                      incs=[(self.esem("sync"), 16)])
                op("tensor", lambda e: e.transpose(pTB[:, :], LL[:, :], self.eyeb[:, :]),
                   deps=("vector",))
                op("vector", lambda e: e.tensor_copy(Llo[:, :], pTB[:, :]), deps=("tensor",))
                op("vector", lambda e: e.scalar_tensor_tensor(Pu[:, :], LL[:, :], -1.0, self.eyeb[:, :],
                                                              ALU.mult, ALU.add), deps=("vector",))
                op("vector", lambda e: e.scalar_tensor_tensor(Plo[:, :], Llo[:, :], -1.0, self.eyeb[:, :],
                                                              ALU.mult, ALU.add))
                pu, plo = LL, Llo
                cu, clo = Pu, Plo
                for gi, k in enumerate((2, 4, 8, 16, 32)):
                    last = (k == 32)
                    npu = POW[:, (gi % 2) * 256:(gi % 2) * 256 + 128]
                    nplo = POW[:, (gi % 2) * 256 + 128:(gi % 2) * 256 + 256]
                    op("tensor", lambda e, pu=pu, plo=plo: e.matmul(pA[:, 0:128], plo, pu,
                                                                    start=True, stop=True),
                       deps=("vector",))
                    op("tensor", lambda e, pu=pu, plo=plo: e.matmul(pA[:, 128:256], pu, plo,
                                                                    start=True, stop=True))
                    op("vector", lambda e, npu=npu: e.tensor_copy(npu, pA[:, 0:128]),
                       deps=("tensor",))
                    op("vector", lambda e, nplo=nplo: e.tensor_copy(nplo, pA[:, 128:256]))
                    op("vector", lambda e, npu=npu: e.scalar_tensor_tensor(
                        Fb[:, :], npu, 1.0, self.eyeb[:, :], ALU.mult, ALU.add), deps=("vector",))
                    op("tensor", lambda e, clo=clo: e.matmul(pA[:, 256:384], clo, Fb[:, :],
                                                             start=True, stop=True), deps=("vector",))
                    if not last:
                        op("tensor", lambda e, clo=clo: e.matmul(pA[:, 384:512], Fb[:, :], clo,
                                                                 start=True, stop=True))
                    ncu = Pq[:, (gi % 2) * 256:(gi % 2) * 256 + 128]
                    nclo = Pq[:, (gi % 2) * 256 + 128:(gi % 2) * 256 + 256]
                    op("vector", lambda e, ncu=ncu: e.tensor_copy(ncu, pA[:, 256:384]), deps=("tensor",))
                    if not last:
                        op("vector", lambda e, nclo=nclo: e.tensor_copy(nclo, pA[:, 384:512]))
                    pu, plo = npu, nplo
                    cu, clo = ncu, nclo
                P.add("sync", lambda e, gck=gck, cu=cu: e.dma_start(g["dMIT"][gck], cu),
                      waits=[(self.esem("vector"), P.cur(self.esem("vector")))],
                      incs=[(self.esem("sync"), 16)])
                vready = [(self.esem("vector"), prep_done)]
                P.add("sync", lambda e, gck=gck, ck=ck: e.dma_start(
                    g["dPF"][gck][:, 0:64], AQK["f"][:, ck * C:(ck + 1) * C]),
                    waits=vready, incs=[(self.esem("sync"), 16)])
                P.add("sync", lambda e, gck=gck, ck=ck: e.dma_start(
                    g["dPF"][gck][:, 64:128], AQK["f"][:, PT + ck * C:PT + (ck + 1) * C]),
                    incs=[(self.esem("sync"), 16)])
                P.add("sync", lambda e, gck=gck, ck=ck: e.dma_start(
                    g["dPF"][gck][:, 128:192], AQK["f"][:, 2 * PT + ck * C:2 * PT + (ck + 1) * C]),
                    incs=[(self.esem("sync"), 16)])
                P.add("sync", lambda e, gck=gck, ck=ck: e.dma_start(
                    g["dPS"][gck][:, 0:64], AQK["s"][:, ck * C:(ck + 1) * C]),
                    incs=[(self.esem("sync"), 16)])
                P.add("sync", lambda e, gck=gck, ck=ck: e.dma_start(
                    g["dPS"][gck][:, 64:128], AQK["s"][:, PT + ck * C:PT + (ck + 1) * C]),
                    incs=[(self.esem("sync"), 16)])
                P.add("sync", lambda e, gck=gck, ck=ck: e.dma_start(
                    g["dPS"][gck][:, 128:192], AQK["s"][:, 2 * PT + ck * C:2 * PT + (ck + 1) * C]),
                    incs=[(self.esem("sync"), 16)])
                for m_, Rp in enumerate((R_pw0, R_pw1, R_pw2)):
                    P.add("sync", lambda e, gck=gck, m_=m_, Rp=Rp, ck=ck: e.dma_start(
                        g["dROWS"][gck][0:1, (2 + m_) * C:(3 + m_) * C], Rp[0:1, ck * C:(ck + 1) * C]),
                        incs=[(self.esem("sync"), 16)])
                P.add("sync", lambda e, gck=gck, ck=ck: e.dma_start(
                    g["dV"][gck], vv[:, ck * C:(ck + 1) * C]), incs=[(self.esem("sync"), 16)])
                op("vector", lambda e, ck=ck, par=par: e.tensor_scalar(
                    xf[:, 3 * PT + par * C:3 * PT + (par + 1) * C], vv[:, ck * C:(ck + 1) * C],
                    SQK, None, ALU.mult), extra_waits=scr_wait)
                P.add("sync", lambda e, gck=gck, par=par: e.dma_start(
                    g["dVs"][gck], xf[:, 3 * PT + par * C:3 * PT + (par + 1) * C]),
                    waits=[(self.esem("vector"), P.cur(self.esem("vector")))],
                    incs=[(self.esem("sync"), 16)])
                P.add("sync", lambda e, gck=gck, ck=ck: e.dma_start(
                    g["dNU"][gck], nu[:, ck * C:(ck + 1) * C]),
                    waits=[(self.esem("scalar"), P.cur(self.esem("scalar")))],
                    incs=[(self.esem("sync"), 16)])
                pc_scr = xf[:, 3 * PT + 2 * C + par * 2:3 * PT + 2 * C + par * 2 + 2]
                op("vector", lambda e, pc_scr=pc_scr, ck=ck: e.tensor_copy(
                    pc_scr[:, 0:1], S["f"]["Pp"][:, ck * C + C - 1:ck * C + C]), extra_waits=scr_wait)
                op("vector", lambda e, pc_scr=pc_scr, ck=ck: e.tensor_copy(
                    pc_scr[:, 1:2], S["s"]["Pp"][:, ck * C + C - 1:ck * C + C]))
                P.add("sync", lambda e, gck=gck, pc_scr=pc_scr: e.dma_start(g["dPC"][gck], pc_scr),
                      waits=[(self.esem("vector"), P.cur(self.esem("vector")))],
                      incs=[(self.esem("sync"), 16)])
                self.sync_after[gck] = P.cur(self.esem("sync"))
        self.a1_end = self.barrier_waits()

    def emit_B(self, ex, ar, arb):
        nc, P, g = self.nc, self.P, self.g
        ps = lambda name, shape, dt=F32: ex.enter_context(nc.psum_tensor(name, shape, dt))
        op = self.op
        NCK = self.NCK
        Ev = self.esem("vector"); Et = self.esem("tensor"); Es_ = self.esem("scalar")
        Esy = self.esem("sync")

        outar = ar.alloc(self.T)
        self.outar = outar
        F0 = ar.alloc(128); S0 = ar.alloc(128); nSp = ar.alloc(1)
        FLAGS = ar.alloc(max(self.NCK, 64))
        self.flags_view = FLAGS
        oas = ar.alloc(192)
        esq = ar.alloc(64)
        vrow = ar.alloc(64)
        stream = []
        for par in range(2):
            stream.append(dict(
                PF=ar.alloc(192), PS=ar.alloc(192), V=ar.alloc(64), Vs=ar.alloc(64),
                NU=ar.alloc(64), PC=ar.alloc(2), ROWS=ar.alloc(512),
                MITf=arb.alloc(64), MITs=arb.alloc(64),
                GQTf=arb.alloc(64), GQTs=arb.alloc(64),
                GOTf=arb.alloc(64), GOTs=arb.alloc(64),
                URf=arb.alloc(128), URs=arb.alloc(128),
                PWBC=ar.alloc(192), SROW=ar.alloc(64),
            ))
        Wb = arb.alloc(128)
        WbTf = arb.alloc(128)
        WbTs = arb.alloc(128)
        SWf = arb.alloc(128)
        SWs = arb.alloc(128)

        psB1 = ps("psB1", [128, 128])
        pT1 = ex.enter_context(nc.psum_tensor("pT1", [64, 128], BF16))
        pT2 = ex.enter_context(nc.psum_tensor("pT2", [64, 128], BF16))
        psSW = ps("psSW", [64, 256])
        psDOO = [ps("psDOO0", [128, 192]), ps("psDOO1", [128, 192])]
        psNEW = ps("psNEW", [128, 256])
        psMISC = ps("psMISC", [128, 192])

        op("vector", lambda e: e.memset(F0[:, :], 0.0), deps=("sync", "tensor", "scalar", "gpsimd"))
        op("vector", lambda e: e.memset(outar[:, :], 0.0))
        op("vector", lambda e: e.memset(S0[:, :], 0.0))
        op("vector", lambda e: e.memset(nSp[:, :], 0.0))
        op("vector", lambda e: e.memset(FLAGS[0:1, :], 1.0))

        def stream_chunk(ck):
            if ck >= NCK:
                return
            par = ck % 2
            sb = stream[par]
            first = True
            def dma(dst, srcfn):
                nonlocal first
                w = [(Ev, P.cur(Ev)), (Et, P.cur(Et)), (Es_, P.cur(Es_))] if first else []
                first = False
                P.add("sync", lambda e, dst=dst, srcfn=srcfn: e.dma_start(dst, srcfn()), waits=w,
                      incs=[(Esy, 16)])
            dma(sb["PF"], lambda ck=ck: g["dPF"][ck])
            dma(sb["PS"], lambda ck=ck: g["dPS"][ck])
            dma(sb["V"], lambda ck=ck: g["dV"][ck])
            dma(sb["Vs"], lambda ck=ck: g["dVs"][ck])
            dma(sb["NU"], lambda ck=ck: g["dNU"][ck])
            dma(sb["PC"], lambda ck=ck: g["dPC"][ck])
            dma(sb["ROWS"][0:1, :], lambda ck=ck: g["dROWS"][ck])
            dma(sb["MITf"][0:64, :], lambda ck=ck: g["dMIT"][ck][0:64, 0:64])
            dma(sb["MITs"][0:64, :], lambda ck=ck: g["dMIT"][ck][64:128, 64:128])
            dma(sb["GQTf"][0:64, :], lambda ck=ck: g["dGQT"][ck][0:64, :])
            dma(sb["GQTs"][0:64, :], lambda ck=ck: g["dGQT"][ck][64:128, :])
            dma(sb["GOTf"][0:64, :], lambda ck=ck: g["dGOT"][ck][0:64, :])
            dma(sb["GOTs"][0:64, :], lambda ck=ck: g["dGOT"][ck][64:128, :])
            dma(sb["URf"][0:64, :], lambda ck=ck: g["dUR"][ck][0:64, :])
            dma(sb["URs"][0:64, :], lambda ck=ck: g["dUR"][ck][64:128, :])

        def do_chunk(ck):
            ST = B_STAGE
            par = ck % 2
            sb = stream[par]
            pD = psDOO[par]
            for m_ in range(3):
                op("tensor", lambda e, sb=sb, m_=m_: e.matmul(
                    psMISC[:, m_ * C:(m_ + 1) * C], self.onesrow[0:1, 0:128],
                    sb["ROWS"][0:1, (2 + m_) * C:(3 + m_) * C], start=True, stop=True),
                   deps=("vector", "sync", "scalar") if m_ == 0 else ())
            op("vector", lambda e, sb=sb: e.tensor_copy(sb["PWBC"][:, :], psMISC[:, :]),
               deps=("tensor",))
            if ST < 2:
                return
            op("tensor", lambda e, sb=sb: e.matmul(psB1[:, 0:64], F0[:, :], sb["PF"][:, 0:64],
                                                   start=True, stop=False), deps=("vector",))
            op("tensor", lambda e, sb=sb: e.matmul(psB1[:, 0:64], self.eyef[:, :], sb["V"],
                                                   start=False, stop=True))
            op("tensor", lambda e, sb=sb: e.matmul(psB1[:, 64:128], S0[:, :], sb["PS"][:, 64:128],
                                                   start=True, stop=False))
            op("tensor", lambda e, sb=sb: e.matmul(psB1[:, 64:128], self.eyef[:, :], sb["V"],
                                                   start=False, stop=True))
            op("vector", lambda e: e.tensor_copy(Wb[:, :], psB1[:, :]), deps=("tensor",))
            if ST < 3:
                return
            op("tensor", lambda e: e.transpose(pT1[:, :], Wb[:, 0:64], self.eyeb[:, :]),
               deps=("vector",))
            op("tensor", lambda e: e.transpose(pT2[:, :], Wb[:, 64:128], self.eyeb[:, :]))
            op("vector", lambda e: e.tensor_copy(WbTf[0:64, :], pT1[:, :]), deps=("tensor",))
            op("vector", lambda e: e.tensor_copy(WbTs[0:64, :], pT2[:, :]))
            if ST < 4:
                return
            op("tensor", lambda e, sb=sb: e.matmul(psSW[:, 0:128], sb["MITf"][0:64, :],
                                                   WbTf[0:64, :], start=True, stop=True),
               deps=("vector",))
            op("tensor", lambda e, sb=sb: e.matmul(psSW[:, 128:256], sb["MITs"][0:64, :],
                                                   WbTs[0:64, :], start=True, stop=True))
            op("vector", lambda e: e.tensor_copy(SWf[0:64, :], psSW[:, 0:128]), deps=("tensor",))
            op("vector", lambda e: e.tensor_copy(SWs[0:64, :], psSW[:, 128:256]))
            if ST < 5:
                return
            op("tensor", lambda e, sb=sb, pD=pD: e.matmul(pD[:, 0:64], F0[:, :],
                                                          sb["PF"][:, 128:192],
                                                          start=True, stop=False),
               deps=("vector", "scalar"))
            op("tensor", lambda e, sb=sb, pD=pD: e.matmul(pD[:, 0:64], S0[:, :],
                                                          sb["PS"][:, 0:64],
                                                          start=False, stop=False))
            op("tensor", lambda e, sb=sb, pD=pD: e.matmul(pD[:, 0:64], self.eyef[:, :],
                                                          sb["Vs"], start=False, stop=False))
            op("tensor", lambda e, sb=sb, pD=pD: e.matmul(pD[:, 0:64], SWf[0:64, :],
                                                          sb["GQTf"][0:64, :],
                                                          start=False, stop=False))
            op("tensor", lambda e, sb=sb, pD=pD: e.matmul(pD[:, 0:64], SWs[0:64, :],
                                                          sb["GQTs"][0:64, :],
                                                          start=False, stop=True))
            op("tensor", lambda e, sb=sb, pD=pD: e.matmul(pD[:, 64:128], F0[:, :],
                                                          sb["PF"][:, 64:128],
                                                          start=True, stop=False))
            op("tensor", lambda e, sb=sb, pD=pD: e.matmul(pD[:, 64:128], SWf[0:64, :],
                                                          sb["GOTf"][0:64, :],
                                                          start=False, stop=True))
            op("tensor", lambda e, sb=sb, pD=pD: e.matmul(pD[:, 128:192], S0[:, :],
                                                          sb["PS"][:, 128:192],
                                                          start=True, stop=False))
            op("tensor", lambda e, sb=sb, pD=pD: e.matmul(pD[:, 128:192], SWs[0:64, :],
                                                          sb["GOTs"][0:64, :],
                                                          start=False, stop=True))
            if ST < 6:
                return
            op("scalar", lambda e, pD=pD: e.activation(esq[:, :], pD[:, 0:64], AF.Square),
               deps=("tensor",))
            if ST == 60:
                return
            op("tensor", lambda e: e.matmul(psMISC[0:1, 0:64], self.consts[:, 0:1], esq[:, :],
                                            start=True, stop=True), deps=("scalar", "vector"))
            if ST == 61:
                return
            op("scalar", lambda e, sb=sb: e.activation(sb["SROW"][0:1, :], psMISC[0:1, 0:64],
                                                       AF.Sigmoid, scale=float(1.0 / (1.0 + 1e-6))),
               deps=("tensor",))
            if ST == 62:
                return
            op("vector", lambda e: e.tensor_scalar(vrow[0:1, :], psMISC[0:1, 0:64], 38.0, None,
                                                   ALU.is_ge), deps=("tensor", "scalar"))
            if ST == 63:
                return
            op("vector", lambda e, ck=ck: e.tensor_reduce(FLAGS[0:1, ck:ck + 1], vrow[0:1, :],
                                                          mybir.AxisListType.X, ALU.min),
               deps=("vector",))
            if ST == 64:
                return
            if ST < 7:
                return
            srow = sb["SROW"]
            op("vector", lambda e, srow=srow: e.tensor_scalar(oas[0:1, 0:C], srow[0:1, :], 0.7,
                                                              None, ALU.is_gt), deps=("scalar",))
            op("vector", lambda e: e.tensor_scalar(oas[0:1, C:2 * C], oas[0:1, 0:C], -0.1, 1.0,
                                                   ALU.mult, ALU.add), deps=("vector",))
            op("tensor", lambda e: e.matmul(psMISC[:, 0:C], self.onesrow[0:1, 0:128],
                                            oas[0:1, 0:C], start=True, stop=True),
               deps=("vector", "scalar"))
            op("tensor", lambda e: e.matmul(psMISC[:, C:2 * C], self.onesrow[0:1, 0:128],
                                            oas[0:1, C:2 * C], start=True, stop=True))
            op("vector", lambda e, sb=sb: e.tensor_tensor(oas[:, 0:C], sb["NU"], psMISC[:, 0:C],
                                                          ALU.mult), deps=("tensor",))
            op("vector", lambda e: e.tensor_copy(oas[:, C:2 * C], psMISC[:, C:2 * C]))
            op("vector", lambda e: e.tensor_tensor(oas[:, 2 * C:2 * C + 1], nSp[:, :],
                                                   oas[:, C:C + 1], ALU.mult), deps=("vector",))
            op("vector", lambda e: e.tensor_tensor(oas[:, 0:1], oas[:, 0:1], oas[:, 2 * C:2 * C + 1],
                                                   ALU.add), deps=("vector",))
            op("vector", lambda e: e.tensor_tensor_scan(oas[:, 2 * C:3 * C], oas[:, C:2 * C],
                                                        oas[:, 0:C], 0.0, ALU.mult, ALU.add),
               deps=("vector",))
            op("vector", lambda e: e.tensor_copy(nSp[:, :], oas[:, 3 * C - 1:3 * C]),
               deps=("vector",))
            oc = outar[:, ck * C:(ck + 1) * C]
            op("vector", lambda e, sb=sb, oc=oc, pD=pD: e.tensor_tensor(
                oc, pD[:, 64:128], sb["PWBC"][:, 0:C], ALU.mult), deps=("scalar",))
            op("vector", lambda e, sb=sb, pD=pD: e.tensor_tensor(
                oas[:, 0:C], pD[:, 128:192], sb["PWBC"][:, C:2 * C], ALU.mult))
            op("vector", lambda e, sb=sb: e.tensor_tensor(
                oas[:, C:2 * C], oas[:, 2 * C:3 * C], sb["PWBC"][:, 2 * C:3 * C], ALU.mult),
               deps=("vector",))
            op("vector", lambda e, oc=oc: e.tensor_tensor(oc, oc, oas[:, 0:C], ALU.add),
               deps=("vector",))
            op("vector", lambda e, oc=oc: e.tensor_tensor(oc, oc, oas[:, C:2 * C], ALU.add),
               deps=("vector",))
            if ST < 8:
                return
            op("tensor", lambda e, sb=sb: e.matmul(psNEW[:, 0:128], sb["URf"][0:64, :],
                                                   SWf[0:64, :], start=True, stop=True),
               deps=("vector",))
            op("tensor", lambda e, sb=sb: e.matmul(psNEW[:, 128:256], sb["URs"][0:64, :],
                                                   SWs[0:64, :], start=True, stop=True))
            op("vector", lambda e, sb=sb: e.scalar_tensor_tensor(
                F0[:, :], F0[:, :], sb["PC"][:, 0:1], psNEW[:, 0:128], ALU.mult, ALU.add),
               deps=("tensor",))
            op("vector", lambda e, sb=sb: e.scalar_tensor_tensor(
                S0[:, :], S0[:, :], sb["PC"][:, 1:2], psNEW[:, 128:256], ALU.mult, ALU.add))

        def emit_dbg(tag):
            nc_ = self.nc
            self.op("vector", lambda e: e.tensor_copy(oas[:, :], psDOO[0][:, :]),
                    deps=("tensor", "scalar", "vector"))
            views = dict(Wb=(Wb, BF16), WbTf=(WbTf[0:64, :], BF16), WbTs=(WbTs[0:64, :], BF16),
                         SWf=(SWf[0:64, :], BF16), SWs=(SWs[0:64, :], BF16),
                         pD=(oas, F32), PWBC=(stream[0]["PWBC"], F32),
                         SROW=(stream[0]["SROW"], F32),
                         F0=(F0, F32), S0=(S0, F32), FLAGS=(FLAGS[0:1, :], F32))
            w = self.barrier_waits()
            for nm, (v, dt) in views.items():
                t = nc_.dram_tensor(f"o_dbg{tag}_{nm}", list(v.shape), dt,
                                    kind="ExternalOutput").ap()
                self.P.add("sync", lambda e, t=t, v=v: e.dma_start(t, v),
                           waits=w, incs=[(self.esem("sync"), 16)])
                w = ()

        stream_chunk(0)
        stream_chunk(1)
        for ck in range(NCK):
            do_chunk(ck)
            if ck == 0 and getattr(self, "_dbgB", False):
                emit_dbg("0")
            stream_chunk(ck + 2)
        self.b_end = self.barrier_waits()


    def emit_CAST(self, ex, ar, arb):
        nc, P, g, op = self.nc, self.P, self.g, self.op
        land = [ar.alloc(2048) for _ in range(4)]
        bout = [arb.alloc(2048) for _ in range(4)]
        Esy = self.esem("sync")
        Egp = self.esem("gpsimd")
        Ev = self.esem("vector")

        units = []
        for e_ in range(16):
            for c_ in range(11):
                units.append((g["W1T"][e_][:, 128 * c_:128 * c_ + 128], 16, g["dW1b"][e_][c_]))
                units.append((g["W3T"][e_][:, 128 * c_:128 * c_ + 128], 16, g["dW3b"][e_][c_]))
            for m_ in range(16):
                units.append((g["W2T"][e_][:, 128 * m_:128 * m_ + 128], 11, g["dW2b"][e_][m_]))
        for c_ in range(22):
            units.append((g["Ws1T"][:, 128 * c_:128 * c_ + 128], 16, g["dWs1b"][c_]))
            units.append((g["Ws3T"][:, 128 * c_:128 * c_ + 128], 16, g["dWs3b"][c_]))
        for hf_ in range(2):
            for m_ in range(16):
                units.append((g["Ws2T"][1408 * hf_:1408 * (hf_ + 1), 128 * m_:128 * m_ + 128],
                              11, g["dWs2b"][hf_][m_]))

        ev_cnt = {}
        gp_cnt = {}
        for i, (srcap, ns, dstap) in enumerate(units):
            par = i % 4
            w = self.barrier_waits() if i == 0 else (
                [(Ev, ev_cnt[i - 4])] if i >= 4 else [])
            P.add("sync", lambda e, srcap=srcap, ns=ns, par=par: e.dma_start(
                land[par][:, 0:128 * ns].rearrange("p (s c) -> p s c", s=ns),
                srcap.rearrange("(s p) c -> p s c", p=128)),
                waits=w, incs=[(Esy, 16)])
            syw = [(Esy, P.cur(Esy))]
            if i >= 4:
                syw.append((Egp, gp_cnt[i - 4]))
            P.add("vector", lambda e, ns=ns, par=par: e.tensor_copy(
                bout[par][:, 0:128 * ns], land[par][:, 0:128 * ns]),
                waits=syw, incs=[(Ev, 1)])
            ev_cnt[i] = P.cur(Ev)
            P.add("gpsimd", lambda e, dstap=dstap, ns=ns, par=par: e.dma_start(
                dstap[:, 0:128 * ns], bout[par][:, 0:128 * ns]),
                waits=[(Ev, ev_cnt[i])], incs=[(Egp, 16)])
            gp_cnt[i] = P.cur(Egp)

    def emit_C(self, ex, ar, arb):
        nc, P, g, op = self.nc, self.P, self.g, self.op
        NC_ = self.n_cores
        my_tok = self.T // NC_
        NBLK = my_tok // 512
        assert my_tok % 512 == 0
        NFF = 11
        NSH = 22

        dOUTA = g["dOUTA"]
        o_y = nc.dram_tensor("o_y", [16, 128, my_tok], F32, kind="ExternalOutput").ap()
        xT4 = [g[f"x4d_{m_}"] for m_ in range(16)]
        _cidc = {}

        def cid(e):
            if NC_ == 1:
                return 0
            k = id(e)
            if k not in _cidc:
                _cidc[k] = e.partition_id()
            return _cidc[k]

        myat = ar.alloc(512)
        sq = ar.alloc(512)
        hsq = [ar.alloc(512), ar.alloc(512)]
        hbuf = ar.alloc(16 * 512)
        xc = [ar.alloc(512), ar.alloc(512)]
        ysb = ar.alloc(16 * 512)
        w1l = ar.alloc(2048)
        gsc = ar.alloc(96)
        h1b = arb.alloc(NFF * 512)
        w1b = [arb.alloc(2048), arb.alloc(2048)]
        w3b = [arb.alloc(2048), arb.alloc(2048)]
        wpbA = Arena(self.wpb[:, :], 6 * NCH * 128)
        h2bf = wpbA.alloc(16 * 512)
        ybf = wpbA.alloc(512)
        gwb = wpbA.alloc(256)
        w2b = [wpbA.alloc(1408), wpbA.alloc(1408)]
        crows = ar.alloc(2560)
        geAll = ar.alloc(512)
        gerow = ar.alloc(512)

        pA = ex.enter_context(nc.psum_tensor("cpA", [128, 512], F32))
        pB = ex.enter_context(nc.psum_tensor("cpB", [128, 512], F32))
        pC = ex.enter_context(nc.psum_tensor("cpC", [128, 512], F32))
        pY = [ex.enter_context(nc.psum_tensor("cpY0", [128, 512], F32)),
              ex.enter_context(nc.psum_tensor("cpY1", [128, 512], F32))]
        pR1 = ex.enter_context(nc.psum_tensor("cpR1", [1, 512], F32))
        pR2 = ex.enter_context(nc.psum_tensor("cpR2", [1, 512], F32))

        Esy = self.esem("sync")

        P.add("sync", lambda e: e.dma_start(
            w1l[:, 0:256].rearrange("p (s e) -> p s e", s=NCH),
            g["gate_wT"].rearrange("(s p) e -> p s e", p=128)), incs=[(Esy, 16)])
        op("vector", lambda e: e.tensor_copy(gwb[:, :], w1l[:, 0:256]), deps=("sync",))
        op("vector", lambda e: e.memset(ysb[:, :], 0.0))

        mc = self.mcols
        for b in range(NBLK):
            tb = b * 512

            P.add("sync", lambda e, b=b: e.dma_start(
                myat[:, :], dOUTA[b][cid(e), :, :]),
                waits=[(self.esem("vector"), P.cur(self.esem("vector")))], incs=[(Esy, 16)])
            op("vector", lambda e: e.tensor_tensor(sq[:, :], myat[:, :], myat[:, :], ALU.mult),
               deps=("sync",))
            op("tensor", lambda e: e.matmul(pR1[0:1, :], self.consts[:, 0:1], sq[:, :],
                                            start=True, stop=True), deps=("vector", "scalar"))
            op("scalar", lambda e: e.activation(crows[0:1, 0:512], pR1[0:1, :], AF.Sqrt,
                                                scale=float(1.0 / 128.0),
                                                bias=self.rows0[0:1, 31:32]), deps=("tensor",))
            op("vector", lambda e: e.reciprocal(crows[0:1, 512:1024], crows[0:1, 0:512]),
               deps=("scalar",))
            op("tensor", lambda e: e.matmul(pA[:, :], self.onesrow[0:1, 0:128],
                                            crows[0:1, 512:1024], start=True, stop=True),
               deps=("vector",))
            op("vector", lambda e: e.tensor_tensor(ybf[:, :], myat[:, :], pA[:, :], ALU.mult),
               deps=("tensor",))

            for m in range(16):
                par = m % 2
                pO = pB if par == 0 else pC
                P.add("sync", lambda e, m=m, b=b, par=par: e.dma_start(
                    xc[par][:, :], xT4[m][b][cid(e), :, :]),
                    waits=[(self.esem("vector"), P.cur(self.esem("vector")))], incs=[(Esy, 16)])
                op("tensor", lambda e, m=m, pO=pO: e.matmul(
                    pO[:, :], self.wof[:, m * 128:(m + 1) * 128], ybf[:, :],
                    start=True, stop=True), deps=("vector",))
                op("vector", lambda e, m=m, pO=pO, par=par: e.scalar_tensor_tensor(
                    hbuf[:, m * 512:(m + 1) * 512], pO[:, :], mc[:, 32 + m:33 + m],
                    xc[par][:, :], ALU.mult, ALU.add), deps=("tensor", "sync"))

            for m in range(16):
                op("tensor", lambda e, m=m: e.matmul(
                    pR1[0:1, :], self.consts[:, 0:1], hbuf[:, m * 512:(m + 1) * 512],
                    start=(m == 0), stop=(m == 15)),
                   deps=("vector",) if m == 0 else ())
            for m in range(16):
                par = m % 2
                op("vector", lambda e, m=m, par=par: e.tensor_tensor(
                    hsq[par][:, :], hbuf[:, m * 512:(m + 1) * 512],
                    hbuf[:, m * 512:(m + 1) * 512], ALU.mult),
                   deps=("tensor",) if m < 2 else ("tensor",))
                op("tensor", lambda e, m=m, par=par: e.matmul(
                    pR2[0:1, :], self.consts[:, 0:1], hsq[par][:, :],
                    start=(m == 0), stop=(m == 15)), deps=("vector",))

            op("vector", lambda e: e.tensor_scalar(crows[0:1, 0:512], pR1[0:1, :],
                                                   float(1.0 / 2048.0), None, ALU.mult),
               deps=("tensor",))
            op("vector", lambda e: e.tensor_tensor(crows[0:1, 512:1024], crows[0:1, 0:512],
                                                   crows[0:1, 0:512], ALU.mult),
               deps=("vector",))
            op("vector", lambda e: e.tensor_scalar(crows[0:1, 1024:1536], pR2[0:1, :],
                                                   float(1.0 / 2048.0), EPS_LN,
                                                   ALU.mult, ALU.add), deps=("vector",))
            op("vector", lambda e: e.tensor_tensor(crows[0:1, 1024:1536], crows[0:1, 1024:1536],
                                                   crows[0:1, 512:1024], ALU.subtract),
               deps=("vector",))
            op("scalar", lambda e: e.activation(crows[0:1, 1536:2048], crows[0:1, 1024:1536],
                                                AF.Sqrt), deps=("vector",))
            op("vector", lambda e: e.reciprocal(crows[0:1, 512:1024], crows[0:1, 1536:2048]),
               deps=("scalar",))
            op("vector", lambda e: e.tensor_tensor(crows[0:1, 2048:2560], crows[0:1, 0:512],
                                                   crows[0:1, 512:1024], ALU.mult),
               deps=("vector",))
            op("tensor", lambda e: e.matmul(pA[:, :], self.onesrow[0:1, 0:128],
                                            crows[0:1, 512:1024], start=True, stop=True),
               deps=("vector",))
            op("tensor", lambda e: e.matmul(pB[:, :], self.onesrow[0:1, 0:128],
                                            crows[0:1, 2048:2560], start=True, stop=True))

            for m in range(16):
                op("vector", lambda e, m=m: e.tensor_tensor(
                    sq[:, :], hbuf[:, m * 512:(m + 1) * 512], pA[:, :], ALU.mult),
                   deps=("tensor",) if m == 0 else ("vector",))
                op("vector", lambda e: e.tensor_tensor(sq[:, :], sq[:, :], pB[:, :],
                                                       ALU.subtract), deps=("vector",))
                op("vector", lambda e, m=m: e.tensor_scalar(
                    h2bf[:, m * 512:(m + 1) * 512], sq[:, :], mc[:, 64 + m:65 + m],
                    mc[:, 48 + m:49 + m], ALU.mult, ALU.add), deps=("vector",))

            for s4 in range(4):
                for s in range(16):
                    op("tensor", lambda e, s=s, s4=s4: e.matmul(
                        pC[:, 0:16], h2bf[:, 512 * s + 128 * s4:512 * s + 128 * s4 + 128],
                        gwb[:, 16 * s:16 * s + 16], start=(s == 0), stop=(s == 15)),
                       deps=("vector", "scalar") if s == 0 else ())
                op("scalar", lambda e: e.activation(gsc[:, 0:16], pC[:, 0:16], AF.Sigmoid),
                   deps=("tensor",))
                for gi in range(4):
                    op("vector", lambda e, gi=gi: e.tensor_reduce(
                        gsc[:, 16 + gi:17 + gi], gsc[:, 4 * gi:4 * gi + 4],
                        mybir.AxisListType.X, ALU.max),
                       deps=("scalar",) if gi == 0 else ("vector",))
                op("vector", lambda e: e.tensor_reduce(gsc[:, 20:21], gsc[:, 16:20],
                                                       mybir.AxisListType.X, ALU.max),
                   deps=("vector",))
                op("vector", lambda e: e.tensor_scalar(gsc[:, 24:28], gsc[:, 16:20],
                                                       gsc[:, 20:21], None, ALU.is_ge),
                   deps=("vector",))
                op("vector", lambda e: e.scalar_tensor_tensor(gsc[:, 28:32], gsc[:, 24:28],
                                                              -1e9, gsc[:, 16:20],
                                                              ALU.mult, ALU.add),
                   deps=("vector",))
                op("vector", lambda e: e.tensor_reduce(gsc[:, 21:22], gsc[:, 28:32],
                                                       mybir.AxisListType.X, ALU.max),
                   deps=("vector",))
                op("vector", lambda e: e.tensor_scalar(gsc[:, 24:28], gsc[:, 16:20],
                                                       gsc[:, 21:22], None, ALU.is_ge),
                   deps=("vector",))
                op("vector", lambda e: e.tensor_tensor(
                    gsc[:, 32:48].rearrange("p (a z) -> p a z", a=4),
                    gsc[:, 0:16].rearrange("p (a z) -> p a z", a=4),
                    gsc[:, 24:28].rearrange("p (a z) -> p a z", a=4).broadcast_to([128, 4, 4]),
                    ALU.mult), deps=("vector",))
                op("vector", lambda e: e.tensor_copy(gsc[:, 48:64], gsc[:, 32:48]),
                   deps=("vector",))
                for it in range(3):
                    op("vector", lambda e: e.tensor_reduce(gsc[:, 22:23], gsc[:, 48:64],
                                                           mybir.AxisListType.X, ALU.max),
                       deps=("vector",))
                    op("vector", lambda e: e.tensor_scalar(gsc[:, 64:80], gsc[:, 48:64],
                                                           gsc[:, 22:23], None, ALU.is_ge),
                       deps=("vector",))
                    op("vector", lambda e: e.scalar_tensor_tensor(gsc[:, 48:64], gsc[:, 64:80],
                                                                  -1e9, gsc[:, 48:64],
                                                                  ALU.mult, ALU.add),
                       deps=("vector",))
                op("vector", lambda e: e.tensor_reduce(gsc[:, 23:24], gsc[:, 48:64],
                                                       mybir.AxisListType.X, ALU.max),
                   deps=("vector",))
                op("vector", lambda e: e.tensor_scalar(gsc[:, 64:80], gsc[:, 32:48],
                                                       gsc[:, 23:24], None, ALU.is_ge),
                   deps=("vector",))
                op("vector", lambda e: e.tensor_tensor(gsc[:, 80:96], gsc[:, 0:16],
                                                       gsc[:, 64:80], ALU.mult),
                   deps=("vector",))
                op("vector", lambda e: e.tensor_reduce(gsc[:, 20:21], gsc[:, 80:96],
                                                       mybir.AxisListType.X, ALU.add),
                   deps=("vector",))
                op("vector", lambda e: e.reciprocal(gsc[:, 21:22], gsc[:, 20:21]),
                   deps=("vector",))
                op("vector", lambda e: e.tensor_scalar(gsc[:, 32:48], gsc[:, 80:96],
                                                       gsc[:, 21:22], None, ALU.mult),
                   deps=("vector",))
                op("tensor", lambda e: e.transpose(pA[0:16, 0:128], gsc[:, 32:48],
                                                   self.eyef[:, :]), deps=("vector",))
                op("vector", lambda e, s4=s4: e.tensor_copy(
                    geAll[0:16, 128 * s4:128 * s4 + 128], pA[0:16, 0:128]), deps=("tensor",))

            Et_ = self.esem("tensor")
            lastread = {"w1_0": 0, "w1_1": 0, "w2_0": 0, "w2_1": 0}

            def ffn_branch(is_shared, e_idx):
                nch_ff = NSH if is_shared else NFF
                halves = 2 if is_shared else 1
                per = nch_ff // halves
                if not is_shared:
                    P.add("sync", lambda e: e.dma_start(gerow[0:1, :], geAll[e_idx:e_idx + 1, :]),
                          waits=[(self.esem("tensor"), P.cur(self.esem("tensor"))),
                                 (self.esem("vector"), P.cur(self.esem("vector")))],
                          incs=[(Esy, 16)])
                    op("tensor", lambda e: e.matmul(pA[:, :], self.onesrow[0:1, 0:128],
                                                    gerow[0:1, :], start=True, stop=True),
                       deps=("vector", "sync"))
                for hf in range(halves):
                    for c in range(per):
                        ch = hf * per + c
                        wpar = ch % 2
                        if is_shared:
                            s1 = g["dWs1b"][ch]
                            s3 = g["dWs3b"][ch]
                        else:
                            s1 = g["dW1b"][e_idx][ch]
                            s3 = g["dW3b"][e_idx][ch]
                        P.add("sync", lambda e, s1=s1, wpar=wpar: e.dma_start(
                            w1b[wpar][:, :], s1),
                            waits=[(Et_, lastread[f"w1_{wpar}"])],
                            incs=[(Esy, 16)])
                        P.add("sync", lambda e, s3=s3, wpar=wpar: e.dma_start(
                            w3b[wpar][:, :], s3), incs=[(Esy, 16)])
                        for s in range(16):
                            op("tensor", lambda e, s=s, wpar=wpar: e.matmul(
                                pB[:, :], w1b[wpar][:, 128 * s:128 * s + 128],
                                h2bf[:, 512 * s:512 * s + 512], start=(s == 0), stop=(s == 15)),
                               deps=("vector", "scalar", "sync") if s == 0 else ())
                        for s in range(16):
                            op("tensor", lambda e, s=s, wpar=wpar: e.matmul(
                                pC[:, :], w3b[wpar][:, 128 * s:128 * s + 128],
                                h2bf[:, 512 * s:512 * s + 512], start=(s == 0), stop=(s == 15)),
                               deps=("vector",) if s == 0 else ())
                        lastread[f"w1_{wpar}"] = P.cur(Et_)
                        op("scalar", lambda e: e.activation(sq[:, :], pB[:, :], AF.Sigmoid),
                           deps=("tensor",))
                        op("vector", lambda e: e.tensor_tensor(sq[:, :], sq[:, :], pB[:, :],
                                                               ALU.mult),
                           deps=("scalar", "vector"))
                        op("vector", lambda e: e.tensor_tensor(sq[:, :], sq[:, :], pC[:, :],
                                                               ALU.mult), deps=("vector",))
                        if is_shared:
                            op("vector", lambda e, c=c: e.tensor_copy(
                                h1b[:, 512 * c:512 * c + 512], sq[:, :]), deps=("vector",))
                        else:
                            op("vector", lambda e, c=c: e.tensor_tensor(
                                h1b[:, 512 * c:512 * c + 512], sq[:, :], pA[:, :], ALU.mult),
                               deps=("vector",))
                    for m in range(16):
                        wpar = m % 2
                        if is_shared:
                            s2 = g["dWs2b"][hf][m]
                        else:
                            s2 = g["dW2b"][e_idx][m]
                        P.add("sync", lambda e, s2=s2, wpar=wpar: e.dma_start(
                            w2b[wpar][:, :], s2),
                            waits=[(Et_, lastread[f"w2_{wpar}"])],
                            incs=[(Esy, 16)])
                        pYm = pY[m % 2]
                        for c in range(per):
                            op("tensor", lambda e, c=c, pYm=pYm, wpar=wpar: e.matmul(
                                pYm[:, :], w2b[wpar][:, 128 * c:128 * c + 128],
                                h1b[:, 512 * c:512 * c + 512], start=(c == 0),
                                stop=(c == per - 1)),
                               deps=("vector", "sync") if c == 0 else ())
                        lastread[f"w2_{wpar}"] = P.cur(Et_)
                        op("vector", lambda e, m=m, pYm=pYm: e.tensor_tensor(
                            ysb[:, 512 * m:512 * m + 512], ysb[:, 512 * m:512 * m + 512],
                            pYm[:, :], ALU.add), deps=("tensor",))

            for e_idx in range(16):
                ffn_branch(False, e_idx)
            ffn_branch(True, 0)

            if b == 0 and getattr(self, "_dbgC", False):
                wdb = self.barrier_waits()
                for nm, v, dt in [("hbuf", hbuf, F32), ("h2bf", h2bf, BF16),
                                  ("geAll", geAll[0:16, :], F32), ("ysb", ysb, F32),
                                  ("myat", myat, F32), ("crows", crows[0:1, :], F32),
                                  ("ybf", ybf, BF16), ("wof", self.wof[:, :], BF16),
                                  ("mcols", self.mcols[:, :], F32)]:
                    tdb = nc.dram_tensor(f"o_dbgC_{nm}", list(v.shape), dt,
                                         kind="ExternalOutput").ap()
                    P.add("sync", lambda e, t=tdb, v=v: e.dma_start(t, v),
                          waits=wdb, incs=[(Esy, 16)])
                    wdb = ()
            for m in range(16):
                op("vector", lambda e, m=m: e.scalar_tensor_tensor(
                    hbuf[:, m * 512:(m + 1) * 512], ysb[:, m * 512:(m + 1) * 512],
                    mc[:, 80 + m:81 + m], hbuf[:, m * 512:(m + 1) * 512],
                    ALU.mult, ALU.add),
                   deps=("vector", "sync") if m == 0 else ("vector",))
                P.add("sync", lambda e, m=m, tb=tb: e.dma_start(
                    o_y[m][:, tb:tb + 512], hbuf[:, m * 512:(m + 1) * 512]),
                    waits=[(self.esem("vector"), P.cur(self.esem("vector")))],
                    incs=[(Esy, 16)])
            if b + 1 < NBLK:
                op("vector", lambda e: e.memset(ysb[:, :], 0.0), deps=("tensor",))

    def finalize(self):
        if hasattr(self, "_dbgcols"):
            t = self.nc.dram_tensor("o_dbgcols", [128, 160], F32, kind="ExternalOutput").ap()
            self.P.add("sync", lambda e, t=t: e.dma_start(t, self.cols[:, :]),
                       waits=self.barrier_waits(), incs=[(self.esem("sync"), 16)])
        self.P.add("sync", lambda e: None, waits=self.barrier_waits())


def build(n_tok, prep_out=True, dbgcols=False, do_B=False, out_B=False, do_C=False,
          n_cores=8):
    from contextlib import ExitStack
    nc = bass.Bass("TRN2", target_bir_lowering=False, debug=False)
    mk = MK(nc, n_tok, n_cores=n_cores)
    if dbgcols: mk._dbgcols = True
    if prep_out:
        pass
    g = mk.declare_io()
    if prep_out:
        NCK = mk.NCK
        for nm, shp, dt in [("dPF", [NCK, 128, 192], F32), ("dPS", [NCK, 128, 192], F32),
                            ("dV", [NCK, 128, 64], F32),
                            ("dVs", [NCK, 128, 64], F32), ("dMIT", [NCK, 128, 128], BF16),
                            ("dGQT", [NCK, 128, C], BF16), ("dGOT", [NCK, 128, C], BF16),
                            ("dUR", [NCK, 128, 128], BF16),
                            ("dNU", [NCK, 128, C], F32), ("dROWS", [NCK, 1, 512], F32),
                            ("dPC", [NCK, 128, 2], F32)]:
            g[nm] = nc.dram_tensor("o_" + nm, shp, dt, kind="ExternalOutput").ap()
    from contextlib import ExitStack
    with ExitStack() as ex:
        arT = ex.enter_context(nc.sbuf_tensor("arenaF", [128, 29696], F32))
        arbT = ex.enter_context(nc.sbuf_tensor("arenaB", [128, 16384], BF16))
        ar = Arena(arT[:, :], 29696)
        arb = Arena(arbT[:, :], 16384)
        with ExitStack() as ex_psum:
            mk.emit_A0(ex, ar, ex_psum)
        ar.pop()
        ar.push()
        with ExitStack() as ex_psA1:
            mk.emit_A1(ex_psA1, ar, arb)
        if do_B:
            ar.pop()
            ar.push()
            arb2 = Arena(arbT[:, :], 16384)
            with ExitStack() as ex_psB:
                mk.emit_B(ex_psB, ar, arb2)
            if out_B:
                t = nc.dram_tensor("o_dOUT", [128, mk.T], F32, kind="ExternalOutput").ap()
                mk.P.add("sync", lambda e, t=t: e.dma_start(t, mk.outar),
                         waits=mk.barrier_waits(), incs=[(mk.esem("sync"), 16)])
        if do_C:
            NC_s = mk.n_cores
            NBLK_s = max((mk.T // NC_s) // 512, 1)
            for b_ in range(NBLK_s):
                mk.P.add("sync", lambda e, b_=b_: e.dma_start(
                    g["dOUTA"][b_].rearrange("c p t -> p c t"),
                    mk.outar.rearrange("p (c b t) -> p c b t", c=NC_s, b=NBLK_s)[:, :, b_, :]),
                    waits=mk.barrier_waits() if b_ == 0 else (),
                    incs=[(mk.esem("sync"), 16)])
            ar.pop()
            ar.push()
            arbK = Arena(arbT[:, :], 16384)
            with ExitStack() as ex_psK:
                mk.emit_CAST(ex_psK, ar, arbK)
            ar.pop()
            arbC = Arena(arbT[:, :], 16384)
            with ExitStack() as ex_psC:
                mk.emit_C(ex_psC, ar, arbC)
        mk.finalize()
        with nc.Block() as block:
            mk.P.emit(block)
    return nc, mk


class ex_psB_wrap:
    def __init__(self, ex_outer, ex_psum):
        self.ex_outer = ex_outer
        self.ex_psum = ex_psum

    def enter_context(self, cm):
        return self.ex_psum.enter_context(cm)

build_phaseA = build



from concourse.bass_utils import run_bass_kernel_spmd

_IN_NAMES = ["xT", "Wk", "Wv", "Wq", "Waf", "Was", "ir_w1", "pw_w", "nm_w1", "nm_w2",
             "ir_w2", "Wo", "wmod", "ccolT", "baf", "bas", "lamf", "lams", "nm_b1",
             "nm_b2", "ir_b1", "ir_b2", "pw_b", "norm_w", "gate_wT", "W1T", "W3T",
             "W2T", "Ws1T", "Ws3T", "Ws2T", "ones_col", "twos_col", "ones_row",
             "eye128", "resetmask", "mSTRICT", "mSTRICTN", "mINCL", "eye64x2"] + [
             f"x4d_{m}" for m in range(16)]

N_TOK = 8192
N_CORES = 8


def kernel(**inputs):
    nc, mk = build(N_TOK, prep_out=False, do_B=True, do_C=True, n_cores=N_CORES)
    gh = host_layout(inputs, N_TOK, n_cores=N_CORES)
    in_map = {nm: np.ascontiguousarray(gh[nm], dtype=np.float32) for nm in _IN_NAMES}
    res = run_bass_kernel_spmd(nc, [in_map] * N_CORES, core_ids=list(range(N_CORES)))
    my_tok = N_TOK // N_CORES
    out = np.zeros((1, N_TOK, 2048), dtype=np.float32)
    for c in range(N_CORES):
        oy = np.asarray(res.results[c]["o_y"], dtype=np.float32)
        out[0, c * my_tok:(c + 1) * my_tok, :] = \
            oy.transpose(2, 0, 1).reshape(my_tok, 2048)
    return out


# version 03 (final)
# speedup vs baseline: 1.3857x; 1.0854x over previous
import numpy as np
import concourse.bass as bass
import concourse.mybir as mybir

F32 = mybir.dt.float32
BF16 = mybir.dt.bfloat16
AF = mybir.ActivationFunctionType
ALU = mybir.AluOpType

D = 2048
NCH = 16
DS = 128
C = 64
PT = 512
SQK = float(np.sqrt(1.0 / (1.0 + 1e-6)))
LN_05SQK = float(np.log(0.5 * SQK))
EPS_LN = 1e-5
EPS_RMS = 1e-6
B_STAGE = 99


class Prog:
    def __init__(self):
        self.ops = []
        self.count = {}

    def add(self, engine, fn, waits=(), incs=()):
        self.ops.append((engine, fn, tuple(waits), tuple(incs)))
        for sem, amt in incs:
            self.count[sem.name] = self.count.get(sem.name, 0) + amt

    def cur(self, sem):
        return self.count.get(sem.name, 0)

    def emit(self, block):
        ops = self.ops

        def body_for(engine_name):
            def body(eng):
                for e, fn, waits, incs in ops:
                    if e != engine_name:
                        continue
                    for sem, val in waits:
                        eng.wait_ge(sem, val)
                    inst = fn(eng)
                    for sem, amt in incs:
                        inst = inst.then_inc(sem, amt)
            return body

        block.tensor(body_for("tensor"))
        block.vector(body_for("vector"))
        block.scalar(body_for("scalar"))
        block.gpsimd(body_for("gpsimd"))
        block.sync(body_for("sync"))


class Arena:

    def __init__(self, tensor, ncols):
        self.t = tensor
        self.n = ncols
        self.off = 0
        self.marks = []

    def alloc(self, ncols):
        assert self.off + ncols <= self.n, f"arena overflow {self.off}+{ncols}>{self.n}"
        v = self.t[:, self.off:self.off + ncols]
        self.off += ncols
        return v

    def push(self):
        self.marks.append(self.off)

    def pop(self):
        self.off = self.marks.pop()


def host_constants():
    cst = {}
    cst["ones_col"] = np.ones((128, 1), dtype=np.float32)
    cst["twos_col"] = np.full((128, 1), 2.0, dtype=np.float32)
    cst["ones_row"] = np.ones((1, PT), dtype=np.float32)
    cst["eye128"] = np.eye(128, dtype=np.float32)
    rm = np.ones((128, PT), dtype=np.float32)
    rm[:, ::C] = 0.0
    cst["resetmask"] = rm
    up_strict = (np.arange(C)[None, :] > np.arange(C)[:, None]).astype(np.float32)
    up_incl = (np.arange(C)[None, :] >= np.arange(C)[:, None]).astype(np.float32)
    cst["mSTRICT"] = up_strict
    cst["mSTRICTN"] = -up_strict
    cst["mINCL"] = up_incl
    cst["eye64x2"] = np.kron(np.eye(2, dtype=np.float32), np.eye(C, dtype=np.float32))
    return cst


def host_layout(inputs, n_tok, n_cores=8):
    g = {}
    x = np.ascontiguousarray(inputs["L0_attn_x"][0, :n_tok, :].T)
    g["xT"] = x
    my_tok = n_tok // n_cores
    nblk = my_tok // 512
    x5 = x.reshape(16, 128, n_cores, nblk, 512).transpose(0, 3, 2, 1, 4)
    for m_ in range(16):
        g[f"x4d_{m_}"] = np.ascontiguousarray(x5[m_])
    for nm in ("Wk", "Wv", "Wq", "Waf", "Was", "ir_w1"):
        g[nm] = np.ascontiguousarray(inputs["L0_attn_" + nm])
    g["pw_w"] = np.ascontiguousarray(inputs["L0_attn_pw_w"])
    g["nm_w1"] = np.ascontiguousarray(inputs["L0_attn_nm_w1"])
    g["nm_w2"] = np.ascontiguousarray(inputs["L0_attn_nm_w2"])
    g["ir_w2"] = np.ascontiguousarray(inputs["L0_attn_ir_w2"])
    g["Wo"] = np.ascontiguousarray(inputs["L0_attn_Wo"])
    g["wmod"] = np.ascontiguousarray(inputs["L0_wmod"])
    g["ccolT"] = np.ascontiguousarray(inputs["L0_c"].reshape(16, 128).T)
    g["baf"] = inputs["L0_attn_baf"].reshape(128, 1)
    g["bas"] = inputs["L0_attn_bas"].reshape(128, 1)
    g["lamf"] = inputs["L0_attn_fast_lambda"].reshape(128, 1)
    g["lams"] = inputs["L0_attn_slow_lambda"].reshape(128, 1)
    g["nm_b1"] = np.ascontiguousarray(inputs["L0_attn_nm_b1"].reshape(2, 128).T)
    g["nm_b2"] = inputs["L0_attn_nm_b2"].reshape(128, 1)
    g["ir_b1"] = inputs["L0_attn_ir_b1"].reshape(128, 1)
    g["ir_b2"] = inputs["L0_attn_ir_b2"].reshape(1, 1)
    g["pw_b"] = inputs["L0_attn_pw_b"].reshape(1, 3)
    g["norm_w"] = inputs["L0_attn_norm_w"].reshape(128, 1)
    g["gate_wT"] = np.ascontiguousarray(inputs["L0_ffn_gate_w"].T)
    g["W1T"] = np.ascontiguousarray(inputs["L0_ffn_W1"].transpose(0, 2, 1))
    g["W3T"] = np.ascontiguousarray(inputs["L0_ffn_W3"].transpose(0, 2, 1))
    g["W2T"] = np.ascontiguousarray(inputs["L0_ffn_W2"].transpose(0, 2, 1))
    g["Ws1T"] = np.ascontiguousarray(inputs["L0_ffn_Ws1"].T)
    g["Ws3T"] = np.ascontiguousarray(inputs["L0_ffn_Ws3"].T)
    g["Ws2T"] = np.ascontiguousarray(inputs["L0_ffn_Ws2"].T)
    g.update(host_constants())
    return g


class MK:
    def __init__(self, nc, n_tok, n_cores=8, do_scan=True, do_ffn=True, debug_taps=()):
        self.nc = nc
        self.T = n_tok
        self.NPT = self.T // PT
        self.NCK = self.T // C
        self.n_cores = n_cores
        self.do_scan = do_scan
        self.do_ffn = do_ffn
        self.P = Prog()
        self.sems = {}
        self.debug_taps = set(debug_taps)
        self.dbg = {}

    def sem(self, name):
        if name not in self.sems:
            self.sems[name] = self.nc.semaphore(name).__enter__()
        return self.sems[name]

    def tap(self, name, view, shape, after_sem, after_val):
        if name not in self.debug_taps:
            return
        t = self.nc.dram_tensor("dbg_" + name, list(shape), view.dtype, kind="ExternalOutput").ap()
        self.dbg[name] = t
        s = self.sem("dbgs")
        self.P.add("sync", lambda e, t=t, v=view: e.dma_start(t, v),
                   waits=[(after_sem, after_val)], incs=[(s, 16)])

    def declare_io(self):
        nc, P = self.nc, self.P
        g = {}
        g["xT"] = nc.dram_tensor("xT", [D, self.T], F32, kind="ExternalInput").ap()
        my_tok_ = self.T // self.n_cores
        for m_ in range(16):
            g[f"x4d_{m_}"] = nc.dram_tensor(f"x4d_{m_}",
                                            [max(my_tok_ // 512, 1), self.n_cores, 128, 512],
                                            F32, kind="ExternalInput").ap()
        for nm, shp in [("Wk", [D, DS]), ("Wv", [D, DS]), ("Wq", [D, DS]),
                        ("Waf", [D, DS]), ("Was", [D, DS]), ("ir_w1", [D, DS]),
                        ("pw_w", [D, 3]), ("nm_w1", [DS, 256]), ("nm_w2", [256, DS]),
                        ("ir_w2", [DS, 1]), ("Wo", [DS, D]), ("wmod", [D, 6 * D]),
                        ("ccolT", [128, 16]), ("baf", [128, 1]), ("bas", [128, 1]),
                        ("lamf", [128, 1]), ("lams", [128, 1]), ("nm_b1", [128, 2]),
                        ("nm_b2", [128, 1]), ("ir_b1", [128, 1]), ("ir_b2", [1, 1]),
                        ("pw_b", [1, 3]), ("norm_w", [128, 1]),
                        ("gate_wT", [D, 16]), ("W1T", [16, D, 1408]), ("W3T", [16, D, 1408]),
                        ("W2T", [16, 1408, D]), ("Ws1T", [D, 2816]), ("Ws3T", [D, 2816]),
                        ("Ws2T", [2816, D]),
                        ("ones_col", [128, 1]), ("twos_col", [128, 1]), ("ones_row", [1, PT]),
                        ("eye128", [128, 128]), ("resetmask", [128, PT]),
                        ("mSTRICT", [C, C]), ("mSTRICTN", [C, C]), ("mINCL", [C, C]),
                        ("eye64x2", [128, 128])]:
            g[nm] = nc.dram_tensor(nm, shp, F32, kind="ExternalInput").ap()
        NCK = self.NCK
        g["dPF"] = nc.dram_tensor("dPF", [NCK, 128, 192], F32).ap()
        g["dPS"] = nc.dram_tensor("dPS", [NCK, 128, 192], F32).ap()
        g["dV"] = nc.dram_tensor("dV", [NCK, 128, 64], F32).ap()
        g["dVs"] = nc.dram_tensor("dVs", [NCK, 128, 64], F32).ap()
        g["dMIT"] = nc.dram_tensor("dMIT", [NCK, 128, 128], BF16).ap()
        g["dGQT"] = nc.dram_tensor("dGQT", [NCK, 128, C], BF16).ap()
        g["dGOT"] = nc.dram_tensor("dGOT", [NCK, 128, C], BF16).ap()
        g["dUR"] = nc.dram_tensor("dUR", [NCK, 128, 128], BF16).ap()
        g["dNU"] = nc.dram_tensor("dNU", [NCK, 128, C], F32).ap()
        g["dROWS"] = nc.dram_tensor("dROWS", [NCK, 1, 512], F32).ap()
        g["dPC"] = nc.dram_tensor("dPC", [NCK, 128, 2], F32).ap()
        my_tok_d = self.T // self.n_cores
        nblk_d = max(my_tok_d // 512, 1)
        g["dOUTA"] = nc.dram_tensor("dOUTA", [nblk_d, self.n_cores, 128, 512], F32).ap()
        g["dW1b"] = nc.dram_tensor("dW1b", [16, 11, 128, 2048], BF16).ap()
        g["dW3b"] = nc.dram_tensor("dW3b", [16, 11, 128, 2048], BF16).ap()
        g["dW2b"] = nc.dram_tensor("dW2b", [16, 16, 128, 1408], BF16).ap()
        g["dWs1b"] = nc.dram_tensor("dWs1b", [22, 128, 2048], BF16).ap()
        g["dWs3b"] = nc.dram_tensor("dWs3b", [22, 128, 2048], BF16).ap()
        g["dWs2b"] = nc.dram_tensor("dWs2b", [2, 16, 128, 1408], BF16).ap()
        self.g = g
        return g

    def emit_A0(self, ex, ar, ex_psum):
        nc, P, g = self.nc, self.P, self.g
        sb = lambda name, shape, dt=F32: ex.enter_context(nc.sbuf_tensor(name, shape, dt))
        ps = lambda name, shape, dt=F32: ex_psum.enter_context(nc.psum_tensor(name, shape, dt))

        self.wpb = sb("wpb", [128, 6 * NCH * 128], BF16)
        self.wof = sb("wof", [128, D], BF16)
        self.pwc = sb("pwc", [128, 3 * NCH], BF16)
        self.nmw1 = sb("nmw1", [128, 256], BF16)
        self.nmw2 = sb("nmw2", [128, 256], BF16)
        self.irw2 = sb("irw2", [128, 1], BF16)
        self.cols = sb("cols", [128, 160])
        self.mcols = sb("mcols", [128, 96])
        self.rows0 = sb("rows0", [1, 2048])
        self.consts = sb("consts", [128, 8])
        self.onesrow = sb("onesrow", [1, PT])
        self.eyef = sb("eyef", [128, 128])
        self.eyeb = sb("eyeb", [128, 128], BF16)
        self.rmask = sb("rmask", [128, PT])
        self.mstrict = sb("mstrict", [C, C])
        self.mstrictn = sb("mstrictn", [C, C])
        self.mincl = sb("mincl", [C, C])
        self.onescolb = sb("onescolb", [128, 1], BF16)

        sd = self.sem("a0d")
        sc = self.sem("a0c")

        def ld(dst, src):
            P.add("sync", lambda e, d=dst, s=src: e.dma_start(d, s), incs=[(sd, 16)])

        ar.push()
        wtmp = ar.alloc(2048 + 64)[:, 0:2048]
        wtmp = ar.t[:, ar.marks[-1]:ar.marks[-1] + 2048]
        self.wtmp = wtmp
        ld(self.consts[:, 0:1], g["ones_col"])
        ld(self.consts[:, 1:2], g["twos_col"])
        ld(self.onesrow[:, :], g["ones_row"])
        ld(self.eyef[:, :], g["eye128"])
        ld(self.rmask[:, :], g["resetmask"])
        ld(self.mstrict[:, :], g["mSTRICT"])
        ld(self.mstrictn[:, :], g["mSTRICTN"])
        ld(self.mincl[:, :], g["mINCL"])
        ld(self.cols[:, 0:1], g["baf"]); ld(self.cols[:, 1:2], g["bas"])
        ld(self.cols[:, 2:3], g["lamf"]); ld(self.cols[:, 3:4], g["lams"])
        ld(self.cols[:, 6:8], g["nm_b1"]); ld(self.cols[:, 8:9], g["nm_b2"])
        ld(self.cols[:, 9:10], g["ir_b1"]); ld(self.cols[:, 10:11], g["norm_w"])
        ld(self.rows0[0:1, 0:1], g["ir_b2"])
        ld(self.rows0[0:1, 4:7], g["pw_b"])
        ld(self.cols[:, 12:28], g["ccolT"])
        w0 = P.cur(sd)

        P.add("vector", lambda e: e.memset(self.cols[:, 60:61], 1e-8), incs=[(sc, 1)])
        P.add("vector", lambda e: e.memset(self.cols[:, 61:62], EPS_LN), incs=[(sc, 1)])
        P.add("vector", lambda e: e.memset(self.cols[:, 62:63], EPS_RMS), incs=[(sc, 1)])
        P.add("vector", lambda e: e.memset(self.rows0[0:1, 30:31], EPS_LN), incs=[(sc, 1)])
        P.add("vector", lambda e: e.memset(self.rows0[0:1, 31:32], EPS_RMS), incs=[(sc, 1)])
        P.add("vector", lambda e: e.tensor_copy(self.eyeb[:, :], self.eyef[:, :]),
              waits=[(sd, 4 * 16)], incs=[(sc, 1)])
        P.add("vector", lambda e: e.tensor_copy(self.onescolb[:, :], self.consts[:, 0:1]),
              waits=[(sd, 16)], incs=[(sc, 1)])
        P.add("scalar", lambda e: e.activation(self.cols[:, 100:101], self.cols[:, 2:3], AF.Sigmoid),
              waits=[(sd, w0)], incs=[(sc, 1)])
        P.add("scalar", lambda e: e.activation(self.cols[:, 102:103], self.cols[:, 100:101], AF.Ln, bias=self.cols[:, 60:61]),
              incs=[(sc, 1)])
        P.add("vector", lambda e: e.tensor_scalar(self.cols[:, 4:5], self.cols[:, 102:103], 8.0, None, ALU.mult),
              waits=[(sc, P.cur(sc))], incs=[(sc, 1)])
        P.add("vector", lambda e: e.tensor_scalar(self.cols[:, 105:106], self.cols[:, 3:4], -1.0, None, ALU.mult),
              incs=[(sc, 1)])
        P.add("scalar", lambda e: e.activation(self.cols[:, 103:104], self.cols[:, 105:106], AF.Exp),
              waits=[(sc, P.cur(sc))], incs=[(sc, 1)])
        P.add("vector", lambda e: e.tensor_scalar(self.cols[:, 104:105], self.cols[:, 103:104], -0.5, 1.0,
                                                  ALU.mult, ALU.add),
              waits=[(sc, P.cur(sc))], incs=[(sc, 1)])
        P.add("vector", lambda e: e.tensor_tensor(self.cols[:, 104:105], self.cols[:, 103:104],
                                                  self.cols[:, 104:105], ALU.mult),
              waits=[(sc, P.cur(sc))], incs=[(sc, 1)])
        P.add("vector", lambda e: e.tensor_scalar(self.cols[:, 5:6], self.cols[:, 104:105], -8.0, 8e-8,
                                                  ALU.mult, ALU.add),
              waits=[(sc, P.cur(sc))], incs=[(sc, 1)])
        P.add("vector", lambda e: e.tensor_scalar(self.cols[:, 106:107], self.cols[:, 8:9], 0.1, None, ALU.mult),
              incs=[(sc, 1)])
        P.add("scalar", lambda e: e.activation(self.cols[:, 28:44], self.cols[:, 12:28], AF.Silu),
              incs=[(sc, 1)])
        sil_done = P.cur(sc)

        pmod = [ps("pmod0", [128, 96]), ps("pmod1", [128, 96])]
        smod = self.sem("a0m")
        wslab = [ar.alloc(6 * D), ar.alloc(6 * D)]
        P.add("vector", lambda e: e.memset(self.mcols[:, :], 0.0), incs=[(smod, 1)])
        add_done = {}
        for dc in range(NCH):
            par = dc % 2
            P.add("sync", lambda e, dc=dc, par=par: e.dma_start(
                wslab[par][:, :], g["wmod"][dc * 128:(dc + 1) * 128, :]),
                waits=[(smod, max(P.cur(smod) - 2, 0))], incs=[(sd, 16)])
            wslab_ready = P.cur(sd)
            for jj in range(96):
                P.add("tensor",
                      lambda e, par=par, dc=dc, jj=jj: e.matmul(
                          pmod[par][:, jj:jj + 1], wslab[par][:, jj * 128:(jj + 1) * 128],
                          self.cols[:, 28 + dc:29 + dc], start=True, stop=True),
                      waits=([(sd, wslab_ready), (sc, sil_done),
                              (smod, add_done.get(dc - 2, 0))] if jj == 0 else []),
                      incs=([(smod, 1)] if jj == 95 else []))
            P.add("vector", lambda e, par=par: e.tensor_tensor(
                self.mcols[:, :], self.mcols[:, :], pmod[par][:, :], ALU.add),
                  waits=[(smod, P.cur(smod))], incs=[(smod, 1)])
            add_done[dc] = P.cur(smod)
        mod_done = P.cur(smod)
        for base in (32, 80):
            P.add("scalar", lambda e, b=base: e.activation(self.cols[:, 44:60], self.mcols[:, b:b + 16], AF.Sign),
                  waits=[(sc, P.cur(sc)), (smod, mod_done)], incs=[(sc, 1)])
            P.add("scalar", lambda e, b=base: e.activation(self.mcols[:, b:b + 16], self.mcols[:, b:b + 16],
                                                           AF.Abs),
                  waits=[(sc, P.cur(sc))], incs=[(sc, 1)])
            P.add("vector", lambda e, b=base: e.tensor_scalar(self.mcols[:, b:b + 16], self.mcols[:, b:b + 16],
                                                              0.25, None, ALU.max),
                  waits=[(sc, P.cur(sc))], incs=[(sc, 1)])
            P.add("vector", lambda e, b=base: e.tensor_tensor(self.mcols[:, b:b + 16], self.mcols[:, b:b + 16],
                                                              self.cols[:, 44:60], ALU.mult),
                  waits=[(sc, P.cur(sc))], incs=[(sc, 1)])
        P.add("vector", lambda e: e.tensor_scalar(self.mcols[:, 16:32], self.mcols[:, 16:32], 1.0, None, ALU.add),
              incs=[(sc, 1)])
        P.add("vector", lambda e: e.tensor_scalar(self.mcols[:, 64:80], self.mcols[:, 64:80], 1.0, None, ALU.add),
              incs=[(sc, 1)])
        a0_mod_done = P.cur(sc)

        pu = ps("pu", [128, 16])
        su = self.sem("a0u")
        wnames = ["Wk", "Wv", "Wq", "Waf", "Was", "ir_w1"]
        for wi, wn in enumerate(wnames):
            P.add("sync", lambda e, wn=wn: e.dma_start(self.wtmp.rearrange("p (c o) -> p c o", c=NCH), g[wn].rearrange("(c p) o -> p c o", p=128)),
                  waits=[(su, P.cur(su))], incs=[(sd, 16)])
            wrdy = P.cur(sd)
            for ch in range(NCH):
                P.add("tensor", lambda e, wi=wi, ch=ch: e.matmul(
                    pu[:, wi + 8:wi + 9], self.wtmp[:, ch * 128:(ch + 1) * 128],
                    self.mcols[:, ch:ch + 1], start=(ch == 0), stop=(ch == NCH - 1)),
                    waits=([(sd, wrdy), (sc, a0_mod_done)] if ch == 0 else []),
                    incs=([(su, 1)] if ch == NCH - 1 else []))
            for ch in range(NCH):
                P.add("vector", lambda e, wi=wi, ch=ch: e.tensor_scalar(
                    self.wpb[:, (wi * NCH + ch) * 128:(wi * NCH + ch + 1) * 128],
                    self.wtmp[:, ch * 128:(ch + 1) * 128],
                    self.mcols[:, 16 + ch:17 + ch], None, ALU.mult),
                    waits=([(su, P.cur(su))] if ch == 0 else []),
                    incs=([(su, 1)] if ch == NCH - 1 else []))
            wfold = P.cur(su)
            for ch in range(NCH):
                P.add("tensor", lambda e, wi=wi, ch=ch: e.matmul(
                    pu[:, wi:wi + 1],
                    self.wpb[:, (wi * NCH + ch) * 128:(wi * NCH + ch + 1) * 128],
                    self.onescolb[:, :], start=(ch == 0), stop=(ch == NCH - 1)),
                    waits=([(su, wfold)] if ch == 0 else []),
                    incs=([(su, 1)] if ch == NCH - 1 else []))
        pwtmp = sb("pwtmp", [128, 3 * NCH])
        P.add("sync", lambda e: e.dma_start(pwtmp.rearrange("p (c o) -> p c o", c=NCH), g["pw_w"].rearrange("(c p) o -> p c o", p=128)),
              incs=[(sd, 16)])
        pwrdy = P.cur(sd)
        for ch in range(NCH):
            P.add("tensor", lambda e, ch=ch: e.matmul(
                pu[0:3, 14:15],
                pwtmp[:, ch * 3:(ch + 1) * 3], self.mcols[:, ch:ch + 1],
                start=(ch == 0), stop=(ch == NCH - 1)),
                waits=([(sd, pwrdy), (sc, a0_mod_done)] if ch == 0 else []),
                incs=([(su, 1)] if ch == NCH - 1 else []))
        for ch in range(NCH):
            P.add("vector", lambda e, ch=ch: e.tensor_scalar(
                self.pwc[:, ch * 3:(ch + 1) * 3], pwtmp[:, ch * 3:(ch + 1) * 3],
                self.mcols[:, 16 + ch:17 + ch], None, ALU.mult),
                incs=([(su, 1)] if ch == NCH - 1 else []))
        pwfold = P.cur(su)
        for ch in range(NCH):
            P.add("tensor", lambda e, ch=ch: e.matmul(
                pu[0:3, 15:16], self.pwc[:, ch * 3:(ch + 1) * 3], self.onescolb[:, :],
                start=(ch == 0), stop=(ch == NCH - 1)),
                waits=([(su, pwfold)] if ch == 0 else []),
                incs=([(su, 1)] if ch == NCH - 1 else []))
        allu = P.cur(su)

        ptr = ps("ptr", [1, 512])
        str_ = self.sem("a0t")
        P.add("vector", lambda e: e.tensor_copy(self.cols[:, 64:80], pu[:, :]),
              waits=[(su, allu)], incs=[(str_, 1)])
        pu_sb = self.cols[:, 64:80]
        for slot, srcc, dst0, w in [(0, 0, 128, 128), (1, 1, 256, 128), (2, 2, 384, 128),
                                    (3, 3, 512, 128), (4, 4, 640, 128), (5, 5, 768, 128),
                                    (6, 8, 896, 128), (7, 9, 1024, 128), (8, 10, 1152, 128),
                                    (9, 11, 1280, 128), (10, 12, 1408, 128), (11, 13, 1536, 128),
                                    (12, 15, 16, 3), (13, 14, 20, 3)]:
            P.add("tensor", lambda e, srcc=srcc: e.transpose(ptr[0:1, 0:128], pu_sb[:, srcc:srcc + 1],
                                                             self.eyef[:, :]),
                  waits=[(str_, P.cur(str_))], incs=[(str_, 1)])
            P.add("vector", lambda e, dst0=dst0, w=w: e.tensor_copy(self.rows0[0:1, dst0:dst0 + w],
                                                                    ptr[0:1, 0:w]),
                  waits=[(str_, P.cur(str_))], incs=[(str_, 1)])

        P.add("sync", lambda e: e.dma_start(self.wtmp[:, 0:256], g["nm_w1"]),
              waits=[(su, P.cur(su)), (str_, P.cur(str_))], incs=[(sd, 16)])
        P.add("sync", lambda e: e.dma_start(self.wtmp[:, 256:512].rearrange("p (c o) -> p c o", c=2),
                                            g["nm_w2"].rearrange("(c p) o -> p c o", p=128)), incs=[(sd, 16)])
        P.add("sync", lambda e: e.dma_start(self.wtmp[:, 512:513], g["ir_w2"]), incs=[(sd, 16)])
        nmr = P.cur(sd)
        P.add("vector", lambda e: e.tensor_copy(self.nmw1[:, :], self.wtmp[:, 0:256]),
              waits=[(sd, nmr)], incs=[(sc, 1)])
        P.add("vector", lambda e: e.tensor_copy(self.nmw2[:, :], self.wtmp[:, 256:512]), incs=[(sc, 1)])
        P.add("vector", lambda e: e.tensor_copy(self.irw2[:, :], self.wtmp[:, 512:513]), incs=[(sc, 1)])
        P.add("sync", lambda e: e.dma_start(self.wtmp[:, 0:2048], g["Wo"]),
              waits=[(sc, P.cur(sc))], incs=[(sd, 16)])
        wordy = P.cur(sd)
        P.add("vector", lambda e: e.tensor_scalar(self.wof[:, :], self.wtmp[:, 0:2048],
                                                  self.cols[:, 10:11], None, ALU.mult),
              waits=[(sd, wordy)], incs=[(sc, 1)])
        self.a0_waits = [(sc, P.cur(sc)), (su, P.cur(su)), (str_, P.cur(str_)),
                         (sd, P.cur(sd)), (smod, P.cur(smod))]

    def esem(self, name):
        return self.sem("E_" + name)

    def op(self, eng, fn, deps=(), extra_waits=(), inc=None):
        P = self.P
        waits = [(self.esem(d), P.cur(self.esem(d))) for d in deps] + list(extra_waits)
        amt = inc if inc is not None else (16 if eng == "sync" else 1)
        P.add(eng, fn, waits=waits, incs=[(self.esem(eng), amt)])

    def barrier_waits(self):
        return [(self.esem(d), self.P.cur(self.esem(d)))
                for d in ("tensor", "vector", "scalar", "gpsimd", "sync")]

    def emit_A1(self, ex, ar, arb):
        nc, P, g = self.nc, self.P, self.g
        ps = lambda name, shape, dt=F32: ex.enter_context(nc.psum_tensor(name, shape, dt))
        op = self.op

        xf = ar.alloc(NCH * PT)
        k_n = ar.alloc(PT); q_n = ar.alloc(PT); vv = ar.alloc(PT)
        e_rf = ar.alloc(PT); e_rs = ar.alloc(PT); e_ir = ar.alloc(PT)
        rbc = ar.alloc(PT)
        nu = ar.alloc(PT)
        TP = [ar.alloc(PT) for _ in range(9)]
        S = {}
        for st in ("f", "s"):
            S[st] = dict(cp=ar.alloc(PT), U=ar.alloc(PT), Pp=ar.alloc(PT),
                         la=TP[0], LA=TP[1], LX=TP[2], E=TP[3], En=TP[4], Es=TP[5],
                         af=TP[6], c=TP[7], Em=TP[8])
        AQK = {"f": ar.alloc(3 * PT), "s": ar.alloc(3 * PT)}
        gevacB16 = arb.alloc(768)
        uevac = ar.alloc(256)
        uevacB = ar.alloc(256)
        rowsA = ar.alloc(2048)
        R_m, R_sd, R_rstd, R_negm, R_pw0, R_pw1, R_pw2, R_imp, R_g, R_a2f, R_a2s, R_sc0, R_sc1 = (
            rowsA[0:1, i * 128:(i + 1) * 128][:, 0:PT] if False else rowsA[0:1, i * PT // 4:(i + 1) * PT // 4]
            for i in range(13))
        R_m = rowsA[0:1, 0:PT]; R_sd = rowsA[0:1, PT:2 * PT]
        R_rstd = rowsA[0:1, 2 * PT:3 * PT]; R_negm = rowsA[0:1, 3 * PT:4 * PT]
        rowsB = ar.alloc(2048)
        R_pw0 = rowsB[0:1, 0:PT]; R_pw1 = rowsB[0:1, PT:2 * PT]; R_pw2 = rowsB[0:1, 2 * PT:3 * PT]
        R_imp = rowsB[0:1, 3 * PT:4 * PT]
        rowsC = ar.alloc(1024)
        R_sumk = rowsC[0:1, 0:PT]; R_sumq = rowsC[0:1, PT:2 * PT]
        rowsD = ar.alloc(512)
        R_a2f = rowsD[0:1, 0:C]; R_a2s = rowsD[0:1, C:2 * C]
        R_a2fB = rowsD[0:1, 2 * C:3 * C]; R_a2sB = rowsD[0:1, 3 * C:4 * C]
        self.sync_after = {}

        xb = arb.alloc(NCH * PT)
        scrb = arb.alloc(2 * PT)
        kb = arb.alloc(PT); qb = arb.alloc(PT); vb = arb.alloc(PT); girb = arb.alloc(PT)
        LL = arb.alloc(128); Llo = arb.alloc(128)
        POW = arb.alloc(512); Pq = arb.alloc(512)
        Pu = arb.alloc(128); Plo = arb.alloc(128); Fb = arb.alloc(128)
        GQTstack = arb.alloc(2 * C); GOTstack = arb.alloc(2 * C)
        URstack = arb.alloc(256)
        Ubf = arb.alloc(PT); Ubs = arb.alloc(PT)
        S_ub = {"f": Ubf, "s": Ubs}

        pA = ps("pA", [128, PT])
        pB = ps("pB", [128, PT])
        pRa = ps("pRa", [1, PT])
        pRb = ps("pRb", [1, PT])
        pC = ps("pC", [128, PT])
        pG = ps("pG", [C, 192])
        pTB = ex.enter_context(nc.psum_tensor("pTB", [128, 128], BF16))
        pW = ps("pW", [1, PT])

        NPT_ = self.NPT
        wb = self.wpb

        first_tile_waits = self.barrier_waits() + self.a0_waits

        for j in range(NPT_):
            tw = first_tile_waits if j == 0 else None
            for ch in range(NCH):
                P.add("sync", lambda e, ch=ch, j=j: e.dma_start(
                    xf[:, ch * PT:(ch + 1) * PT], g["xT"][ch * 128:(ch + 1) * 128, j * PT:(j + 1) * PT]),
                    waits=((tw if tw is not None else self.barrier_waits()) if ch == 0 else []),
                    incs=[(self.esem("sync"), 16)])
            for ch in range(NCH):
                op("vector", lambda e, ch=ch: e.tensor_copy(xb[:, ch * PT:(ch + 1) * PT],
                                                            xf[:, ch * PT:(ch + 1) * PT]),
                   deps=("sync",) if ch == 0 else ())
            cast_done = P.cur(self.esem("vector"))
            for ch in range(NCH):
                op("tensor", lambda e, ch=ch: e.matmul(
                    pRa[0:1, :], self.onescolb[:, :], xb[:, ch * PT:(ch + 1) * PT],
                    start=(ch == 0), stop=(ch == NCH - 1)), deps=("vector",) if ch == 0 else ())
            for ch in range(NCH):
                op("gpsimd", lambda e, ch=ch: e.tensor_tensor(
                    scrb[:, (ch % 2) * PT:(ch % 2 + 1) * PT], xb[:, ch * PT:(ch + 1) * PT],
                    xb[:, ch * PT:(ch + 1) * PT], ALU.mult),
                   deps=("vector", "tensor") if ch == 0 else (("tensor",) if ch >= 2 else ()))
                op("tensor", lambda e, ch=ch: e.matmul(
                    pRb[0:1, :], self.onescolb[:, :], scrb[:, (ch % 2) * PT:(ch % 2 + 1) * PT],
                    start=(ch == 0), stop=(ch == NCH - 1)), deps=("gpsimd",))
            op("vector", lambda e: e.tensor_scalar(R_m, pRa[0:1, :], 1.0 / D, None, ALU.mult), deps=("tensor",))
            op("vector", lambda e: e.tensor_scalar(R_sd, pRb[0:1, :], 1.0 / D, None, ALU.mult))
            op("vector", lambda e: e.tensor_tensor(R_negm, R_m, R_m, ALU.mult), deps=("vector",))
            op("vector", lambda e: e.tensor_tensor(R_sd, R_sd, R_negm, ALU.subtract), deps=("vector",))
            op("scalar", lambda e: e.activation(R_sd, R_sd, AF.Sqrt, bias=self.rows0[0:1, 30:31]), deps=("vector",))
            op("vector", lambda e: e.reciprocal(R_rstd, R_sd), deps=("scalar",))
            op("vector", lambda e: e.tensor_scalar(R_negm, R_m, -1.0, None, ALU.mult))
            op("tensor", lambda e: e.matmul(pC[:, :], self.onesrow[0:1, 0:128], R_rstd,
                                            start=True, stop=True), deps=("vector",))
            op("vector", lambda e: e.tensor_copy(rbc[:, :], pC[:, :]), deps=("tensor",))
            rbc_done = P.cur(self.esem("vector"))

            projs = [("k", 0, None), ("v", 1, vv), ("q", 2, None), ("rf", 3, e_rf),
                     ("rs", 4, e_rs), ("ir1", 5, e_ir)]
            for pi, (pname, wi, dst) in enumerate(projs):
                pp = pA if pi % 2 == 0 else pB
                for ch in range(NCH):
                    op("tensor", lambda e, wi=wi, ch=ch, pp=pp: e.matmul(
                        pp[:, :], wb[:, (wi * NCH + ch) * 128:(wi * NCH + ch + 1) * 128],
                        xb[:, ch * PT:(ch + 1) * PT], start=(ch == 0), stop=False),
                       deps=("vector",) if (pi < 2 and ch == 0) else ())
                op("tensor", lambda e, wi=wi, pp=pp: e.matmul(
                    pp[:, :], self.rows0[0:1, 128 * (1 + wi):128 * (2 + wi)], R_negm,
                    start=False, stop=False))
                op("tensor", lambda e, wi=wi, pp=pp: e.matmul(
                    pp[:, :], self.rows0[0:1, 128 * (7 + wi):128 * (8 + wi)], R_sd,
                    start=False, stop=True))
                tgt = {"k": k_n, "q": q_n}.get(pname, dst)
                op("vector", lambda e, pp=pp, tgt=tgt: e.tensor_tensor(tgt, pp[:, :], rbc, ALU.mult),
                   deps=("tensor",))
            for m_ in range(3):
                for ch in range(NCH):
                    op("tensor", lambda e, m_=m_, ch=ch: e.matmul(
                        pW[0:1, :], self.pwc[:, ch * 3 + m_:ch * 3 + m_ + 1], xb[:, ch * PT:(ch + 1) * PT],
                        start=(ch == 0), stop=False), deps=("vector",) if ch == 0 else ())
                op("tensor", lambda e, m_=m_: e.matmul(
                    pW[0:1, :], self.rows0[0:1, 16 + m_:17 + m_], R_negm, start=False, stop=False))
                op("tensor", lambda e, m_=m_: e.matmul(
                    pW[0:1, :], self.rows0[0:1, 20 + m_:21 + m_], R_sd, start=False, stop=True))
                dstr = (R_pw0, R_pw1, R_pw2)[m_]
                op("vector", lambda e, dstr=dstr: e.tensor_tensor(dstr, pW[0:1, :], R_rstd, ALU.mult),
                   deps=("tensor",))
                op("scalar", lambda e, dstr=dstr, m_=m_: e.activation(
                    dstr, dstr, AF.Exp, bias=self.rows0[0:1, 4 + m_:5 + m_]), deps=("vector",))
            op("vector", lambda e: e.tensor_tensor(R_sumk, R_pw0, R_pw1, ALU.add), deps=("scalar",))
            op("vector", lambda e: e.tensor_tensor(R_sumk, R_sumk, R_pw2, ALU.add), deps=("vector",))
            op("vector", lambda e: e.reciprocal(R_sumq, R_sumk), deps=("vector",))
            first = True
            for dstr in (R_pw0, R_pw1, R_pw2):
                op("vector", lambda e, dstr=dstr: e.tensor_tensor(dstr, dstr, R_sumq, ALU.mult),
                   deps=("vector",) if first else ())
                first = False
            pw_done = P.cur(self.esem("vector"))

            for li, (src, srow) in enumerate(((k_n, R_sumk), (q_n, R_sumq))):
                pRx = pRa if li == 0 else pRb
                op("gpsimd", lambda e, src=src: e.tensor_tensor(xf[:, 0:PT], src, src, ALU.mult),
                   deps=("vector",))
                op("tensor", lambda e, pRx=pRx: e.matmul(pRx[0:1, :], self.consts[:, 0:1], xf[:, 0:PT],
                                                         start=True, stop=True), deps=("gpsimd", "vector"))
                op("scalar", lambda e, srow=srow, pRx=pRx: e.activation(srow, pRx[0:1, :], AF.Sqrt),
                   deps=("tensor",))
                op("vector", lambda e, srow=srow: e.tensor_scalar(srow, srow, 1e-12, None, ALU.max),
                   deps=("scalar",))
                op("vector", lambda e, srow=srow: e.reciprocal(srow, srow), deps=("vector",))
                op("tensor", lambda e, srow=srow: e.matmul(pC[:, :], self.onesrow[0:1, 0:128], srow,
                                                           start=True, stop=True), deps=("vector",))
                op("vector", lambda e, src=src: e.tensor_tensor(src, src, pC[:, :], ALU.mult),
                   deps=("tensor",))
            kq_done = P.cur(self.esem("vector"))
            op("vector", lambda e: e.tensor_copy(vb[:, :], vv))
            op("vector", lambda e: e.tensor_copy(kb[:, :], k_n))
            op("vector", lambda e: e.tensor_copy(qb[:, :], q_n))

            for h_ in range(2):
                op("tensor", lambda e, h_=h_: e.matmul(pA[:, :], self.nmw1[:, h_ * 128:(h_ + 1) * 128],
                                                       vb[:, :], start=True, stop=True),
                   deps=("vector",) if h_ == 0 else ("scalar",))
                op("scalar", lambda e, h_=h_: e.activation(scrb[:, h_ * PT:(h_ + 1) * PT],
                                                           pA[:, :], AF.Gelu,
                                                           bias=self.cols[:, 6 + h_:7 + h_]), deps=("tensor",))
            for h_ in range(2):
                op("tensor", lambda e, h_=h_: e.matmul(pB[:, :], self.nmw2[:, h_ * 128:(h_ + 1) * 128],
                                                       scrb[:, h_ * PT:(h_ + 1) * PT],
                                                       start=(h_ == 0), stop=(h_ == 1)),
                   deps=("scalar", "vector") if h_ == 0 else ())
            op("scalar", lambda e: e.activation(nu[:, 0:PT] if False else nu, pB[:, :], AF.Identity,
                                                bias=self.cols[:, 106:107], scale=0.1), deps=("tensor",))

            op("scalar", lambda e: e.activation(girb[:, :], e_ir, AF.Gelu, bias=self.cols[:, 9:10]),
               deps=("vector",))
            op("tensor", lambda e: e.matmul(pW[0:1, :], self.irw2[:, :], girb[:, :], start=True, stop=True),
               deps=("scalar", "vector"))
            op("scalar", lambda e: e.activation(R_imp, pW[0:1, :], AF.Sigmoid,
                                                bias=self.rows0[0:1, 0:1]), deps=("tensor",))
            op("vector", lambda e: e.tensor_scalar(R_imp, R_imp, 0.5, None, ALU.is_gt), deps=("scalar",))
            op("tensor", lambda e: e.matmul(pC[:, :], self.onesrow[0:1, 0:128], R_imp,
                                            start=True, stop=True), deps=("vector",))
            op("vector", lambda e: e.tensor_copy(xf[:, PT:2 * PT], pC[:, :]), deps=("tensor",))

            for st, e_r, bcol, lacol in (("f", e_rf, 0, 4), ("s", e_rs, 1, 5)):
                ss = S[st]
                op("scalar", lambda e, e_r=e_r, bcol=bcol, ss=ss: e.activation(
                    ss["af"], e_r, AF.Sigmoid, bias=self.cols[:, bcol:bcol + 1]), deps=("vector",))
                op("vector", lambda e, ss=ss, lacol=lacol: e.tensor_scalar(
                    ss["la"], ss["af"], self.cols[:, lacol:lacol + 1], None, ALU.mult), deps=("scalar",))
                op("vector", lambda e, ss=ss: e.tensor_tensor_scan(
                    ss["LA"], self.rmask[:, :], ss["la"], 0.0, ALU.mult, ALU.add), deps=("vector",))
                op("vector", lambda e, ss=ss: e.tensor_tensor(ss["LX"], ss["LA"], ss["la"], ALU.subtract),
                   deps=("vector",))
                op("scalar", lambda e, ss=ss: e.activation(ss["E"], ss["LX"], AF.Exp), deps=("vector",))
                op("scalar", lambda e, ss=ss: e.activation(ss["Pp"], ss["LA"], AF.Exp))
                op("scalar", lambda e, ss=ss: e.activation(ss["af"], ss["la"], AF.Exp))
                op("scalar", lambda e, ss=ss: e.activation(ss["Em"], ss["LA"], AF.Exp, scale=-1.0))
                op("vector", lambda e, ss=ss: e.tensor_scalar(ss["En"], ss["E"], -1.0, None, ALU.mult),
                   deps=("scalar",))
                op("vector", lambda e, ss=ss: e.tensor_scalar(ss["Es"], ss["E"], -0.5 * SQK, None, ALU.mult))
                op("vector", lambda e, ss=ss: e.tensor_scalar(ss["c"], ss["af"], -1.0, 1.0, ALU.mult, ALU.add))
                op("vector", lambda e, ss=ss: e.tensor_tensor(ss["c"], ss["c"], k_n, ALU.mult), deps=("vector",))
                if st == "s":
                    op("vector", lambda e, ss=ss: e.tensor_tensor(ss["c"], ss["c"], xf[:, PT:2 * PT], ALU.mult),
                       deps=("vector",))
                op("vector", lambda e, ss=ss: e.tensor_tensor(ss["cp"], ss["c"], ss["Em"], ALU.mult),
                   deps=("vector",))
                lac = ss["LA"].rearrange("p (ck t) -> p ck t", t=C)[:, :, C - 1:C]
                lacb = lac.broadcast_to([128, PT // C, C])
                op("vector", lambda e, ss=ss, lacb=lacb: e.tensor_tensor(
                    ss["LX"].rearrange("p (ck t) -> p ck t", t=C), lacb,
                    ss["LA"].rearrange("p (ck t) -> p ck t", t=C), ALU.subtract))
                op("scalar", lambda e, ss=ss: e.activation(ss["LX"], ss["LX"], AF.Exp), deps=("vector",))
                op("vector", lambda e, ss=ss, st=st: e.tensor_tensor(S_ub[st][:, :], ss["c"], ss["LX"],
                                                                     ALU.mult),
                   deps=("scalar", "vector"))
                A = AQK[st]
                o_kn = 0 if st == "f" else PT
                o_qp = PT if st == "f" else 2 * PT
                o_qs = 2 * PT if st == "f" else 0
                op("vector", lambda e, ss=ss, A=A, o=o_qs: e.tensor_tensor(A[:, o:o + PT], q_n, ss["Es"], ALU.mult),
                   deps=("vector",))
                op("vector", lambda e, ss=ss, A=A, o=o_kn: e.tensor_tensor(A[:, o:o + PT], k_n, ss["En"], ALU.mult))
                op("vector", lambda e, ss=ss, A=A, o=o_qp: e.tensor_tensor(A[:, o:o + PT], q_n, ss["Pp"], ALU.mult))
            prep_done = P.cur(self.esem("vector"))

            if j == 0:
                op("vector", lambda e: e.memset(LL[:, :], 0.0), deps=("vector",))
            for ck in range(PT // C):
                gck = j * (PT // C) + ck
                par = gck % 2
                scr_wait = [(self.esem("sync"), self.sync_after.get(gck - 2, 0))]
                cpf_ck = S["f"]["cp"][:, ck * C:(ck + 1) * C]
                cps_ck = S["s"]["cp"][:, ck * C:(ck + 1) * C]
                rhsf = AQK["f"].rearrange("p (s t) -> p s t", s=3)[:, :, ck * C:(ck + 1) * C]
                rhss = AQK["s"].rearrange("p (s t) -> p s t", s=3)[:, :, ck * C:(ck + 1) * C]
                GQTst = GQTstack[:, par * C:(par + 1) * C]
                GOTst = GOTstack[:, par * C:(par + 1) * C]
                URst = URstack[:, par * 128:(par + 1) * 128]
                op("tensor", lambda e, cpf_ck=cpf_ck, rhsf=rhsf: e.matmul(pG[:, :], cpf_ck, rhsf,
                                                                          start=True, stop=True),
                   deps=("vector",))
                op("vector", lambda e: e.tensor_tensor(LL[0:C, 0:C], pG[:, 0:C], self.mstrictn[:, :],
                                                       ALU.mult), deps=("tensor",), extra_waits=scr_wait)
                op("vector", lambda e, GOTst=GOTst: e.tensor_tensor(GOTst[0:C, :], pG[:, C:2 * C],
                                                                    self.mincl[:, :], ALU.mult))
                op("vector", lambda e, GQTst=GQTst: e.tensor_tensor(GQTst[0:C, :], pG[:, 2 * C:3 * C],
                                                                    self.mstrict[:, :], ALU.mult))
                op("tensor", lambda e, cps_ck=cps_ck, rhss=rhss: e.matmul(pG[:, :], cps_ck, rhss,
                                                                          start=True, stop=True),
                   deps=("vector",))
                op("vector", lambda e: e.tensor_tensor(LL[C:2 * C, C:2 * C], pG[:, 0:C],
                                                       self.mstrictn[:, :], ALU.mult), deps=("tensor",))
                op("vector", lambda e, GOTst=GOTst: e.tensor_tensor(GOTst[C:2 * C, :], pG[:, C:2 * C],
                                                                    self.mincl[:, :], ALU.mult))
                op("vector", lambda e, GQTst=GQTst: e.tensor_tensor(GQTst[C:2 * C, :], pG[:, 2 * C:3 * C],
                                                                    self.mstrict[:, :], ALU.mult))
                P.add("sync", lambda e, gck=gck, GQTst=GQTst: e.dma_start(g["dGQT"][gck], GQTst),
                      waits=[(self.esem("vector"), P.cur(self.esem("vector")))],
                      incs=[(self.esem("sync"), 16)])
                P.add("sync", lambda e, gck=gck, GOTst=GOTst: e.dma_start(g["dGOT"][gck], GOTst),
                      incs=[(self.esem("sync"), 16)])
                for st, half in (("f", 0), ("s", 1)):
                    useg = S_ub[st][:, ck * C:(ck + 1) * C]
                    op("tensor", lambda e, useg=useg: e.transpose(pTB[0:C, :], useg, self.eyeb[:, :]),
                       deps=("vector",))
                    op("vector", lambda e, URst=URst, half=half: e.tensor_copy(
                        URst[half * C:(half + 1) * C, :], pTB[0:C, :]), deps=("tensor",))
                P.add("sync", lambda e, gck=gck, URst=URst: e.dma_start(g["dUR"][gck], URst),
                      waits=[(self.esem("vector"), P.cur(self.esem("vector")))],
                      incs=[(self.esem("sync"), 16)])
                op("tensor", lambda e: e.transpose(pTB[:, :], LL[:, :], self.eyeb[:, :]),
                   deps=("vector",))
                op("vector", lambda e: e.tensor_copy(Llo[:, :], pTB[:, :]), deps=("tensor",))
                op("vector", lambda e: e.scalar_tensor_tensor(Pu[:, :], LL[:, :], -1.0, self.eyeb[:, :],
                                                              ALU.mult, ALU.add), deps=("vector",))
                op("vector", lambda e: e.scalar_tensor_tensor(Plo[:, :], Llo[:, :], -1.0, self.eyeb[:, :],
                                                              ALU.mult, ALU.add))
                pu, plo = LL, Llo
                cu, clo = Pu, Plo
                for gi, k in enumerate((2, 4, 8, 16, 32)):
                    last = (k == 32)
                    npu = POW[:, (gi % 2) * 256:(gi % 2) * 256 + 128]
                    nplo = POW[:, (gi % 2) * 256 + 128:(gi % 2) * 256 + 256]
                    op("tensor", lambda e, pu=pu, plo=plo: e.matmul(pA[:, 0:128], plo, pu,
                                                                    start=True, stop=True),
                       deps=("vector",))
                    op("tensor", lambda e, pu=pu, plo=plo: e.matmul(pA[:, 128:256], pu, plo,
                                                                    start=True, stop=True))
                    op("vector", lambda e, npu=npu: e.tensor_copy(npu, pA[:, 0:128]),
                       deps=("tensor",))
                    op("vector", lambda e, nplo=nplo: e.tensor_copy(nplo, pA[:, 128:256]))
                    op("vector", lambda e, npu=npu: e.scalar_tensor_tensor(
                        Fb[:, :], npu, 1.0, self.eyeb[:, :], ALU.mult, ALU.add), deps=("vector",))
                    op("tensor", lambda e, clo=clo: e.matmul(pA[:, 256:384], clo, Fb[:, :],
                                                             start=True, stop=True), deps=("vector",))
                    if not last:
                        op("tensor", lambda e, clo=clo: e.matmul(pA[:, 384:512], Fb[:, :], clo,
                                                                 start=True, stop=True))
                    ncu = Pq[:, (gi % 2) * 256:(gi % 2) * 256 + 128]
                    nclo = Pq[:, (gi % 2) * 256 + 128:(gi % 2) * 256 + 256]
                    op("vector", lambda e, ncu=ncu: e.tensor_copy(ncu, pA[:, 256:384]), deps=("tensor",))
                    if not last:
                        op("vector", lambda e, nclo=nclo: e.tensor_copy(nclo, pA[:, 384:512]))
                    pu, plo = npu, nplo
                    cu, clo = ncu, nclo
                P.add("sync", lambda e, gck=gck, cu=cu: e.dma_start(g["dMIT"][gck], cu),
                      waits=[(self.esem("vector"), P.cur(self.esem("vector")))],
                      incs=[(self.esem("sync"), 16)])
                vready = [(self.esem("vector"), prep_done)]
                P.add("sync", lambda e, gck=gck, ck=ck: e.dma_start(
                    g["dPF"][gck][:, 0:64], AQK["f"][:, ck * C:(ck + 1) * C]),
                    waits=vready, incs=[(self.esem("sync"), 16)])
                P.add("sync", lambda e, gck=gck, ck=ck: e.dma_start(
                    g["dPF"][gck][:, 64:128], AQK["f"][:, PT + ck * C:PT + (ck + 1) * C]),
                    incs=[(self.esem("sync"), 16)])
                P.add("sync", lambda e, gck=gck, ck=ck: e.dma_start(
                    g["dPF"][gck][:, 128:192], AQK["f"][:, 2 * PT + ck * C:2 * PT + (ck + 1) * C]),
                    incs=[(self.esem("sync"), 16)])
                P.add("sync", lambda e, gck=gck, ck=ck: e.dma_start(
                    g["dPS"][gck][:, 0:64], AQK["s"][:, ck * C:(ck + 1) * C]),
                    incs=[(self.esem("sync"), 16)])
                P.add("sync", lambda e, gck=gck, ck=ck: e.dma_start(
                    g["dPS"][gck][:, 64:128], AQK["s"][:, PT + ck * C:PT + (ck + 1) * C]),
                    incs=[(self.esem("sync"), 16)])
                P.add("sync", lambda e, gck=gck, ck=ck: e.dma_start(
                    g["dPS"][gck][:, 128:192], AQK["s"][:, 2 * PT + ck * C:2 * PT + (ck + 1) * C]),
                    incs=[(self.esem("sync"), 16)])
                for m_, Rp in enumerate((R_pw0, R_pw1, R_pw2)):
                    P.add("sync", lambda e, gck=gck, m_=m_, Rp=Rp, ck=ck: e.dma_start(
                        g["dROWS"][gck][0:1, (2 + m_) * C:(3 + m_) * C], Rp[0:1, ck * C:(ck + 1) * C]),
                        incs=[(self.esem("sync"), 16)])
                P.add("sync", lambda e, gck=gck, ck=ck: e.dma_start(
                    g["dV"][gck], vv[:, ck * C:(ck + 1) * C]), incs=[(self.esem("sync"), 16)])
                op("vector", lambda e, ck=ck, par=par: e.tensor_scalar(
                    xf[:, 3 * PT + par * C:3 * PT + (par + 1) * C], vv[:, ck * C:(ck + 1) * C],
                    SQK, None, ALU.mult), extra_waits=scr_wait)
                P.add("sync", lambda e, gck=gck, par=par: e.dma_start(
                    g["dVs"][gck], xf[:, 3 * PT + par * C:3 * PT + (par + 1) * C]),
                    waits=[(self.esem("vector"), P.cur(self.esem("vector")))],
                    incs=[(self.esem("sync"), 16)])
                P.add("sync", lambda e, gck=gck, ck=ck: e.dma_start(
                    g["dNU"][gck], nu[:, ck * C:(ck + 1) * C]),
                    waits=[(self.esem("scalar"), P.cur(self.esem("scalar")))],
                    incs=[(self.esem("sync"), 16)])
                pc_scr = xf[:, 3 * PT + 2 * C + par * 2:3 * PT + 2 * C + par * 2 + 2]
                op("vector", lambda e, pc_scr=pc_scr, ck=ck: e.tensor_copy(
                    pc_scr[:, 0:1], S["f"]["Pp"][:, ck * C + C - 1:ck * C + C]), extra_waits=scr_wait)
                op("vector", lambda e, pc_scr=pc_scr, ck=ck: e.tensor_copy(
                    pc_scr[:, 1:2], S["s"]["Pp"][:, ck * C + C - 1:ck * C + C]))
                P.add("sync", lambda e, gck=gck, pc_scr=pc_scr: e.dma_start(g["dPC"][gck], pc_scr),
                      waits=[(self.esem("vector"), P.cur(self.esem("vector")))],
                      incs=[(self.esem("sync"), 16)])
                self.sync_after[gck] = P.cur(self.esem("sync"))
        self.a1_end = self.barrier_waits()

    def emit_B(self, ex, ar, arb):
        nc, P, g = self.nc, self.P, self.g
        ps = lambda name, shape, dt=F32: ex.enter_context(nc.psum_tensor(name, shape, dt))
        op = self.op
        NCK = self.NCK
        Ev = self.esem("vector"); Et = self.esem("tensor"); Es_ = self.esem("scalar")
        Esy = self.esem("sync")

        outar = ar.alloc(self.T)
        self.outar = outar
        F0 = ar.alloc(128); S0 = ar.alloc(128); nSp = ar.alloc(1)
        FLAGS = ar.alloc(max(self.NCK, 64))
        self.flags_view = FLAGS
        oas = ar.alloc(192)
        esq = ar.alloc(64)
        vrow = ar.alloc(64)
        stream = []
        for par in range(2):
            stream.append(dict(
                PF=ar.alloc(192), PS=ar.alloc(192), V=ar.alloc(64), Vs=ar.alloc(64),
                NU=ar.alloc(64), PC=ar.alloc(2), ROWS=ar.alloc(512),
                MITf=arb.alloc(64), MITs=arb.alloc(64),
                GQTf=arb.alloc(64), GQTs=arb.alloc(64),
                GOTf=arb.alloc(64), GOTs=arb.alloc(64),
                URf=arb.alloc(128), URs=arb.alloc(128),
                PWBC=ar.alloc(192), SROW=ar.alloc(64),
            ))
        Wb = arb.alloc(128)
        WbTf = arb.alloc(128)
        WbTs = arb.alloc(128)
        SWf = arb.alloc(128)
        SWs = arb.alloc(128)

        psB1 = ps("psB1", [128, 128])
        pT1 = ex.enter_context(nc.psum_tensor("pT1", [64, 128], BF16))
        pT2 = ex.enter_context(nc.psum_tensor("pT2", [64, 128], BF16))
        psSW = ps("psSW", [64, 256])
        psDOO = [ps("psDOO0", [128, 192]), ps("psDOO1", [128, 192])]
        psNEW = ps("psNEW", [128, 256])
        psMISC = ps("psMISC", [128, 192])

        op("vector", lambda e: e.memset(F0[:, :], 0.0), deps=("sync", "tensor", "scalar", "gpsimd"))
        op("vector", lambda e: e.memset(outar[:, :], 0.0))
        op("vector", lambda e: e.memset(S0[:, :], 0.0))
        op("vector", lambda e: e.memset(nSp[:, :], 0.0))
        op("vector", lambda e: e.memset(FLAGS[0:1, :], 1.0))

        def stream_chunk(ck):
            if ck >= NCK:
                return
            par = ck % 2
            sb = stream[par]
            first = True
            def dma(dst, srcfn):
                nonlocal first
                w = [(Ev, P.cur(Ev)), (Et, P.cur(Et)), (Es_, P.cur(Es_))] if first else []
                first = False
                P.add("sync", lambda e, dst=dst, srcfn=srcfn: e.dma_start(dst, srcfn()), waits=w,
                      incs=[(Esy, 16)])
            dma(sb["PF"], lambda ck=ck: g["dPF"][ck])
            dma(sb["PS"], lambda ck=ck: g["dPS"][ck])
            dma(sb["V"], lambda ck=ck: g["dV"][ck])
            dma(sb["Vs"], lambda ck=ck: g["dVs"][ck])
            dma(sb["NU"], lambda ck=ck: g["dNU"][ck])
            dma(sb["PC"], lambda ck=ck: g["dPC"][ck])
            dma(sb["ROWS"][0:1, :], lambda ck=ck: g["dROWS"][ck])
            dma(sb["MITf"][0:64, :], lambda ck=ck: g["dMIT"][ck][0:64, 0:64])
            dma(sb["MITs"][0:64, :], lambda ck=ck: g["dMIT"][ck][64:128, 64:128])
            dma(sb["GQTf"][0:64, :], lambda ck=ck: g["dGQT"][ck][0:64, :])
            dma(sb["GQTs"][0:64, :], lambda ck=ck: g["dGQT"][ck][64:128, :])
            dma(sb["GOTf"][0:64, :], lambda ck=ck: g["dGOT"][ck][0:64, :])
            dma(sb["GOTs"][0:64, :], lambda ck=ck: g["dGOT"][ck][64:128, :])
            dma(sb["URf"][0:64, :], lambda ck=ck: g["dUR"][ck][0:64, :])
            dma(sb["URs"][0:64, :], lambda ck=ck: g["dUR"][ck][64:128, :])

        def do_chunk(ck):
            ST = B_STAGE
            par = ck % 2
            sb = stream[par]
            pD = psDOO[par]
            for m_ in range(3):
                op("tensor", lambda e, sb=sb, m_=m_: e.matmul(
                    psMISC[:, m_ * C:(m_ + 1) * C], self.onesrow[0:1, 0:128],
                    sb["ROWS"][0:1, (2 + m_) * C:(3 + m_) * C], start=True, stop=True),
                   deps=("vector", "sync", "scalar") if m_ == 0 else ())
            op("vector", lambda e, sb=sb: e.tensor_copy(sb["PWBC"][:, :], psMISC[:, :]),
               deps=("tensor",))
            if ST < 2:
                return
            op("tensor", lambda e, sb=sb: e.matmul(psB1[:, 0:64], F0[:, :], sb["PF"][:, 0:64],
                                                   start=True, stop=False), deps=("vector",))
            op("tensor", lambda e, sb=sb: e.matmul(psB1[:, 0:64], self.eyef[:, :], sb["V"],
                                                   start=False, stop=True))
            op("tensor", lambda e, sb=sb: e.matmul(psB1[:, 64:128], S0[:, :], sb["PS"][:, 64:128],
                                                   start=True, stop=False))
            op("tensor", lambda e, sb=sb: e.matmul(psB1[:, 64:128], self.eyef[:, :], sb["V"],
                                                   start=False, stop=True))
            op("vector", lambda e: e.tensor_copy(Wb[:, :], psB1[:, :]), deps=("tensor",))
            if ST < 3:
                return
            op("tensor", lambda e: e.transpose(pT1[:, :], Wb[:, 0:64], self.eyeb[:, :]),
               deps=("vector",))
            op("tensor", lambda e: e.transpose(pT2[:, :], Wb[:, 64:128], self.eyeb[:, :]))
            op("vector", lambda e: e.tensor_copy(WbTf[0:64, :], pT1[:, :]), deps=("tensor",))
            op("vector", lambda e: e.tensor_copy(WbTs[0:64, :], pT2[:, :]))
            if ST < 4:
                return
            op("tensor", lambda e, sb=sb: e.matmul(psSW[:, 0:128], sb["MITf"][0:64, :],
                                                   WbTf[0:64, :], start=True, stop=True),
               deps=("vector",))
            op("tensor", lambda e, sb=sb: e.matmul(psSW[:, 128:256], sb["MITs"][0:64, :],
                                                   WbTs[0:64, :], start=True, stop=True))
            op("vector", lambda e: e.tensor_copy(SWf[0:64, :], psSW[:, 0:128]), deps=("tensor",))
            op("vector", lambda e: e.tensor_copy(SWs[0:64, :], psSW[:, 128:256]))
            if ST < 5:
                return
            op("tensor", lambda e, sb=sb, pD=pD: e.matmul(pD[:, 0:64], F0[:, :],
                                                          sb["PF"][:, 128:192],
                                                          start=True, stop=False),
               deps=("vector", "scalar"))
            op("tensor", lambda e, sb=sb, pD=pD: e.matmul(pD[:, 0:64], S0[:, :],
                                                          sb["PS"][:, 0:64],
                                                          start=False, stop=False))
            op("tensor", lambda e, sb=sb, pD=pD: e.matmul(pD[:, 0:64], self.eyef[:, :],
                                                          sb["Vs"], start=False, stop=False))
            op("tensor", lambda e, sb=sb, pD=pD: e.matmul(pD[:, 0:64], SWf[0:64, :],
                                                          sb["GQTf"][0:64, :],
                                                          start=False, stop=False))
            op("tensor", lambda e, sb=sb, pD=pD: e.matmul(pD[:, 0:64], SWs[0:64, :],
                                                          sb["GQTs"][0:64, :],
                                                          start=False, stop=True))
            op("tensor", lambda e, sb=sb, pD=pD: e.matmul(pD[:, 64:128], F0[:, :],
                                                          sb["PF"][:, 64:128],
                                                          start=True, stop=False))
            op("tensor", lambda e, sb=sb, pD=pD: e.matmul(pD[:, 64:128], SWf[0:64, :],
                                                          sb["GOTf"][0:64, :],
                                                          start=False, stop=True))
            op("tensor", lambda e, sb=sb, pD=pD: e.matmul(pD[:, 128:192], S0[:, :],
                                                          sb["PS"][:, 128:192],
                                                          start=True, stop=False))
            op("tensor", lambda e, sb=sb, pD=pD: e.matmul(pD[:, 128:192], SWs[0:64, :],
                                                          sb["GOTs"][0:64, :],
                                                          start=False, stop=True))
            if ST < 6:
                return
            op("scalar", lambda e, pD=pD: e.activation(esq[:, :], pD[:, 0:64], AF.Square),
               deps=("tensor",))
            if ST == 60:
                return
            op("tensor", lambda e: e.matmul(psMISC[0:1, 0:64], self.consts[:, 0:1], esq[:, :],
                                            start=True, stop=True), deps=("scalar", "vector"))
            if ST == 61:
                return
            op("scalar", lambda e, sb=sb: e.activation(sb["SROW"][0:1, :], psMISC[0:1, 0:64],
                                                       AF.Sigmoid, scale=float(1.0 / (1.0 + 1e-6))),
               deps=("tensor",))
            if ST == 62:
                return
            op("vector", lambda e: e.tensor_scalar(vrow[0:1, :], psMISC[0:1, 0:64], 38.0, None,
                                                   ALU.is_ge), deps=("tensor", "scalar"))
            if ST == 63:
                return
            op("vector", lambda e, ck=ck: e.tensor_reduce(FLAGS[0:1, ck:ck + 1], vrow[0:1, :],
                                                          mybir.AxisListType.X, ALU.min),
               deps=("vector",))
            if ST == 64:
                return
            if ST < 7:
                return
            srow = sb["SROW"]
            op("vector", lambda e, srow=srow: e.tensor_scalar(oas[0:1, 0:C], srow[0:1, :], 0.7,
                                                              None, ALU.is_gt), deps=("scalar",))
            op("vector", lambda e: e.tensor_scalar(oas[0:1, C:2 * C], oas[0:1, 0:C], -0.1, 1.0,
                                                   ALU.mult, ALU.add), deps=("vector",))
            op("tensor", lambda e: e.matmul(psMISC[:, 0:C], self.onesrow[0:1, 0:128],
                                            oas[0:1, 0:C], start=True, stop=True),
               deps=("vector", "scalar"))
            op("tensor", lambda e: e.matmul(psMISC[:, C:2 * C], self.onesrow[0:1, 0:128],
                                            oas[0:1, C:2 * C], start=True, stop=True))
            op("vector", lambda e, sb=sb: e.tensor_tensor(oas[:, 0:C], sb["NU"], psMISC[:, 0:C],
                                                          ALU.mult), deps=("tensor",))
            op("vector", lambda e: e.tensor_copy(oas[:, C:2 * C], psMISC[:, C:2 * C]))
            op("vector", lambda e: e.tensor_tensor(oas[:, 2 * C:2 * C + 1], nSp[:, :],
                                                   oas[:, C:C + 1], ALU.mult), deps=("vector",))
            op("vector", lambda e: e.tensor_tensor(oas[:, 0:1], oas[:, 0:1], oas[:, 2 * C:2 * C + 1],
                                                   ALU.add), deps=("vector",))
            op("vector", lambda e: e.tensor_tensor_scan(oas[:, 2 * C:3 * C], oas[:, C:2 * C],
                                                        oas[:, 0:C], 0.0, ALU.mult, ALU.add),
               deps=("vector",))
            op("vector", lambda e: e.tensor_copy(nSp[:, :], oas[:, 3 * C - 1:3 * C]),
               deps=("vector",))
            oc = outar[:, ck * C:(ck + 1) * C]
            op("vector", lambda e, sb=sb, oc=oc, pD=pD: e.tensor_tensor(
                oc, pD[:, 64:128], sb["PWBC"][:, 0:C], ALU.mult), deps=("scalar",))
            op("vector", lambda e, sb=sb, pD=pD: e.tensor_tensor(
                oas[:, 0:C], pD[:, 128:192], sb["PWBC"][:, C:2 * C], ALU.mult))
            op("vector", lambda e, sb=sb: e.tensor_tensor(
                oas[:, C:2 * C], oas[:, 2 * C:3 * C], sb["PWBC"][:, 2 * C:3 * C], ALU.mult),
               deps=("vector",))
            op("vector", lambda e, oc=oc: e.tensor_tensor(oc, oc, oas[:, 0:C], ALU.add),
               deps=("vector",))
            op("vector", lambda e, oc=oc: e.tensor_tensor(oc, oc, oas[:, C:2 * C], ALU.add),
               deps=("vector",))
            if ST < 8:
                return
            op("tensor", lambda e, sb=sb: e.matmul(psNEW[:, 0:128], sb["URf"][0:64, :],
                                                   SWf[0:64, :], start=True, stop=True),
               deps=("vector",))
            op("tensor", lambda e, sb=sb: e.matmul(psNEW[:, 128:256], sb["URs"][0:64, :],
                                                   SWs[0:64, :], start=True, stop=True))
            op("vector", lambda e, sb=sb: e.scalar_tensor_tensor(
                F0[:, :], F0[:, :], sb["PC"][:, 0:1], psNEW[:, 0:128], ALU.mult, ALU.add),
               deps=("tensor",))
            op("vector", lambda e, sb=sb: e.scalar_tensor_tensor(
                S0[:, :], S0[:, :], sb["PC"][:, 1:2], psNEW[:, 128:256], ALU.mult, ALU.add))

        def emit_dbg(tag):
            nc_ = self.nc
            self.op("vector", lambda e: e.tensor_copy(oas[:, :], psDOO[0][:, :]),
                    deps=("tensor", "scalar", "vector"))
            views = dict(Wb=(Wb, BF16), WbTf=(WbTf[0:64, :], BF16), WbTs=(WbTs[0:64, :], BF16),
                         SWf=(SWf[0:64, :], BF16), SWs=(SWs[0:64, :], BF16),
                         pD=(oas, F32), PWBC=(stream[0]["PWBC"], F32),
                         SROW=(stream[0]["SROW"], F32),
                         F0=(F0, F32), S0=(S0, F32), FLAGS=(FLAGS[0:1, :], F32))
            w = self.barrier_waits()
            for nm, (v, dt) in views.items():
                t = nc_.dram_tensor(f"o_dbg{tag}_{nm}", list(v.shape), dt,
                                    kind="ExternalOutput").ap()
                self.P.add("sync", lambda e, t=t, v=v: e.dma_start(t, v),
                           waits=w, incs=[(self.esem("sync"), 16)])
                w = ()

        stream_chunk(0)
        stream_chunk(1)
        for ck in range(NCK):
            do_chunk(ck)
            if ck == 0 and getattr(self, "_dbgB", False):
                emit_dbg("0")
            stream_chunk(ck + 2)
        self.b_end = self.barrier_waits()


    def emit_CAST(self, ex, ar, arb):
        nc, P, g, op = self.nc, self.P, self.g, self.op
        land = [ar.alloc(2048) for _ in range(4)]
        bout = [arb.alloc(2048) for _ in range(4)]
        Esy = self.esem("sync")
        Egp = self.esem("gpsimd")
        Ev = self.esem("vector")

        units = []
        for e_ in range(16):
            for c_ in range(11):
                units.append((g["W1T"][e_][:, 128 * c_:128 * c_ + 128], 16, g["dW1b"][e_][c_]))
                units.append((g["W3T"][e_][:, 128 * c_:128 * c_ + 128], 16, g["dW3b"][e_][c_]))
            for m_ in range(16):
                units.append((g["W2T"][e_][:, 128 * m_:128 * m_ + 128], 11, g["dW2b"][e_][m_]))
        for c_ in range(22):
            units.append((g["Ws1T"][:, 128 * c_:128 * c_ + 128], 16, g["dWs1b"][c_]))
            units.append((g["Ws3T"][:, 128 * c_:128 * c_ + 128], 16, g["dWs3b"][c_]))
        for hf_ in range(2):
            for m_ in range(16):
                units.append((g["Ws2T"][1408 * hf_:1408 * (hf_ + 1), 128 * m_:128 * m_ + 128],
                              11, g["dWs2b"][hf_][m_]))

        ev_cnt = {}
        gp_cnt = {}
        for i, (srcap, ns, dstap) in enumerate(units):
            par = i % 4
            w = self.barrier_waits() if i == 0 else (
                [(Ev, ev_cnt[i - 4])] if i >= 4 else [])
            P.add("sync", lambda e, srcap=srcap, ns=ns, par=par: e.dma_start(
                land[par][:, 0:128 * ns].rearrange("p (s c) -> p s c", s=ns),
                srcap.rearrange("(s p) c -> p s c", p=128)),
                waits=w, incs=[(Esy, 16)])
            syw = [(Esy, P.cur(Esy))]
            if i >= 4:
                syw.append((Egp, gp_cnt[i - 4]))
            P.add("vector", lambda e, ns=ns, par=par: e.tensor_copy(
                bout[par][:, 0:128 * ns], land[par][:, 0:128 * ns]),
                waits=syw, incs=[(Ev, 1)])
            ev_cnt[i] = P.cur(Ev)
            P.add("gpsimd", lambda e, dstap=dstap, ns=ns, par=par: e.dma_start(
                dstap[:, 0:128 * ns], bout[par][:, 0:128 * ns]),
                waits=[(Ev, ev_cnt[i])], incs=[(Egp, 16)])
            gp_cnt[i] = P.cur(Egp)

    def emit_C(self, ex, ar, arb):
        nc, P, g, op = self.nc, self.P, self.g, self.op
        NC_ = self.n_cores
        my_tok = self.T // NC_
        NBLK = my_tok // 512
        assert my_tok % 512 == 0
        NFF = 11
        NSH = 22

        dOUTA = g["dOUTA"]
        o_y = nc.dram_tensor("o_y", [16, 128, my_tok], F32, kind="ExternalOutput").ap()
        xT4 = [g[f"x4d_{m_}"] for m_ in range(16)]
        _cidc = {}

        def cid(e):
            if NC_ == 1:
                return 0
            k = id(e)
            if k not in _cidc:
                _cidc[k] = e.partition_id()
            return _cidc[k]

        myat = ar.alloc(512)
        sq = ar.alloc(512)
        hsq = [ar.alloc(512), ar.alloc(512)]
        hbuf = ar.alloc(16 * 512)
        xc = [ar.alloc(512), ar.alloc(512)]
        ysb = ar.alloc(16 * 512)
        w1l = ar.alloc(2048)
        gsc = ar.alloc(96)
        h1b = arb.alloc(NFF * 512)
        w1b = [arb.alloc(2048), arb.alloc(2048)]
        w3b = [arb.alloc(2048), arb.alloc(2048)]
        wpbA = Arena(self.wpb[:, :], 6 * NCH * 128)
        h2bf = wpbA.alloc(16 * 512)
        ybf = wpbA.alloc(512)
        gwb = wpbA.alloc(256)
        w2b = [wpbA.alloc(1408), wpbA.alloc(1408)]
        crows = ar.alloc(2560)
        geAll = ar.alloc(512)
        gerow = ar.alloc(512)

        pA = ex.enter_context(nc.psum_tensor("cpA", [128, 512], F32))
        pB = ex.enter_context(nc.psum_tensor("cpB", [128, 512], F32))
        pC = ex.enter_context(nc.psum_tensor("cpC", [128, 512], F32))
        pY = [ex.enter_context(nc.psum_tensor("cpY0", [128, 512], F32)),
              ex.enter_context(nc.psum_tensor("cpY1", [128, 512], F32))]
        pR1 = ex.enter_context(nc.psum_tensor("cpR1", [1, 512], F32))
        pR2 = ex.enter_context(nc.psum_tensor("cpR2", [1, 512], F32))

        Esy = self.esem("sync")

        P.add("sync", lambda e: e.dma_start(
            w1l[:, 0:256].rearrange("p (s e) -> p s e", s=NCH),
            g["gate_wT"].rearrange("(s p) e -> p s e", p=128)), incs=[(Esy, 16)])
        op("vector", lambda e: e.tensor_copy(gwb[:, :], w1l[:, 0:256]), deps=("sync",))
        op("vector", lambda e: e.memset(ysb[:, :], 0.0))

        mc = self.mcols
        for b in range(NBLK):
            tb = b * 512

            P.add("sync", lambda e, b=b: e.dma_start(
                myat[:, :], dOUTA[b][cid(e), :, :]),
                waits=[(self.esem("vector"), P.cur(self.esem("vector")))], incs=[(Esy, 16)])
            op("vector", lambda e: e.tensor_tensor(sq[:, :], myat[:, :], myat[:, :], ALU.mult),
               deps=("sync",))
            op("tensor", lambda e: e.matmul(pR1[0:1, :], self.consts[:, 0:1], sq[:, :],
                                            start=True, stop=True), deps=("vector", "scalar"))
            op("scalar", lambda e: e.activation(crows[0:1, 0:512], pR1[0:1, :], AF.Sqrt,
                                                scale=float(1.0 / 128.0),
                                                bias=self.rows0[0:1, 31:32]), deps=("tensor",))
            op("vector", lambda e: e.reciprocal(crows[0:1, 512:1024], crows[0:1, 0:512]),
               deps=("scalar",))
            op("tensor", lambda e: e.matmul(pA[:, :], self.onesrow[0:1, 0:128],
                                            crows[0:1, 512:1024], start=True, stop=True),
               deps=("vector",))
            op("vector", lambda e: e.tensor_tensor(ybf[:, :], myat[:, :], pA[:, :], ALU.mult),
               deps=("tensor",))

            for m in range(16):
                par = m % 2
                pO = pB if par == 0 else pC
                P.add("sync", lambda e, m=m, b=b, par=par: e.dma_start(
                    xc[par][:, :], xT4[m][b][cid(e), :, :]),
                    waits=[(self.esem("vector"), P.cur(self.esem("vector")))], incs=[(Esy, 16)])
                op("tensor", lambda e, m=m, pO=pO: e.matmul(
                    pO[:, :], self.wof[:, m * 128:(m + 1) * 128], ybf[:, :],
                    start=True, stop=True), deps=("vector",))
                op("vector", lambda e, m=m, pO=pO, par=par: e.scalar_tensor_tensor(
                    hbuf[:, m * 512:(m + 1) * 512], pO[:, :], mc[:, 32 + m:33 + m],
                    xc[par][:, :], ALU.mult, ALU.add), deps=("tensor", "sync"))

            for m in range(16):
                op("tensor", lambda e, m=m: e.matmul(
                    pR1[0:1, :], self.consts[:, 0:1], hbuf[:, m * 512:(m + 1) * 512],
                    start=(m == 0), stop=(m == 15)),
                   deps=("vector",) if m == 0 else ())
            for m in range(16):
                par = m % 2
                op("vector", lambda e, m=m, par=par: e.tensor_tensor(
                    hsq[par][:, :], hbuf[:, m * 512:(m + 1) * 512],
                    hbuf[:, m * 512:(m + 1) * 512], ALU.mult),
                   deps=("tensor",) if m < 2 else ("tensor",))
                op("tensor", lambda e, m=m, par=par: e.matmul(
                    pR2[0:1, :], self.consts[:, 0:1], hsq[par][:, :],
                    start=(m == 0), stop=(m == 15)), deps=("vector",))

            op("vector", lambda e: e.tensor_scalar(crows[0:1, 0:512], pR1[0:1, :],
                                                   float(1.0 / 2048.0), None, ALU.mult),
               deps=("tensor",))
            op("vector", lambda e: e.tensor_tensor(crows[0:1, 512:1024], crows[0:1, 0:512],
                                                   crows[0:1, 0:512], ALU.mult),
               deps=("vector",))
            op("vector", lambda e: e.tensor_scalar(crows[0:1, 1024:1536], pR2[0:1, :],
                                                   float(1.0 / 2048.0), EPS_LN,
                                                   ALU.mult, ALU.add), deps=("vector",))
            op("vector", lambda e: e.tensor_tensor(crows[0:1, 1024:1536], crows[0:1, 1024:1536],
                                                   crows[0:1, 512:1024], ALU.subtract),
               deps=("vector",))
            op("scalar", lambda e: e.activation(crows[0:1, 1536:2048], crows[0:1, 1024:1536],
                                                AF.Sqrt), deps=("vector",))
            op("vector", lambda e: e.reciprocal(crows[0:1, 512:1024], crows[0:1, 1536:2048]),
               deps=("scalar",))
            op("vector", lambda e: e.tensor_tensor(crows[0:1, 2048:2560], crows[0:1, 0:512],
                                                   crows[0:1, 512:1024], ALU.mult),
               deps=("vector",))
            op("tensor", lambda e: e.matmul(pA[:, :], self.onesrow[0:1, 0:128],
                                            crows[0:1, 512:1024], start=True, stop=True),
               deps=("vector",))
            op("tensor", lambda e: e.matmul(pB[:, :], self.onesrow[0:1, 0:128],
                                            crows[0:1, 2048:2560], start=True, stop=True))

            for m in range(16):
                op("vector", lambda e, m=m: e.tensor_tensor(
                    sq[:, :], hbuf[:, m * 512:(m + 1) * 512], pA[:, :], ALU.mult),
                   deps=("tensor",) if m == 0 else ("vector",))
                op("vector", lambda e: e.tensor_tensor(sq[:, :], sq[:, :], pB[:, :],
                                                       ALU.subtract), deps=("vector",))
                op("vector", lambda e, m=m: e.tensor_scalar(
                    h2bf[:, m * 512:(m + 1) * 512], sq[:, :], mc[:, 64 + m:65 + m],
                    mc[:, 48 + m:49 + m], ALU.mult, ALU.add), deps=("vector",))

            for s4 in range(4):
                for s in range(16):
                    op("tensor", lambda e, s=s, s4=s4: e.matmul(
                        pC[:, 0:16], h2bf[:, 512 * s + 128 * s4:512 * s + 128 * s4 + 128],
                        gwb[:, 16 * s:16 * s + 16], start=(s == 0), stop=(s == 15)),
                       deps=("vector", "scalar") if s == 0 else ())
                op("scalar", lambda e: e.activation(gsc[:, 0:16], pC[:, 0:16], AF.Sigmoid),
                   deps=("tensor",))
                for gi in range(4):
                    op("vector", lambda e, gi=gi: e.tensor_reduce(
                        gsc[:, 16 + gi:17 + gi], gsc[:, 4 * gi:4 * gi + 4],
                        mybir.AxisListType.X, ALU.max),
                       deps=("scalar",) if gi == 0 else ("vector",))
                op("vector", lambda e: e.tensor_reduce(gsc[:, 20:21], gsc[:, 16:20],
                                                       mybir.AxisListType.X, ALU.max),
                   deps=("vector",))
                op("vector", lambda e: e.tensor_scalar(gsc[:, 24:28], gsc[:, 16:20],
                                                       gsc[:, 20:21], None, ALU.is_ge),
                   deps=("vector",))
                op("vector", lambda e: e.scalar_tensor_tensor(gsc[:, 28:32], gsc[:, 24:28],
                                                              -1e9, gsc[:, 16:20],
                                                              ALU.mult, ALU.add),
                   deps=("vector",))
                op("vector", lambda e: e.tensor_reduce(gsc[:, 21:22], gsc[:, 28:32],
                                                       mybir.AxisListType.X, ALU.max),
                   deps=("vector",))
                op("vector", lambda e: e.tensor_scalar(gsc[:, 24:28], gsc[:, 16:20],
                                                       gsc[:, 21:22], None, ALU.is_ge),
                   deps=("vector",))
                op("vector", lambda e: e.tensor_tensor(
                    gsc[:, 32:48].rearrange("p (a z) -> p a z", a=4),
                    gsc[:, 0:16].rearrange("p (a z) -> p a z", a=4),
                    gsc[:, 24:28].rearrange("p (a z) -> p a z", a=4).broadcast_to([128, 4, 4]),
                    ALU.mult), deps=("vector",))
                op("vector", lambda e: e.tensor_copy(gsc[:, 48:64], gsc[:, 32:48]),
                   deps=("vector",))
                for it in range(3):
                    op("vector", lambda e: e.tensor_reduce(gsc[:, 22:23], gsc[:, 48:64],
                                                           mybir.AxisListType.X, ALU.max),
                       deps=("vector",))
                    op("vector", lambda e: e.tensor_scalar(gsc[:, 64:80], gsc[:, 48:64],
                                                           gsc[:, 22:23], None, ALU.is_ge),
                       deps=("vector",))
                    op("vector", lambda e: e.scalar_tensor_tensor(gsc[:, 48:64], gsc[:, 64:80],
                                                                  -1e9, gsc[:, 48:64],
                                                                  ALU.mult, ALU.add),
                       deps=("vector",))
                op("vector", lambda e: e.tensor_reduce(gsc[:, 23:24], gsc[:, 48:64],
                                                       mybir.AxisListType.X, ALU.max),
                   deps=("vector",))
                op("vector", lambda e: e.tensor_scalar(gsc[:, 64:80], gsc[:, 32:48],
                                                       gsc[:, 23:24], None, ALU.is_ge),
                   deps=("vector",))
                op("vector", lambda e: e.tensor_tensor(gsc[:, 80:96], gsc[:, 0:16],
                                                       gsc[:, 64:80], ALU.mult),
                   deps=("vector",))
                op("vector", lambda e: e.tensor_reduce(gsc[:, 20:21], gsc[:, 80:96],
                                                       mybir.AxisListType.X, ALU.add),
                   deps=("vector",))
                op("vector", lambda e: e.reciprocal(gsc[:, 21:22], gsc[:, 20:21]),
                   deps=("vector",))
                op("vector", lambda e: e.tensor_scalar(gsc[:, 32:48], gsc[:, 80:96],
                                                       gsc[:, 21:22], None, ALU.mult),
                   deps=("vector",))
                op("tensor", lambda e: e.transpose(pA[0:16, 0:128], gsc[:, 32:48],
                                                   self.eyef[:, :]), deps=("vector",))
                op("vector", lambda e, s4=s4: e.tensor_copy(
                    geAll[0:16, 128 * s4:128 * s4 + 128], pA[0:16, 0:128]), deps=("tensor",))

            Et_ = self.esem("tensor")
            Ev_ = self.esem("vector")
            lastread = {"w1_0": 0, "w1_1": 0, "w2_0": 0, "w2_1": 0}
            sched = {"pbpc": P.cur(Ev_), "pY_0": P.cur(Ev_), "pY_1": P.cur(Ev_)}

            def ffn_branch(is_shared, e_idx):
                nch_ff = NSH if is_shared else NFF
                halves = 2 if is_shared else 1
                per = nch_ff // halves
                if not is_shared:
                    P.add("sync", lambda e: e.dma_start(gerow[0:1, :], geAll[e_idx:e_idx + 1, :]),
                          waits=[(self.esem("tensor"), P.cur(self.esem("tensor"))),
                                 (self.esem("vector"), P.cur(self.esem("vector")))],
                          incs=[(Esy, 16)])
                    op("tensor", lambda e: e.matmul(pA[:, :], self.onesrow[0:1, 0:128],
                                                    gerow[0:1, :], start=True, stop=True),
                       deps=("vector", "sync"))
                for hf in range(halves):
                    for c in range(per):
                        ch = hf * per + c
                        wpar = ch % 2
                        if is_shared:
                            s1 = g["dWs1b"][ch]
                            s3 = g["dWs3b"][ch]
                        else:
                            s1 = g["dW1b"][e_idx][ch]
                            s3 = g["dW3b"][e_idx][ch]
                        P.add("sync", lambda e, s1=s1, wpar=wpar: e.dma_start(
                            w1b[wpar][:, :], s1),
                            waits=[(Et_, lastread[f"w1_{wpar}"])],
                            incs=[(Esy, 16)])
                        P.add("sync", lambda e, s3=s3, wpar=wpar: e.dma_start(
                            w3b[wpar][:, :], s3), incs=[(Esy, 16)])
                        for s in range(16):
                            op("tensor", lambda e, s=s, wpar=wpar: e.matmul(
                                pB[:, :], w1b[wpar][:, 128 * s:128 * s + 128],
                                h2bf[:, 512 * s:512 * s + 512], start=(s == 0), stop=(s == 15)),
                               deps=("scalar", "sync") if s == 0 else (),
                               extra_waits=[(Ev_, sched["pbpc"])] if s == 0 else ())
                        for s in range(16):
                            op("tensor", lambda e, s=s, wpar=wpar: e.matmul(
                                pC[:, :], w3b[wpar][:, 128 * s:128 * s + 128],
                                h2bf[:, 512 * s:512 * s + 512], start=(s == 0), stop=(s == 15)))
                        lastread[f"w1_{wpar}"] = P.cur(Et_)
                        op("scalar", lambda e: e.activation(sq[:, :], pB[:, :], AF.Sigmoid),
                           deps=("tensor",))
                        op("vector", lambda e: e.tensor_tensor(sq[:, :], sq[:, :], pB[:, :],
                                                               ALU.mult),
                           deps=("scalar", "vector"))
                        op("vector", lambda e: e.tensor_tensor(sq[:, :], sq[:, :], pC[:, :],
                                                               ALU.mult), deps=("vector",))
                        sched["pbpc"] = P.cur(Ev_)
                        if is_shared:
                            op("vector", lambda e, c=c: e.tensor_copy(
                                h1b[:, 512 * c:512 * c + 512], sq[:, :]), deps=("vector",))
                        else:
                            op("vector", lambda e, c=c: e.tensor_tensor(
                                h1b[:, 512 * c:512 * c + 512], sq[:, :], pA[:, :], ALU.mult),
                               deps=("vector",))
                    ev_h1_done = P.cur(Ev_)
                    for m in range(16):
                        wpar = m % 2
                        if is_shared:
                            s2 = g["dWs2b"][hf][m]
                        else:
                            s2 = g["dW2b"][e_idx][m]
                        P.add("sync", lambda e, s2=s2, wpar=wpar: e.dma_start(
                            w2b[wpar][:, :], s2),
                            waits=[(Et_, lastread[f"w2_{wpar}"])],
                            incs=[(Esy, 16)])
                        pYm = pY[m % 2]
                        for c in range(per):
                            op("tensor", lambda e, c=c, pYm=pYm, wpar=wpar: e.matmul(
                                pYm[:, :], w2b[wpar][:, 128 * c:128 * c + 128],
                                h1b[:, 512 * c:512 * c + 512], start=(c == 0),
                                stop=(c == per - 1)),
                               deps=("sync",) if c == 0 else (),
                               extra_waits=[(Ev_, max(ev_h1_done, sched[f"pY_{m % 2}"]))]
                               if c == 0 else ())
                        lastread[f"w2_{wpar}"] = P.cur(Et_)
                        op("vector", lambda e, m=m, pYm=pYm: e.tensor_tensor(
                            ysb[:, 512 * m:512 * m + 512], ysb[:, 512 * m:512 * m + 512],
                            pYm[:, :], ALU.add), deps=("tensor",))
                        sched[f"pY_{m % 2}"] = P.cur(Ev_)

            for e_idx in range(16):
                ffn_branch(False, e_idx)
            ffn_branch(True, 0)

            if b == 0 and getattr(self, "_dbgC", False):
                wdb = self.barrier_waits()
                for nm, v, dt in [("hbuf", hbuf, F32), ("h2bf", h2bf, BF16),
                                  ("geAll", geAll[0:16, :], F32), ("ysb", ysb, F32),
                                  ("myat", myat, F32), ("crows", crows[0:1, :], F32),
                                  ("ybf", ybf, BF16), ("wof", self.wof[:, :], BF16),
                                  ("mcols", self.mcols[:, :], F32)]:
                    tdb = nc.dram_tensor(f"o_dbgC_{nm}", list(v.shape), dt,
                                         kind="ExternalOutput").ap()
                    P.add("sync", lambda e, t=tdb, v=v: e.dma_start(t, v),
                          waits=wdb, incs=[(Esy, 16)])
                    wdb = ()
            for m in range(16):
                op("vector", lambda e, m=m: e.scalar_tensor_tensor(
                    hbuf[:, m * 512:(m + 1) * 512], ysb[:, m * 512:(m + 1) * 512],
                    mc[:, 80 + m:81 + m], hbuf[:, m * 512:(m + 1) * 512],
                    ALU.mult, ALU.add),
                   deps=("vector", "sync") if m == 0 else ("vector",))
                P.add("sync", lambda e, m=m, tb=tb: e.dma_start(
                    o_y[m][:, tb:tb + 512], hbuf[:, m * 512:(m + 1) * 512]),
                    waits=[(self.esem("vector"), P.cur(self.esem("vector")))],
                    incs=[(Esy, 16)])
            if b + 1 < NBLK:
                op("vector", lambda e: e.memset(ysb[:, :], 0.0), deps=("tensor",))

    def finalize(self):
        if hasattr(self, "_dbgcols"):
            t = self.nc.dram_tensor("o_dbgcols", [128, 160], F32, kind="ExternalOutput").ap()
            self.P.add("sync", lambda e, t=t: e.dma_start(t, self.cols[:, :]),
                       waits=self.barrier_waits(), incs=[(self.esem("sync"), 16)])
        self.P.add("sync", lambda e: None, waits=self.barrier_waits())


def build(n_tok, prep_out=True, dbgcols=False, do_B=False, out_B=False, do_C=False,
          n_cores=8):
    from contextlib import ExitStack
    nc = bass.Bass("TRN2", target_bir_lowering=False, debug=False)
    mk = MK(nc, n_tok, n_cores=n_cores)
    if dbgcols: mk._dbgcols = True
    if prep_out:
        pass
    g = mk.declare_io()
    if prep_out:
        NCK = mk.NCK
        for nm, shp, dt in [("dPF", [NCK, 128, 192], F32), ("dPS", [NCK, 128, 192], F32),
                            ("dV", [NCK, 128, 64], F32),
                            ("dVs", [NCK, 128, 64], F32), ("dMIT", [NCK, 128, 128], BF16),
                            ("dGQT", [NCK, 128, C], BF16), ("dGOT", [NCK, 128, C], BF16),
                            ("dUR", [NCK, 128, 128], BF16),
                            ("dNU", [NCK, 128, C], F32), ("dROWS", [NCK, 1, 512], F32),
                            ("dPC", [NCK, 128, 2], F32)]:
            g[nm] = nc.dram_tensor("o_" + nm, shp, dt, kind="ExternalOutput").ap()
    from contextlib import ExitStack
    with ExitStack() as ex:
        arT = ex.enter_context(nc.sbuf_tensor("arenaF", [128, 29696], F32))
        arbT = ex.enter_context(nc.sbuf_tensor("arenaB", [128, 16384], BF16))
        ar = Arena(arT[:, :], 29696)
        arb = Arena(arbT[:, :], 16384)
        with ExitStack() as ex_psum:
            mk.emit_A0(ex, ar, ex_psum)
        ar.pop()
        ar.push()
        with ExitStack() as ex_psA1:
            mk.emit_A1(ex_psA1, ar, arb)
        if do_B:
            ar.pop()
            ar.push()
            arb2 = Arena(arbT[:, :], 16384)
            with ExitStack() as ex_psB:
                mk.emit_B(ex_psB, ar, arb2)
            if out_B:
                t = nc.dram_tensor("o_dOUT", [128, mk.T], F32, kind="ExternalOutput").ap()
                mk.P.add("sync", lambda e, t=t: e.dma_start(t, mk.outar),
                         waits=mk.barrier_waits(), incs=[(mk.esem("sync"), 16)])
        if do_C:
            NC_s = mk.n_cores
            NBLK_s = max((mk.T // NC_s) // 512, 1)
            for b_ in range(NBLK_s):
                mk.P.add("sync", lambda e, b_=b_: e.dma_start(
                    g["dOUTA"][b_].rearrange("c p t -> p c t"),
                    mk.outar.rearrange("p (c b t) -> p c b t", c=NC_s, b=NBLK_s)[:, :, b_, :]),
                    waits=mk.barrier_waits() if b_ == 0 else (),
                    incs=[(mk.esem("sync"), 16)])
            ar.pop()
            ar.push()
            arbK = Arena(arbT[:, :], 16384)
            with ExitStack() as ex_psK:
                mk.emit_CAST(ex_psK, ar, arbK)
            ar.pop()
            arbC = Arena(arbT[:, :], 16384)
            with ExitStack() as ex_psC:
                mk.emit_C(ex_psC, ar, arbC)
        mk.finalize()
        with nc.Block() as block:
            mk.P.emit(block)
    return nc, mk


class ex_psB_wrap:
    def __init__(self, ex_outer, ex_psum):
        self.ex_outer = ex_outer
        self.ex_psum = ex_psum

    def enter_context(self, cm):
        return self.ex_psum.enter_context(cm)

build_phaseA = build



from concourse.bass_utils import run_bass_kernel_spmd

_IN_NAMES = ["xT", "Wk", "Wv", "Wq", "Waf", "Was", "ir_w1", "pw_w", "nm_w1", "nm_w2",
             "ir_w2", "Wo", "wmod", "ccolT", "baf", "bas", "lamf", "lams", "nm_b1",
             "nm_b2", "ir_b1", "ir_b2", "pw_b", "norm_w", "gate_wT", "W1T", "W3T",
             "W2T", "Ws1T", "Ws3T", "Ws2T", "ones_col", "twos_col", "ones_row",
             "eye128", "resetmask", "mSTRICT", "mSTRICTN", "mINCL", "eye64x2"] + [
             f"x4d_{m}" for m in range(16)]

N_TOK = 8192
N_CORES = 8


def kernel(**inputs):
    nc, mk = build(N_TOK, prep_out=False, do_B=True, do_C=True, n_cores=N_CORES)
    gh = host_layout(inputs, N_TOK, n_cores=N_CORES)
    in_map = {nm: np.ascontiguousarray(gh[nm], dtype=np.float32) for nm in _IN_NAMES}
    res = run_bass_kernel_spmd(nc, [in_map] * N_CORES, core_ids=list(range(N_CORES)))
    my_tok = N_TOK // N_CORES
    out = np.zeros((1, N_TOK, 2048), dtype=np.float32)
    for c in range(N_CORES):
        oy = np.asarray(res.results[c]["o_y"], dtype=np.float32)
        out[0, c * my_tok:(c + 1) * my_tok, :] = \
            oy.transpose(2, 0, 1).reshape(my_tok, 2048)
    return out
